# version 13
# baseline (speedup 1.0000x reference)
import contextlib
import numpy as np
import ml_dtypes
import concourse.bass as bass
import concourse.mybir as mybir
from concourse.bass_utils import run_bass_kernel_spmd

F32 = mybir.dt.float32
BF16 = mybir.dt.bfloat16
AF = mybir.ActivationFunctionType
ALU = mybir.AluOpType
bfloat16 = ml_dtypes.bfloat16

NSEQ = 4
D = 1024
H = 8
DK = 128
CH = 32
HFF = 4096
ST = 512
ENG2SEM = {"sync": "qc", "gpsimd": "G", "tensor": "T",
           "vector": "V", "scalar": "S"}


DMA_SEMS = {"qc", "qcg", "qxb0", "qxb1", "qxf0", "qxf1", "qgs", "qst0", "qst1", "qat", "qat2", "qcr", "qoa0", "qoa1", "qoy", "qod"}


class Emitter:

    def __init__(self, nc, sems):
        self.nc = nc
        self.sems = sems

    def begin_pass(self, engine, eng):
        self.cur = engine
        self.eng = eng
        self.cnt = {k: 0 for k in self.sems}
        self.res = {}
        self.bk = {}
        self.lastwait = {}

    def _wait(self, sem, val):
        if val <= 0 or self.lastwait.get(sem, 0) >= val:
            return
        self.lastwait[sem] = val
        self.eng.wait_ge(self.sems[sem], val)

    def op(self, engine, fn, reads=(), writes=(), banks=(), sem=None, inc=1,
           extra_waits=(), grp=None):
        if sem is None:
            sem = ENG2SEM[engine]
        if self.cur == engine:
            own = ENG2SEM[engine]
            waits = {}

            def add(s, v, selfok=False):
                if (s == own and not selfok and s not in DMA_SEMS
                        and engine == "tensor"):
                    return
                waits[s] = max(waits.get(s, 0), v)

            for r in reads:
                e = self.res.get(r)
                if e and e["w"]:
                    add(*e["w"])
            for r in writes:
                e = self.res.get(r)
                if e:
                    if e["w"]:
                        add(*e["w"])
                    for sv in e["r"].values():
                        add(*sv)
            for b in banks:
                for oeng, (sv, og) in self.bk.get(b, {}).items():
                    if oeng == engine and engine == "tensor":
                        continue
                    add(sv[0], sv[1], selfok=True)
            for s, v in extra_waits:
                add(s, v, selfok=True)
            for s in sorted(waits):
                if s in DMA_SEMS:
                    self._wait(s, self.cnt[s])
                else:
                    self._wait(s, waits[s])
            inst = fn(self.eng)
            inst.then_inc(self.sems[sem], inc)
        self.cnt[sem] += inc
        snap = (sem, self.cnt[sem])
        for r in writes:
            self.res[r] = {"w": snap, "r": {}}
        for r in reads:
            e = self.res.setdefault(r, {"w": None, "r": {}})
            e["r"][engine] = snap
        for b in banks:
            self.bk.setdefault(b, {})[engine] = (snap, grp)
        return snap

    def barrier(self):
        for s, v in [(s, self.cnt[s]) for s in self.sems]:
            self._wait(s, v)
        self.res = {}
        self.bk = {}


def build_core(L=2048, dbg=False):
    nst = L // 128
    T = L * NSEQ

    nc = bass.Bass("TRN2", target_bir_lowering=False,
                   detect_race_conditions=False)
    dt = nc.dram_tensor

    x_d = dt("x", [D, T], F32, kind="ExternalInput").ap()
    wq_d = dt("w_q", [D, D], F32, kind="ExternalInput").ap()
    wk_d = dt("w_k", [D, D], F32, kind="ExternalInput").ap()
    wv_d = dt("w_v", [D, D], F32, kind="ExternalInput").ap()
    cwq_d = dt("cw_q", [D, 4], F32, kind="ExternalInput").ap()
    cwk_d = dt("cw_k", [D, 4], F32, kind="ExternalInput").ap()
    cwv_d = dt("cw_v", [D, 4], F32, kind="ExternalInput").ap()
    bw_d = dt("b_w", [D, H], F32, kind="ExternalInput").ap()
    mw_d = dt("mix_w", [D, H], F32, kind="ExternalInput").ap()
    mb_d = dt("mix_b", [H, 1], F32, kind="ExternalInput").ap()
    mb2_d = dt("mix_bias", [H, 1], F32, kind="ExternalInput").ap()
    onw_d = dt("o_norm_w", [DK, 1], F32, kind="ExternalInput").ap()
    ow_d = dt("o_w", [D, D], F32, kind="ExternalInput").ap()
    wfcT_d = dt("w_fcT", [D, HFF], F32, kind="ExternalInput").ap()
    lnw_d = dt("ln_w", [HFF], F32, kind="ExternalInput").ap()
    lnb_d = dt("ln_b", [HFF], F32, kind="ExternalInput").ap()
    wprT_d = dt("w_prT", [HFF, D], F32, kind="ExternalInput").ap()
    ident_d = dt("c_ident", [128, 128], BF16, kind="ExternalInput").ap()
    maskbd_d = dt("c_maskbd", [128, 128], BF16, kind="ExternalInput").ap()
    masku_d = dt("c_masku", [128, CH], BF16, kind="ExternalInput").ap()
    ones_d = dt("c_ones", [128, 1], BF16, kind="ExternalInput").ap()
    onesf_d = dt("c_onesf", [128, 1], F32, kind="ExternalInput").ap()
    onesr_d = dt("c_onesr", [1, ST], BF16, kind="ExternalInput").ap()
    eps_d = dt("c_eps", [128, 4], F32, kind="ExternalInput").ap()
    onesrf_d = dt("c_onesrf", [1, 128], F32, kind="ExternalInput").ap()
    identf_d = dt("c_identf", [128, 128], F32, kind="ExternalInput").ap()

    y_d = dt("y", [D, T], F32, kind="ExternalOutput").ap()
    attn_kind = {"kind": "ExternalOutput"} if dbg else {}
    attn_d = dt("attn_s", [D, T], F32, **attn_kind).ap()
    if dbg:
        dq_d = dt("dbg_q", [D, T], BF16, kind="ExternalOutput").ap()
        dk_d = dt("dbg_k", [D, T], BF16, kind="ExternalOutput").ap()
        dv_d = dt("dbg_v", [D, T], BF16, kind="ExternalOutput").ap()
        dbg_bg_d = dt("dbg_bg", [H, 2 * T], BF16, kind="ExternalOutput").ap()
        dbg_u_d = dt("dbg_u", [128, 16384], BF16, kind="ExternalOutput").ap()
        dofm_d = dt("dbg_ofm", [D, T], BF16, kind="ExternalOutput").ap()

    stk = contextlib.ExitStack()
    sbG = lambda n, sh, d_: stk.enter_context(nc.sbuf_tensor(n, sh, d_))
    psG = lambda n, sh, d_: stk.enter_context(nc.psum_tensor(n, sh, d_))
    sems = {s: stk.enter_context(nc.semaphore(s))
            for s in ("qc", "qcg", "qxb0", "qxb1", "qxf0", "qxf1", "qgs", "qst0", "qst1", "qat", "qat2", "qcr", "qoa0", "qoa1", "qoy", "qod",
                      "G", "T", "V", "S")}

    P_prA = psG("P_prA", [128, 512], F32)
    P_prB = psG("P_prB", [128, 512], F32)
    P_fA = psG("P_fA", [128, 512], F32)
    P_fB = psG("P_fB", [128, 512], F32)
    P_bA = psG("P_bA", [128, 1024], BF16)
    P_bB = psG("P_bB", [128, 1024], BF16)
    P_sA = psG("P_sA", [128, 512], F32)
    P_sB = psG("P_sB", [128, 512], F32)
    PPR = [(P_prA, "bk_prA"), (P_prB, "bk_prB")]
    PF = [(P_fA, "bk_fA"), (P_fB, "bk_fB")]
    PBb = [(P_bA, "bk_bA"), (P_bB, "bk_bB")]
    PSP = [(P_sA, "bk_sA"), (P_sB, "bk_sB")]
    P4 = PPR + PSP

    ident = sbG("ident", [128, 128], BF16)
    maskbd = sbG("maskbd", [128, 128], BF16)
    masku = sbG("masku", [128, CH], BF16)
    ones_c = sbG("ones_c", [128, 1], BF16)
    ones_cf = sbG("ones_cf", [128, 1], F32)
    ones_r = sbG("ones_r", [1, ST], BF16)
    ones_rf = sbG("ones_rf", [1, 128], F32)
    rows = sbG("rows", [1, 2 * ST], F32)
    rbf_mu = sbG("rbf_mu", [1, ST], BF16)
    rbf_rs = sbG("rbf_rs", [1, ST], BF16)
    bc_f = sbG("bc_f", [128, ST], F32)
    gam = sbG("gam", [1, 8], F32)
    gcoln = sbG("gcoln", [128, 2], F32)
    gcolp = sbG("gcolp", [128, 2], F32)
    gpart = sbG("gpart", [128, 16], F32)
    gred = sbG("gred", [128, 32], F32)
    gr4 = sbG("gr4", [128, 4], F32)
    ktmp = sbG("ktmp", [128, 4], F32)
    ktmp2 = sbG("ktmp2", [128, 4], F32)
    grow = sbG("grow", [1, 128], F32)
    lnwb = sbG("lnwb", [128, 64], BF16)
    lnw_f = sbG("lnw_f", [128, 32], F32)
    c12 = sbG("c12", [2, D], BF16)
    eps_s = sbG("eps_s", [128, 4], F32)
    crhs = sbG("crhs", [2, ST], BF16)


    p1 = contextlib.ExitStack()
    sb1 = lambda n, sh, d_: p1.enter_context(nc.sbuf_tensor(n, sh, d_))
    row_scr = sb1("row_scr", [1, HFF], BF16)
    w_q = sb1("w_qs", [128, 8192], BF16)
    w_k = sb1("w_ks", [128, 8192], BF16)
    w_v = sb1("w_vs", [128, 8192], BF16)
    o_w = sb1("o_ws", [128, 8192], BF16)
    bw_s = sb1("bw_s", [128, 64], BF16)
    mw_s = sb1("mw_s", [128, 64], BF16)
    mb_s = sb1("mb_s", [8, 2], F32)
    onw_s = sb1("onw_s", [128, 1], F32)
    cw = {p: sb1(f"cwsb_{p}", [128, 32], F32) for p in "qkv"}
    colsum = sb1("colsum", [1, 3 * D + 2 * H], BF16)
    xfc = [sb1(f"xfc{i}", [128, ST], F32) for i in range(2)]
    xb_ = [sb1(f"xb{i}", [128, 8 * ST], BF16) for i in range(2)]
    fm = {p: sb1(f"fm{p}", [128, 8 * ST], BF16) for p in "qkv"}
    convb = [sb1(f"convb{i}", [128, 4 * 131], BF16) for i in range(2)]
    acc0 = sb1("acc0", [128, ST], BF16)
    acc1 = sb1("acc1", [128, ST], BF16)
    tails = sb1("tails", [128, 24 * 12], BF16)
    bgfm = sb1("bgfm", [8, 2 * ST], BF16)
    bgtm = sb1("bgtm", [128, 64], BF16)
    ofm = sb1("ofm", [128, 8 * ST], BF16)
    attn_sb = [sb1(f"attnsb{i}", [128, ST], F32) for i in range(2)]
    S_bf = sb1("S_bf", [128, H * 512], BF16)
    NSL = 4
    sl = {}
    for nm, w, d_ in (("Abd", 128, BF16), ("C0", 128, BF16), ("D1", 128, BF16),
                      ("C1", 128, BF16), ("D2", 128, BF16), ("C2", 128, BF16),
                      ("D3", 128, BF16), ("C3", 128, BF16), ("C4", 128, BF16),
                      ("ktm", 128, BF16), ("vtm", 128, BF16), ("Xu", 128, BF16),
                      ("Ya", 128, BF16), ("Yb", 128, BF16), ("oms", 128, BF16),
                      ("ofin", 128, BF16), ("Mt", CH, BF16)):
        sl[nm] = [sb1(f"sl_{nm}{i}", [128, w], d_) for i in range(NSL)]
    ubd = [sb1(f"ubd{i}", [128, 512], BF16) for i in range(NSL)]
    scaq = sb1("scaq", [128, 12 * H], F32)
    scr_s = [sb1("scr_s0", [128, 128], BF16)]
    gstage = [sb1("gstage0", [128, 512], F32)]
    ident_f = sb1("ident_f", [128, 128], F32)
    p1.close()

    p2w = contextlib.ExitStack()
    sb2w = lambda n, sh, d_: p2w.enter_context(nc.sbuf_tensor(n, sh, d_))
    Wfc = sb2w("Wfc", [128, 8 * HFF], BF16)
    Wpr = sb2w("Wpr", [128, 32 * D], BF16)
    p2t = contextlib.ExitStack()
    sb2t = lambda n, sh, d_: p2t.enter_context(nc.sbuf_tensor(n, sh, d_))
    qstage = [sb2t(f"qstage{i}", [128, HFF], F32) for i in range(2)]
    sgn = sb2t("sgn", [128, HFF], BF16)
    p2t.close()
    p2b = contextlib.ExitStack()
    sb2b = lambda n, sh, d_: p2b.enter_context(nc.sbuf_tensor(n, sh, d_))
    attb = sb2b("attb", [128, 8 * ST], BF16)
    attf = [sb2b(f"attf{i}", [128, ST], F32) for i in range(2)]
    h_sb = sb2b("h_sb", [128, 32 * ST], BF16)
    csrow = sb2b("csrow", [1, HFF], BF16)
    ysb = sb2b("ysb", [128, ST], F32)
    x2s_p2 = [sb2b(f"x2s_p2{i}", [128, ST], BF16) for i in range(2)]

    nc.const_aps.register(F32, 1e-5, eps_s[:, 0:1])
    nc.const_aps.register(F32, 1e-12, eps_s[:, 1:2])
    nc.const_aps.register(F32, 1.0 / DK, eps_s[:, 2:3])

    B = Emitter(nc, sems)

    def dma(q, dst, src, tag=None, reads=(), extra=(), sem="qc"):
        return B.op(q, lambda e: e.dma_start(out=dst, in_=src),
                    reads=reads, writes=(tag,) if tag else (),
                    sem=sem, inc=16, extra_waits=extra)

    def schedule():
        consts = [
            dma("sync", ident[:, :], ident_d[:, :], "ident"),
            dma("sync", maskbd[:, :], maskbd_d[:, :], "maskbd"),
            dma("sync", masku[:, :], masku_d[:, :], "masku"),
            dma("sync", ones_c[:, :], ones_d[:, :], "ones_c"),
            dma("sync", ones_cf[:, :], onesf_d[:, :], "ones_cf"),
            dma("sync", ones_r[:, :], onesr_d[:, :], "ones_r"),
            dma("sync", ones_rf[:, :], onesrf_d[:, :], "ones_rf"),
            dma("sync", onw_s[:, :], onw_d[:, :], "onw"),
            dma("sync", mb_s[:, 0:1], mb_d[:, :], "mb0"),
            dma("sync", mb_s[:, 1:2], mb2_d[:, :], "mb1"),
            dma("sync", eps_s[:, :], eps_d[:, :], "eps"),
            dma("sync", ident_f[:, :], identf_d[:, :], "ident_f"),
        ]
        for wsb, wd in ((w_q, wq_d), (w_k, wk_d), (w_v, wv_d), (o_w, ow_d)):
            src = bass.AP(wd.tensor, 0, [[D, 128], [128 * D, 8], [1, D]])
            dst = bass.AP(wsb, 0, [[8192, 128], [1024, 8], [1, 1024]])
            consts.append(dma("gpsimd", dst, src, wsb.name, sem="qcg"))
        for wsb, wd in ((bw_s, bw_d), (mw_s, mw_d)):
            src = bass.AP(wd.tensor, 0, [[H, 128], [128 * H, 8], [1, H]])
            dst = bass.AP(wsb, 0, [[64, 128], [8, 8], [1, 8]])
            consts.append(dma("gpsimd", dst, src, wsb.name, sem="qcg"))
        for p, cd in (("q", cwq_d), ("k", cwk_d), ("v", cwv_d)):
            src = bass.AP(cd.tensor, 0, [[4, 128], [512, 8], [1, 4]])
            dst = bass.AP(cw[p], 0, [[32, 128], [4, 8], [1, 4]])
            consts.append(dma("sync", dst, src, f"cw{p}"))
        consts.append(dma("gpsimd", row_scr[0:1, :],
                          bass.AP(lnw_d.tensor, 0, [[HFF, 1], [1, HFF]]), "lnw_r",
                          sem="qcg"))

        def load_xb(s):
            par = s % 2
            for a in range(8):
                src = bass.AP(x_d.tensor, a * 128 * T + ST * s,
                              [[T, 128], [1, ST]])
                dma("gpsimd", xb_[par][:, a * ST:(a + 1) * ST], src,
                    f"xb{par}_{a}", sem=f"qxb{par}")

        def load_xfc(s, a):
            src = bass.AP(x_d.tensor, a * 128 * T + ST * s,
                          [[T, 128], [1, ST]])
            dma("sync", xfc[a % 2][:, :], src, f"xfc{a % 2}", sem=f"qxf{a % 2}")

        load_xb(0)

        B.op("vector", lambda e: e.tensor_add(mb_s[:, 0:1], mb_s[:, 0:1],
                                              mb_s[:, 1:2]),
             reads=("mb0", "mb1"), writes=("mbsum",), extra_waits=consts)
        B.op("vector", lambda e: e.tensor_scalar(
            o_w[:, :], o_w[:, :], onw_s[:, :], None, ALU.mult),
            reads=("o_ws", "onw"), writes=("o_ws",))
        for nm, wsb, off, width in (("q", w_q, 0, D), ("k", w_k, D, D),
                                    ("v", w_v, 2 * D, D),
                                    ("b", bw_s, 3 * D, H),
                                    ("m", mw_s, 3 * D + H, H)):
            for half in range(max(1, width // 512)):
                n0 = half * 512
                nn = min(width - n0, 512)
                for a in range(8):
                    rhs = (wsb[:, a * 1024 + n0:a * 1024 + n0 + nn]
                           if width == D else wsb[:, a * 8:(a + 1) * 8])
                    B.op("tensor", lambda e, rhs=rhs, a=a, nn=nn: e.matmul(
                        P_sA[0:1, 0:nn], ones_c[:, :], rhs,
                        start=(a == 0), stop=(a == 7)),
                        reads=(wsb.name, "ones_c"), banks=("bk_sA",))
                B.op("vector", lambda e, off=off, n0=n0, nn=nn: e.tensor_copy(
                    colsum[0:1, off + n0:off + n0 + nn], P_sA[0:1, 0:nn]),
                    banks=("bk_sA",), writes=("colsum",))
        for ci in range(32):
            B.op("tensor", lambda e, ci=ci: e.matmul(
                P_sB[:, ci:ci + 1], row_scr[0:1, ci * 128:(ci + 1) * 128],
                ones_r[0:1, 0:1], start=True, stop=True),
                reads=("lnw_r", "ones_r"), banks=("bk_sB",))
        B.op("vector", lambda e: e.tensor_copy(lnwb[:, 0:32], P_sB[:, 0:32]),
             banks=("bk_sB",), writes=("lnw",))
        B.op("vector", lambda e: e.tensor_copy(lnw_f[:, :], P_sB[:, 0:32]),
             banks=("bk_sB",), writes=("lnwf",))
        dma("gpsimd", row_scr[0:1, :],
            bass.AP(lnb_d.tensor, 0, [[HFF, 1], [1, HFF]]), "lnw_r",
            reads=("lnw_r",), sem="qcg")
        for ci in range(32):
            B.op("tensor", lambda e, ci=ci: e.matmul(
                P_sB[:, 32 + ci:33 + ci], row_scr[0:1, ci * 128:(ci + 1) * 128],
                ones_r[0:1, 0:1], start=True, stop=True),
                reads=("lnw_r", "ones_r"), banks=("bk_sB",))
        B.op("vector", lambda e: e.tensor_copy(lnwb[:, 32:64], P_sB[:, 32:64]),
             banks=("bk_sB",), writes=("lnb",))
        for i in range(NSL):
            B.op("gpsimd", lambda e, i=i: e.memset(ubd[i][:, :], 0),
                 writes=(f"ubd{i}",), sem="G", inc=1)
        for q in range(H):
            B.op("gpsimd", lambda e, q=q: e.memset(
                S_bf[:, q * 512:(q + 1) * 512], 0),
                writes=(f"S_{q}",), sem="G", inc=1)

        def ln_stats_rows(get_rhs, xtag, sqsrc, x2t, x2tag, dd):
            for a in range(8):
                B.op("tensor", lambda e, a=a: e.matmul(
                    P_sA[0:1, 0:ST], ones_c[:, :], get_rhs(a),
                    start=(a == 0), stop=(a == 7)),
                    reads=(f"{xtag}_{a}", "ones_c"), banks=("bk_sA",))
            for a in range(8):
                B.op("scalar", lambda e, a=a: e.activation(
                    x2t[0][:, :], sqsrc(a), AF.Square),
                    reads=(f"{xtag}_{a}",), writes=(f"{x2tag}0",))
                B.op("tensor", lambda e, a=a: e.matmul(
                    P_sB[0:1, 0:ST], ones_c[:, :], x2t[0][:, :],
                    start=(a == 0), stop=(a == 7)),
                    reads=(f"{x2tag}0", "ones_c"), banks=("bk_sB",))
            B.op("vector", lambda e, dd=dd: e.tensor_single_scalar(
                rows[0:1, 0:ST], P_sA[0:1, 0:ST], -1.0 / dd, ALU.mult),
                banks=("bk_sA",), writes=("rows0",))
            B.op("vector", lambda e, dd=dd: e.tensor_single_scalar(
                rows[0:1, ST:2 * ST], P_sB[0:1, 0:ST], 1.0 / dd, ALU.mult),
                banks=("bk_sB",), writes=("rows1",))
            B.op("vector", lambda e: e.tensor_mul(
                bc_f[0:1, 0:ST], rows[0:1, 0:ST], rows[0:1, 0:ST]),
                reads=("rows0",), writes=("bc_f",))
            B.op("vector", lambda e: e.tensor_sub(
                rows[0:1, ST:2 * ST], rows[0:1, ST:2 * ST], bc_f[0:1, 0:ST]),
                reads=("rows1", "bc_f"), writes=("rows1",))
            B.op("scalar", lambda e: e.activation(
                rows[0:1, ST:2 * ST], rows[0:1, ST:2 * ST], AF.Sqrt, bias=1e-5),
                reads=("rows1",), writes=("rows1",))
            B.op("vector", lambda e: e.reciprocal(rows[0:1, ST:2 * ST], rows[0:1, ST:2 * ST]),
                 reads=("rows1",), writes=("rows1",))
            B.op("vector", lambda e: e.tensor_copy(rbf_mu[0:1, :], rows[0:1, 0:ST]),
                 reads=("rows0",), writes=("rbf_mu",))
            B.op("vector", lambda e: e.tensor_copy(rbf_rs[0:1, :], rows[0:1, ST:2 * ST]),
                 reads=("rows1",), writes=("rbf_rs",))

        def bc_row(src_row, dsttag, dst, bank):
            bk, bkn = bank
            B.op("tensor", lambda e: e.matmul(
                bk[:, 0:ST], ones_r[0:1, 0:128], src_row, start=True, stop=True),
                reads=("rbf_rs", "rbf_mu", "ones_r"), banks=(bkn,))
            B.op("vector", lambda e: e.tensor_copy(dst[:, :], bk[:, 0:ST]),
                 banks=(bkn,), writes=(dsttag,))

        def stripe_phase1(s):
            par = s % 2
            xbs = xb_[par]
            if s + 1 < nst:
                load_xb(s + 1)
            ln_stats_rows(lambda a: xbs[:, a * ST:(a + 1) * ST], f"xb{par}",
                          lambda a: xbs[:, a * ST:(a + 1) * ST], [acc0, acc1],
                          "acc", D)
            bc_row(rbf_rs[0:1, :], "bc_f", bc_f, PPR[0])

            for gi, (wsb, csoff) in enumerate(((bw_s, 3 * D), (mw_s, 3 * D + H))):
                pb, bkn = PSP[gi]
                for a in range(8):
                    B.op("tensor", lambda e, a=a, wsb=wsb, pb=pb: e.matmul(
                        pb[0:8, 0:ST], wsb[:, a * 8:(a + 1) * 8],
                        xbs[:, a * ST:(a + 1) * ST], start=(a == 0), stop=False),
                        reads=(f"xb{par}_{a}", wsb.name), banks=(bkn,),
                        grp=("bg", s, gi))
                B.op("tensor", lambda e, csoff=csoff, pb=pb: e.matmul(
                    pb[0:8, 0:ST], colsum[0:1, csoff:csoff + 8],
                    rbf_mu[0:1, :], start=False, stop=True),
                    reads=("colsum", "rbf_mu"), banks=(bkn,),
                    grp=("bg", s, gi))
                B.op("vector", lambda e, pb=pb: e.tensor_mul(
                    pb[0:8, 0:ST], pb[0:8, 0:ST], bc_f[0:8, :]),
                    reads=("bc_f",), banks=(bkn,))
                outap = bass.AP(bgfm, gi * ST,
                                [[2 * ST, 8], [32, 4], [128, 4], [1, 32]])
                bias = mb_s[:, 0:1] if gi == 1 else 0.0
                B.op("scalar", lambda e, pb=pb, outap=outap, bias=bias:
                     e.activation(outap, pb[0:8, 0:ST], AF.Sigmoid, bias=bias),
                     reads=("mbsum",), banks=(bkn,), writes=(f"bgfm{gi}",))

            pi = 0
            for h in range(H):
                for p, wsb, csoff in (("k", w_k, D), ("v", w_v, 2 * D),
                                      ("q", w_q, 0)):
                    fmdst = fm[p]
                    pb, bkn = P4[pi % 4]
                    cb = convb[pi % 2]
                    cbn = f"convb{pi % 2}"
                    pi += 1
                    for a in range(8):
                        B.op("tensor", lambda e, a=a, h=h, wsb=wsb, pb=pb:
                             e.matmul(pb[:, 0:ST],
                                      wsb[:, a * 1024 + h * 128:a * 1024 + h * 128 + 128],
                                      xbs[:, a * ST:(a + 1) * ST],
                                      start=(a == 0), stop=False),
                             reads=(f"xb{par}_{a}", wsb.name), banks=(bkn,),
                             grp=("qkv", s, p, h))
                    B.op("tensor", lambda e, h=h, csoff=csoff, pb=pb: e.matmul(
                        pb[:, 0:ST],
                        colsum[0:1, csoff + h * 128:csoff + h * 128 + 128],
                        rbf_mu[0:1, :], start=False, stop=True),
                        reads=("colsum", "rbf_mu"), banks=(bkn,),
                        grp=("qkv", s, p, h))
                    ti = "qkv".index(p) * 8 + h
                    if s > 0:
                        B.op("gpsimd", lambda e, cb=cb, ti=ti: e.tensor_copy(
                            bass.AP(cb, 0, [[524, 128], [131, 4], [1, 3]]),
                            bass.AP(tails, ti * 12, [[288, 128], [3, 4], [1, 3]])),
                            reads=(f"tails{ti}",), writes=(cbn,), sem="G", inc=1)
                    else:
                        B.op("gpsimd", lambda e, cb=cb: e.memset(
                            bass.AP(cb, 0, [[524, 128], [131, 4], [1, 3]]), 0),
                            writes=(cbn,), sem="G", inc=1)
                    B.op("vector", lambda e, cb=cb, pb=pb: e.tensor_tensor(
                        bass.AP(cb, 3, [[524, 128], [131, 4], [1, 128]]),
                        pb[:, 0:ST], bc_f[:, :], ALU.mult),
                        reads=("bc_f", cbn), banks=(bkn,), writes=(cbn,))
                    B.op("vector", lambda e, cb=cb, p=p, h=h: e.tensor_single_scalar(
                        acc0[:, :], bass.AP(cb, 0, [[524, 128], [131, 4], [1, 128]]),
                        cw[p][:, h * 4:h * 4 + 1], ALU.mult),
                        reads=(cbn, f"cw{p}"), writes=("acc0",)),
                    B.op("vector", lambda e, cb=cb, p=p, h=h: e.scalar_tensor_tensor(
                        acc1[:, :], bass.AP(cb, 1, [[524, 128], [131, 4], [1, 128]]),
                        cw[p][:, h * 4 + 1:h * 4 + 2], acc0[:, :], ALU.mult, ALU.add),
                        reads=(cbn, "acc0", f"cw{p}"), writes=("acc1",))
                    B.op("vector", lambda e, cb=cb, p=p, h=h: e.scalar_tensor_tensor(
                        acc0[:, :], bass.AP(cb, 2, [[524, 128], [131, 4], [1, 128]]),
                        cw[p][:, h * 4 + 2:h * 4 + 3], acc1[:, :], ALU.mult, ALU.add),
                        reads=(cbn, "acc1", f"cw{p}"), writes=("acc0",))
                    B.op("vector", lambda e, cb=cb, p=p, h=h: e.scalar_tensor_tensor(
                        acc1[:, :], bass.AP(cb, 3, [[524, 128], [131, 4], [1, 128]]),
                        cw[p][:, h * 4 + 3:h * 4 + 4], acc0[:, :], ALU.mult, ALU.add),
                        reads=(cbn, "acc0", f"cw{p}"), writes=("acc1",))
                    B.op("gpsimd", lambda e, cb=cb, ti=ti: e.tensor_copy(
                        bass.AP(tails, ti * 12, [[288, 128], [3, 4], [1, 3]]),
                        bass.AP(cb, 128, [[524, 128], [131, 4], [1, 3]])),
                        reads=(cbn,), writes=(f"tails{ti}",), sem="G", inc=1)
                    outap = bass.AP(fmdst, h * ST,
                                    [[8 * ST, 128], [32, 4], [128, 4], [1, 32]])
                    B.op("scalar", lambda e, outap=outap: e.activation(
                        outap, acc1[:, :], AF.Silu),
                        reads=("acc1",), writes=(f"fm{p}_{h}",))

            if dbg:
                for p, dd in (("q", dq_d), ("k", dk_d), ("v", dv_d)):
                    for h in range(H):
                        dma("scalar",
                            bass.AP(dd.tensor, h * 128 * T + s * ST,
                                    [[T, 128], [1, ST]]),
                            fm[p][:, h * ST:(h + 1) * ST],
                            reads=(f"fm{p}_{h}",), sem="qod")
                dma("scalar", bass.AP(dbg_bg_d.tensor, 2 * ST * s,
                    [[2 * T, 8], [1, 2 * ST]]), bgfm[:, :],
                    reads=("bgfm0", "bgfm1"), sem="qod")

            for c4 in range(4):
                cpar = c4 % 2
                pbB, bknB = PBb[cpar]
                B.op("tensor", lambda e, c4=c4, pbB=pbB: e.transpose(
                    pbB[:, 768:776], bgfm[:, c4 * 128:(c4 + 1) * 128],
                    ident[0:8, 0:8]),
                    reads=("bgfm0", "ident"), banks=(bknB,))
                B.op("tensor", lambda e, c4=c4, pbB=pbB: e.transpose(
                    pbB[:, 776:784], bgfm[:, ST + c4 * 128:ST + (c4 + 1) * 128],
                    ident[0:8, 0:8]),
                    reads=("bgfm1", "ident"), banks=(bknB,))
                B.op("vector", lambda e, c4=c4, pbB=pbB: e.tensor_copy(
                    bgtm[:, c4 * 16:(c4 + 1) * 16], pbB[:, 768:784]),
                    banks=(bknB,), writes=(f"bgtm{c4}",))

                FB4 = [(P_fA, "bk_fA"), (P_fB, "bk_fB"),
                       (P_sA, "bk_sA"), (P_sB, "bk_sB")]
                for grp in range(2):
                    pks = []
                    for idx in range(4):
                        q = grp * 4 + idx
                        pf, bknF = FB4[idx]
                        pb, bknB2 = PBb[idx % 2]
                        pks.append(dict(
                            q=q, i=idx, pf=pf, bkF=bknF, pb=pb, bkB=bknB2,
                            bo=384 * (idx // 2),
                            kfn=f"fmk_{q}", qfn=f"fmq_{q}", vfn=f"fmv_{q}",
                            co=q * ST + c4 * 128, sc=q * 12,
                            Sq=S_bf[:, q * 512:(q + 1) * 512], Sqn=f"S_{q}"))

                    for pk in pks:
                        for col, pnm in ((0, "k"), (128, "v"), (256, "q")):
                            B.op("tensor", lambda e, pk=pk, col=col, pnm=pnm:
                                 e.transpose(
                                     pk["pb"][:, pk["bo"] + col:pk["bo"] + col + 128],
                                     fm[pnm][:, pk["co"]:pk["co"] + 128],
                                     ident[:, :]),
                                 reads=(f"fm{pnm}_{pk['q']}", "ident"),
                                 banks=(pk["bkB"],))
                    for pk in pks:
                        for i in range(4):
                            B.op("tensor", lambda e, pk=pk, i=i: e.matmul(
                                pk["pf"][32 * i:32 * i + 32, 32 * i:32 * i + 32],
                                fm["k"][:, pk["co"] + 32 * i:pk["co"] + 32 * i + 32],
                                fm["k"][:, pk["co"] + 32 * i:pk["co"] + 32 * i + 32],
                                start=True, stop=True, tile_position=(0, 32 * i)),
                                reads=(pk["kfn"],), banks=(pk["bkF"],))
                        for i in range(4):
                            B.op("tensor", lambda e, pk=pk, i=i: e.matmul(
                                pk["pf"][32 * i:32 * i + 32, 128:160],
                                fm["k"][:, pk["co"] + 32 * i:pk["co"] + 32 * i + 32],
                                fm["q"][:, pk["co"] + 32 * i:pk["co"] + 32 * i + 32],
                                start=True, stop=True, tile_position=(0, 32 * i)),
                                reads=(pk["kfn"], pk["qfn"]), banks=(pk["bkF"],))
                    for pk in pks:
                        i_, sc = pk["i"], pk["sc"]
                        B.op("scalar", lambda e, pk=pk: e.activation(
                            scr_s[0][:, :], pk["pb"][:, pk["bo"]:pk["bo"] + 128],
                            AF.Square, accum_out=scaq[:, pk["sc"]:pk["sc"] + 1]),
                            banks=(pk["bkB"],),
                            writes=(f"ssk{i_}", "scr0"))
                        B.op("scalar", lambda e, pk=pk: e.activation(
                            scr_s[0][:, :], pk["pb"][:, pk["bo"] + 256:pk["bo"] + 384],
                            AF.Square, accum_out=scaq[:, pk["sc"] + 1:pk["sc"] + 2]),
                            banks=(pk["bkB"],),
                            writes=(f"ssq{i_}", "scr0"))
                        B.op("vector", lambda e, pk=pk: e.tensor_copy(
                            sl["ktm"][pk["i"]][:, :], pk["pb"][:, pk["bo"]:pk["bo"] + 128]),
                            banks=(pk["bkB"],), writes=(f"ktm{i_}",))
                        B.op("vector", lambda e, pk=pk: e.tensor_copy(
                            sl["vtm"][pk["i"]][:, :], pk["pb"][:, pk["bo"] + 128:pk["bo"] + 256]),
                            banks=(pk["bkB"],), writes=(f"vtm{i_}",))
                        B.op("scalar", lambda e, sc=sc: e.activation(
                            scaq[:, sc + 2:sc + 3], scaq[:, sc:sc + 1],
                            AF.Sqrt, bias=1e-12),
                            reads=(f"ssk{i_}",), writes=(f"sqk{i_}",))
                        B.op("scalar", lambda e, sc=sc: e.activation(
                            scaq[:, sc + 3:sc + 4], scaq[:, sc + 1:sc + 2],
                            AF.Sqrt, bias=1e-12),
                            reads=(f"ssq{i_}",), writes=(f"sqq{i_}",))
                        B.op("vector", lambda e, sc=sc: e.reciprocal(
                            scaq[:, sc + 4:sc + 5], scaq[:, sc + 2:sc + 3]),
                            reads=(f"sqk{i_}",), writes=(f"rsk{i_}",))
                        B.op("vector", lambda e, sc=sc: e.reciprocal(
                            scaq[:, sc + 5:sc + 6], scaq[:, sc + 3:sc + 4]),
                            reads=(f"sqq{i_}",), writes=(f"rsq{i_}",))
                        bcol = bgtm[:, c4 * 16 + pk["q"]:c4 * 16 + pk["q"] + 1]
                        gcol = bgtm[:, c4 * 16 + 8 + pk["q"]:c4 * 16 + 8 + pk["q"] + 1]
                        B.op("vector", lambda e, sc=sc: e.tensor_mul(
                            scaq[:, sc + 6:sc + 7], scaq[:, sc + 4:sc + 5],
                            scaq[:, sc + 4:sc + 5]),
                            reads=(f"rsk{i_}",), writes=(f"rs2{i_}",))
                        B.op("vector", lambda e, sc=sc, bcol=bcol:
                             e.scalar_tensor_tensor(
                                 scaq[:, sc + 7:sc + 8], scaq[:, sc + 6:sc + 7],
                                 -1.0, bcol, ALU.mult, ALU.mult),
                             reads=(f"rs2{i_}", f"bgtm{c4}"), writes=(f"nrs2b{i_}",))
                        B.op("vector", lambda e, sc=sc, bcol=bcol: e.tensor_mul(
                            scaq[:, sc + 8:sc + 9], scaq[:, sc + 4:sc + 5], bcol),
                            reads=(f"rsk{i_}", f"bgtm{c4}"), writes=(f"rsb{i_}",))
                        B.op("vector", lambda e, sc=sc, gcol=gcol: e.tensor_mul(
                            scaq[:, sc + 9:sc + 10], gcol, scaq[:, sc + 5:sc + 6]),
                            reads=(f"rsq{i_}", f"bgtm{c4}"), writes=(f"grsq{i_}",))
                        B.op("vector", lambda e, sc=sc, gcol=gcol: e.tensor_scalar(
                            scaq[:, sc + 10:sc + 11], gcol, -1.0, 1.0,
                            ALU.mult, ALU.add),
                            reads=(f"bgtm{c4}",), writes=(f"omg{i_}",))
                        B.op("vector", lambda e, pk=pk, sc=sc:
                             e.scalar_tensor_tensor(
                                 sl["Abd"][pk["i"]][:, :], pk["pf"][:, 0:128],
                                 scaq[:, sc + 7:sc + 8], maskbd[:, :],
                                 ALU.mult, ALU.mult),
                             reads=(f"nrs2b{i_}", "maskbd"), banks=(pk["bkF"],),
                             writes=(f"Abd{i_}",))
                        B.op("vector", lambda e, pk=pk: e.tensor_mul(
                            sl["Mt"][pk["i"]][:, :], pk["pf"][:, 128:160],
                            masku[:, :]),
                            reads=("masku",), banks=(pk["bkF"],),
                            writes=(f"Mt{i_}",))
                        B.op("vector", lambda e, pk=pk, sc=sc:
                             e.tensor_single_scalar(
                                 sl["Xu"][pk["i"]][:, :], pk["pb"][:, pk["bo"] + 128:pk["bo"] + 256],
                                 scaq[:, sc + 8:sc + 9], ALU.mult),
                             reads=(f"rsb{i_}",), banks=(pk["bkB"],),
                             writes=(f"Xu{i_}",))
                    for pk in pks:
                        i_ = pk["i"]
                        ppR, bknR = PPR[pk["i"] % 2]
                        rco = (pk["i"] // 2) * 128
                        for i in range(4):
                            B.op("tensor", lambda e, pk=pk, i=i, ppR=ppR,
                                 rco=rco: e.matmul(
                                ppR[32 * i:32 * i + 32, rco:rco + 128],
                                fm["k"][:, pk["co"] + 32 * i:pk["co"] + 32 * i + 32],
                                pk["Sq"][:, 128 * i:128 * (i + 1)],
                                start=True, stop=True, tile_position=(0, 32 * i)),
                                reads=(pk["kfn"], pk["Sqn"]), banks=(bknR,))
                    for pk in pks:
                        i_ = pk["i"]
                        B.op("tensor", lambda e, pk=pk: e.transpose(
                            pk["pb"][:, pk["bo"]:pk["bo"] + 128], sl["Abd"][pk["i"]][:, :],
                            ident[:, :]),
                            reads=(f"Abd{i_}", "ident"), banks=(pk["bkB"],))
                        B.op("scalar", lambda e, pk=pk: e.copy(
                            sl["C0"][pk["i"]][:, :], pk["pb"][:, pk["bo"]:pk["bo"] + 128]),
                            banks=(pk["bkB"],), writes=(f"C0{i_}",))
                    for lvl in ((("D1", "C0", "Abd"), ("C1", "Abd", "C0")),
                                (("D2", "C1", "D1"), ("C2", "D1", "C1")),
                                (("D3", "C2", "D2"), ("C3", "D2", "C2")),
                                (("C4", "D3", "C3"),)):
                        for pk in pks:
                            i_ = pk["i"]
                            for li, (dstn, ltn, rtn) in enumerate(lvl):
                                co = 160 + 128 * li
                                B.op("tensor", lambda e, pk=pk, ltn=ltn,
                                     rtn=rtn, co=co:
                                     e.matmul(pk["pf"][:, co:co + 128],
                                              sl[ltn][pk["i"]][:, :],
                                              sl[rtn][pk["i"]][:, :],
                                              start=True, stop=True),
                                     reads=(f"{ltn}{i_}", f"{rtn}{i_}"),
                                     banks=(pk["bkF"],))
                        for pk in pks:
                            i_ = pk["i"]
                            for li, (dstn, ltn, rtn) in enumerate(lvl):
                                co = 160 + 128 * li
                                if li == 0:
                                    B.op("vector", lambda e, pk=pk, dstn=dstn,
                                         co=co: e.tensor_copy(
                                        sl[dstn][pk["i"]][:, :],
                                        pk["pf"][:, co:co + 128]),
                                        banks=(pk["bkF"],),
                                        writes=(f"{dstn}{i_}",))
                                else:
                                    B.op("scalar", lambda e, pk=pk, dstn=dstn,
                                         co=co: e.copy(
                                        sl[dstn][pk["i"]][:, :],
                                        pk["pf"][:, co:co + 128]),
                                        banks=(pk["bkF"],),
                                        writes=(f"{dstn}{i_}",))
                    for pk in pks:
                        i_ = pk["i"]
                        ppR, bknR = PPR[pk["i"] % 2]
                        rco = (pk["i"] // 2) * 128
                        B.op("vector", lambda e, pk=pk, ppR=ppR, rco=rco:
                             e.scalar_tensor_tensor(
                            sl["Ya"][pk["i"]][:, :], ppR[:, rco:rco + 128],
                            scaq[:, pk["sc"] + 7:pk["sc"] + 8],
                            sl["Xu"][pk["i"]][:, :], ALU.mult, ALU.add),
                            reads=(f"nrs2b{i_}", f"Xu{i_}"), banks=(bknR,),
                            writes=(f"Ya{i_}",))
                    for ai, cn in enumerate(("C4", "C3", "C2", "C1")):
                        yi, yo = ("Ya", "Yb") if ai % 2 == 0 else ("Yb", "Ya")
                        for pk in pks:
                            i_ = pk["i"]
                            B.op("tensor", lambda e, pk=pk, cn=cn, yi=yi:
                                 e.matmul(pk["pf"][:, 288:416],
                                          sl[cn][pk["i"]][:, :],
                                          sl[yi][pk["i"]][:, :],
                                          start=True, stop=False),
                                 reads=(f"{cn}{i_}", f"{yi}{i_}"),
                                 banks=(pk["bkF"],))
                            B.op("tensor", lambda e, pk=pk, yi=yi:
                                 e.matmul(pk["pf"][:, 288:416],
                                          ident[:, :],
                                          sl[yi][pk["i"]][:, :],
                                          start=False, stop=True),
                                 reads=(f"{yi}{i_}", "ident"),
                                 banks=(pk["bkF"],))
                            B.op("scalar", lambda e, pk=pk, yo=yo: e.copy(
                                sl[yo][pk["i"]][:, :], pk["pf"][:, 288:416]),
                                banks=(pk["bkF"],), writes=(f"{yo}{i_}",))
                    for pk in pks:
                        i_ = pk["i"]
                        B.op("tensor", lambda e, pk=pk: e.matmul(
                            pk["pf"][:, 288:416], sl["C0"][pk["i"]][:, :],
                            sl["Ya"][pk["i"]][:, :], start=True, stop=False),
                            reads=(f"C0{i_}", f"Ya{i_}"), banks=(pk["bkF"],))
                        B.op("tensor", lambda e, pk=pk: e.matmul(
                            pk["pf"][:, 288:416], ident[:, :],
                            sl["Ya"][pk["i"]][:, :], start=False, stop=True),
                            reads=(f"Ya{i_}", "ident"), banks=(pk["bkF"],))
                        for i in range(4):
                            B.op("scalar", lambda e, pk=pk, i=i: e.copy(
                                ubd[pk["i"]][32 * i:32 * i + 32,
                                             128 * i:128 * (i + 1)],
                                pk["pf"][32 * i:32 * i + 32, 288:416]),
                                banks=(pk["bkF"],), writes=(f"ubd{i_}",))
                    if dbg and s == 0:
                        for pk in pks:
                            dma("scalar", bass.AP(
                                dbg_u_d.tensor,
                                (c4 * 8 + pk["q"]) * 512, [[16384, 128], [1, 512]]),
                                ubd[pk["i"]][:, :],
                                reads=(f"ubd{pk['i']}",), sem="qod")
                    for pk in pks:
                        i_ = pk["i"]
                        ppR, bknR = PPR[pk["i"] % 2]
                        oco = 256 + (pk["i"] // 2) * 128
                        for i in range(4):
                            B.op("tensor", lambda e, pk=pk, i=i, ppR=ppR,
                                 oco=oco: e.matmul(
                                ppR[32 * i:32 * i + 32, oco:oco + 128],
                                fm["q"][:, pk["co"] + 32 * i:pk["co"] + 32 * i + 32],
                                pk["Sq"][:, 128 * i:128 * (i + 1)],
                                start=True, stop=True, tile_position=(0, 32 * i)),
                                reads=(pk["qfn"], pk["Sqn"]), banks=(bknR,))
                        for i in range(4):
                            B.op("tensor", lambda e, pk=pk, i=i, ppR=ppR,
                                 oco=oco: e.matmul(
                                ppR[32 * i:32 * i + 32, oco:oco + 128],
                                sl["Mt"][pk["i"]][32 * i:32 * i + 32, :],
                                ubd[pk["i"]][32 * i:32 * i + 32,
                                             128 * i:128 * (i + 1)],
                                start=False, stop=True,
                                tile_position=(32 * i, 32 * i),
                                skip_group_check=True),
                                reads=(f"Mt{i_}", f"ubd{i_}"), banks=(bknR,))
                    for pk in pks:
                        i_, sc = pk["i"], pk["sc"]
                        B.op("vector", lambda e, pk=pk, sc=sc:
                             e.tensor_single_scalar(
                                 sl["ofin"][pk["i"]][:, :],
                                 sl["vtm"][pk["i"]][:, :],
                                 scaq[:, sc + 10:sc + 11], ALU.mult),
                             reads=(f"vtm{i_}", f"omg{i_}"), writes=(f"vcon{i_}",))
                        ppR, bknR = PPR[pk["i"] % 2]
                        oco = 256 + (pk["i"] // 2) * 128
                        B.op("vector", lambda e, pk=pk, sc=sc, ppR=ppR,
                             oco=oco:
                             e.scalar_tensor_tensor(
                                 sl["oms"][pk["i"]][:, :], ppR[:, oco:oco + 128],
                                 scaq[:, sc + 9:sc + 10],
                                 sl["ofin"][pk["i"]][:, :], ALU.mult, ALU.add),
                             reads=(f"grsq{i_}", f"vcon{i_}"), banks=(bknR,),
                             writes=(f"oms{i_}",))
                        B.op("scalar", lambda e, pk=pk, sc=sc: e.activation(
                            scr_s[0][:, :], sl["oms"][pk["i"]][:, :],
                            AF.Square, accum_out=scaq[:, sc + 11:sc + 12]),
                            reads=(f"oms{i_}",),
                            writes=(f"sso{i_}", "scr0"))
                        B.op("scalar", lambda e, sc=sc: e.activation(
                            scaq[:, sc + 11:sc + 12], scaq[:, sc + 11:sc + 12],
                            AF.Sqrt, bias=1e-5, scale=1.0 / DK),
                            reads=(f"sso{i_}",), writes=(f"sso{i_}",))
                        B.op("vector", lambda e, sc=sc: e.reciprocal(
                            scaq[:, sc + 11:sc + 12], scaq[:, sc + 11:sc + 12]),
                            reads=(f"sso{i_}",), writes=(f"sso{i_}",))
                        B.op("vector", lambda e, pk=pk, sc=sc:
                             e.tensor_single_scalar(
                                 sl["ofin"][pk["i"]][:, :],
                                 sl["oms"][pk["i"]][:, :],
                                 scaq[:, sc + 11:sc + 12], ALU.mult),
                             reads=(f"oms{i_}", f"sso{i_}"), writes=(f"ofin{i_}",))
                    for pk in pks:
                        i_ = pk["i"]
                        pp, bknP = PPR[pk["i"] % 2]
                        B.op("tensor", lambda e, pk=pk, pp=pp: e.matmul(
                            pp[:, :], sl["ktm"][pk["i"]][:, :],
                            ubd[pk["i"]][:, :], start=True, stop=False),
                            reads=(f"ktm{i_}", f"ubd{i_}"), banks=(bknP,))
                        B.op("tensor", lambda e, pk=pk, pp=pp: e.matmul(
                            pp[:, :], ident[:, :], pk["Sq"][:, :],
                            start=False, stop=True),
                            reads=(pk["Sqn"], "ident"), banks=(bknP,))
                        B.op("scalar", lambda e, pk=pk, pp=pp: e.copy(
                            pk["Sq"][:, :], pp[:, :]),
                            banks=(bknP,), writes=(pk["Sqn"],))
                    for pk in pks:
                        i_ = pk["i"]
                        B.op("tensor", lambda e, pk=pk: e.transpose(
                            pk["pb"][:, pk["bo"] + 128:pk["bo"] + 256], sl["ofin"][pk["i"]][:, :],
                            ident[:, :]),
                            reads=(f"ofin{i_}", "ident"), banks=(pk["bkB"],))
                        B.op("scalar", lambda e, pk=pk, c4=c4: e.copy(
                            ofm[:, pk["q"] * ST + c4 * 128:pk["q"] * ST + c4 * 128 + 128],
                            pk["pb"][:, pk["bo"] + 128:pk["bo"] + 256]),
                            banks=(pk["bkB"],), writes=(f"ofm{pk['q']}_{c4}",))

            load_xfc(s, 0)
            for oc in range(8):
                if oc + 1 < 8:
                    load_xfc(s, oc + 1)
                pb, bkn = PPR[oc % 2]
                for a in range(8):
                    B.op("tensor", lambda e, a=a, oc=oc, pb=pb: e.matmul(
                        pb[:, 0:ST],
                        o_w[:, a * 1024 + oc * 128:a * 1024 + oc * 128 + 128],
                        ofm[:, a * ST:(a + 1) * ST], start=(a == 0), stop=(a == 7)),
                        reads=tuple(f"ofm{a}_{cc}" for cc in range(4)) + ("o_ws",),
                        banks=(bkn,), grp=("o", s, oc))
                asb = attn_sb[oc % 2]
                xin = bass.AP(xfc[oc % 2], 0,
                              [[ST, 128], [32, 4], [128, 4], [1, 32]])
                B.op("vector", lambda e, pb=pb, asb=asb, xin=xin: e.tensor_add(
                    asb[:, :], pb[:, 0:ST], xin),
                    reads=(f"xfc{oc % 2}", f"asb{oc % 2}"), banks=(bkn,),
                    writes=(f"asb{oc % 2}",))
                dma("scalar",
                    bass.AP(attn_d.tensor, oc * 128 * T + s * ST,
                            [[T, 128], [1, ST]]),
                    asb[:, :], reads=(f"asb{oc % 2}",), sem=f"qoa{oc % 2}")
            ntile = 64 + 64
            per = (ntile + nst - 1) // nst
            for k in range(per * s, min(per * (s + 1), ntile)):
                if k < 64:
                    rr, cc = k // 8, k % 8
                    src = bass.AP(wfcT_d.tensor, rr * 128 * HFF + cc * 512,
                                  [[HFF, 128], [1, 512]])
                    gi = 0
                else:
                    rr, cc = (k - 64) // 2, (k - 64) % 2
                    src = bass.AP(wprT_d.tensor, rr * 128 * D + cc * 512,
                                  [[D, 128], [1, 512]])
                    gi = 1
                dma("sync", gstage[0][:, :], src, "gstage0", sem="qgs")
                B.op("vector", lambda e: e.tensor_reduce(
                    gred[:, :], bass.AP(gstage[0], 0, [[512, 128], [16, 32], [1, 16]]),
                    mybir.AxisListType.X, ALU.add, apply_absolute_value=True),
                    reads=("gstage0",), writes=("gred",))
                B.op("vector", lambda e: e.tensor_reduce(
                    gr4[:, :], bass.AP(gred, 0, [[32, 128], [8, 4], [1, 8]]),
                    mybir.AxisListType.X, ALU.add),
                    reads=("gred",), writes=("gr4",))
                ac = gpart[:, 8 * gi:8 * gi + 4]
                cp = gpart[:, 8 * gi + 4:8 * gi + 8]
                if k in (0, 64):
                    B.op("vector", lambda e, ac=ac: e.tensor_copy(ac, gr4[:, :]),
                        reads=("gr4",), writes=(f"gpart{gi}",))
                    B.op("vector", lambda e, cp=cp: e.memset(cp, 0),
                        writes=(f"gcomp{gi}",))
                else:
                    B.op("vector", lambda e, ac=ac: e.tensor_add(
                        ktmp[:, :], ac, gr4[:, :]),
                        reads=("gr4", f"gpart{gi}"), writes=("ktmp",))
                    B.op("vector", lambda e, ac=ac: e.tensor_sub(
                        ktmp2[:, :], ac, ktmp[:, :]),
                        reads=("ktmp", f"gpart{gi}"), writes=("ktmp2",))
                    B.op("vector", lambda e: e.tensor_add(
                        ktmp2[:, :], ktmp2[:, :], gr4[:, :]),
                        reads=("ktmp2", "gr4"), writes=("ktmp2",))
                    B.op("vector", lambda e, cp=cp: e.tensor_add(
                        cp, cp, ktmp2[:, :]),
                        reads=("ktmp2", f"gcomp{gi}"), writes=(f"gcomp{gi}",))
                    B.op("vector", lambda e, ac=ac: e.tensor_copy(ac, ktmp[:, :]),
                        reads=("ktmp",), writes=(f"gpart{gi}",))

            if dbg:
                for a in range(8):
                    dma("scalar", bass.AP(dofm_d.tensor, a * 128 * T + s * ST,
                        [[T, 128], [1, ST]]), ofm[:, a * ST:(a + 1) * ST],
                        reads=tuple(f"ofm{a}_{cc}" for cc in range(4)), sem="qod")

        B.op("vector", lambda e: e.memset(P_fA[:, 0:160], 0), banks=("bk_fA",))
        B.op("vector", lambda e: e.memset(P_fB[:, 0:160], 0), banks=("bk_fB",))
        B.op("vector", lambda e: e.memset(P_sA[:, 0:160], 0), banks=("bk_sA",))
        B.op("vector", lambda e: e.memset(P_sB[:, 0:160], 0), banks=("bk_sB",))
        B.barrier()
        for s in range(nst):
            stripe_phase1(s)

        for gi in range(2):
            B.op("vector", lambda e, gi=gi: e.tensor_add(
                ktmp[:, :], gpart[:, 8 * gi:8 * gi + 4],
                gpart[:, 8 * gi + 4:8 * gi + 8]),
                reads=(f"gpart{gi}", f"gcomp{gi}"), writes=("ktmp",))
            B.op("vector", lambda e: e.tensor_reduce(
                gr4[:, 0:1], ktmp[:, :], mybir.AxisListType.X, ALU.add),
                reads=("ktmp",), writes=("gr4",))
            B.op("tensor", lambda e: e.transpose(
                P_sA[0:1, 0:128], gr4[:, 0:1], ident_f[:, :]),
                reads=("gr4", "ident_f"), banks=("bk_sA",))
            B.op("vector", lambda e: e.tensor_copy(grow[0:1, :], P_sA[0:1, 0:128]),
                 banks=("bk_sA",), writes=("grow",))
            for half in (64, 32, 16, 8, 4, 2, 1):
                B.op("vector", lambda e, half=half: e.tensor_add(
                    grow[0:1, 0:half], grow[0:1, 0:half],
                    grow[0:1, half:2 * half]),
                    reads=("grow",), writes=("grow",))
            B.op("vector", lambda e, gi=gi: e.tensor_scalar(
                gam[0:1, 4 * gi:4 * gi + 1], grow[0:1, 0:1],
                0.5 / float(HFF * D), 0.5e-6, ALU.mult, ALU.add),
                reads=("grow",), writes=(f"gam{gi}",))
            B.op("tensor", lambda e, gi=gi: e.matmul(
                P_sA[:, 0:1], ones_rf[0:1, 0:128],
                gam[0:1, 4 * gi:4 * gi + 1], start=True, stop=True),
                reads=(f"gam{gi}", "ones_rf"), banks=("bk_sA",))
            B.op("vector", lambda e, gi=gi: e.tensor_single_scalar(
                gcolp[:, gi:gi + 1], P_sA[:, 0:1], 1.0, ALU.mult),
                banks=("bk_sA",), writes=(f"gcolp{gi}",))
            B.op("vector", lambda e, gi=gi: e.tensor_single_scalar(
                gcoln[:, gi:gi + 1], P_sA[:, 0:1], -1.0, ALU.mult),
                banks=("bk_sA",), writes=(f"gcoln{gi}",))

        B.barrier()
        for k in range(40):
            qp = k % 2
            if k < 8:
                src = wfcT_d[128 * k:128 * (k + 1), :]
                dst = Wfc[:, k * HFF:(k + 1) * HFF]
                gi, ww, dn = 0, HFF, f"Wfc{k}"
            else:
                kk = k - 8
                src = wprT_d[128 * kk:128 * (kk + 1), :]
                dst = Wpr[:, kk * D:(kk + 1) * D]
                gi, ww, dn = 1, D, f"Wpr{kk}"
            dma("sync", qstage[qp][:, 0:ww], src, f"qstage{qp}", sem=f"qst{qp}")
            B.op("scalar", lambda e, qp=qp, gi=gi, ww=ww: e.activation(
                sgn[:, 0:ww], qstage[qp][:, 0:ww], AF.Sign,
                bias=gcoln[:, gi:gi + 1]),
                reads=(f"qstage{qp}", f"gcoln{gi}", "sgn_r"), writes=("sgn",))
            B.op("scalar", lambda e, qp=qp, gi=gi, ww=ww, dst=dst: e.activation(
                dst, qstage[qp][:, 0:ww], AF.Sign, bias=gcolp[:, gi:gi + 1]),
                reads=(f"qstage{qp}", f"gcolp{gi}"), writes=(dn,))
            B.op("vector", lambda e, ww=ww, dst=dst: e.tensor_add(
                dst, dst, sgn[:, 0:ww]),
                reads=("sgn", dn), writes=(dn, "sgn_r"))
            B.op("vector", lambda e, dst=dst: e.tensor_single_scalar(
                dst, dst, 0.5, ALU.mult),
                reads=(dn,), writes=(dn,))

        for half in range(2):
            n0 = half * 512
            pbh, bknH = PSP[half]
            for hc in range(32):
                lh = bass.AP(lnwb, hc, [[64, 128], [32, 2]])
                B.op("tensor", lambda e, hc=hc, n0=n0, pbh=pbh, lh=lh: e.matmul(
                    pbh[0:2, 0:512], lh, Wpr[:, hc * D + n0:hc * D + n0 + 512],
                    start=(hc == 0), stop=(hc == 31)),
                    reads=(f"Wpr{hc}", "lnw", "lnb"), banks=(bknH,))
            B.op("vector", lambda e, n0=n0, pbh=pbh: e.tensor_copy(
                c12[:, n0:n0 + 512], pbh[0:2, 0:512]),
                banks=(bknH,), writes=("c12",))
        for hc in range(32):
            B.op("vector", lambda e, hc=hc: e.tensor_scalar(
                Wpr[:, hc * D:(hc + 1) * D], Wpr[:, hc * D:(hc + 1) * D],
                lnw_f[:, hc:hc + 1], None, ALU.mult),
                reads=(f"Wpr{hc}", "lnwf"), writes=(f"Wpr{hc}",))

        B.barrier()
        for jc in range(8):
            for a in range(8):
                B.op("tensor", lambda e, jc=jc, a=a: e.matmul(
                    P_sA[0:1, 0:512], ones_c[:, :],
                    Wfc[:, a * HFF + jc * 512:a * HFF + jc * 512 + 512],
                    start=(a == 0), stop=(a == 7)),
                    reads=(f"Wfc{a}", "ones_c"), banks=("bk_sA",))
            B.op("vector", lambda e, jc=jc: e.tensor_copy(
                csrow[0:1, jc * 512:(jc + 1) * 512], P_sA[0:1, 0:512]),
                banks=("bk_sA",), writes=("csrow",))

        def load_att(s):
            for a in range(8):
                src = bass.AP(attn_d.tensor, a * 128 * T + s * ST,
                              [[T, 128], [1, ST]])
                dma("gpsimd", attb[:, a * ST:(a + 1) * ST], src, f"attb_{a}",
                    sem="qat2")

        def load_attf(s, a):
            src = bass.AP(attn_d.tensor, a * 128 * T + s * ST,
                          [[T, 128], [1, ST]])
            dma("sync", attf[a % 2][:, :], src, f"attf{a % 2}", sem="qat")

        load_att(0)
        for s in range(nst):
            ln_stats_rows(lambda a: attb[:, a * ST:(a + 1) * ST], "attb",
                          lambda a: attb[:, a * ST:(a + 1) * ST], x2s_p2,
                          "x2p", D)
            bc_row(rbf_rs[0:1, :], "bc_f", bc_f, PSP[1])
            def fc_group(j):
                pb, bkn = P4[j % 4]
                for a in range(8):
                    B.op("tensor", lambda e, a=a, j=j, pb=pb: e.matmul(
                        pb[:, 0:ST],
                        Wfc[:, a * HFF + j * 128:a * HFF + j * 128 + 128],
                        attb[:, a * ST:(a + 1) * ST],
                        start=(a == 0), stop=False),
                        reads=(f"attb_{a}", f"Wfc{a}"), banks=(bkn,),
                        grp=("fc", s, j))
                B.op("tensor", lambda e, j=j, pb=pb: e.matmul(
                    pb[:, 0:ST], csrow[0:1, j * 128:(j + 1) * 128],
                    rbf_mu[0:1, :], start=False, stop=True),
                    reads=("csrow", "rbf_mu"), banks=(bkn,),
                    grp=("fc", s, j))

            def fc_epi(j):
                pb, bkn = P4[j % 4]
                B.op("vector", lambda e, j=j, pb=pb: e.tensor_mul(
                    ysb[:, :], pb[:, 0:ST], bc_f[:, :]),
                    banks=(bkn,), reads=("bc_f",), writes=("ysb",))
                B.op("scalar", lambda e, j=j: e.activation(
                    h_sb[:, j * ST:(j + 1) * ST], ysb[:, :], AF.Gelu),
                    reads=("ysb",), writes=(f"h_{j}",))
                B.op("tensor", lambda e, j=j: e.matmul(
                    P_fA[0:1, 0:ST], ones_c[:, :], h_sb[:, j * ST:(j + 1) * ST],
                    start=(j == 0), stop=(j == 31)),
                    reads=(f"h_{j}", "ones_c"), banks=("bk_fA",))
                B.op("scalar", lambda e, j=j: e.activation(
                    x2s_p2[j % 2][:, :], h_sb[:, j * ST:(j + 1) * ST], AF.Square),
                    reads=(f"h_{j}",), writes=(f"x2p{j % 2}",))
                B.op("tensor", lambda e, j=j: e.matmul(
                    P_fB[0:1, 0:ST], ones_c[:, :], x2s_p2[j % 2][:, :],
                    start=(j == 0), stop=(j == 31)),
                    reads=(f"x2p{j % 2}", "ones_c"), banks=("bk_fB",))

            for j in range(32):
                fc_group(j)
                if j > 0:
                    fc_epi(j - 1)
            fc_epi(31)
            B.op("vector", lambda e: e.tensor_single_scalar(
                rows[0:1, 0:ST], P_fA[0:1, 0:ST], -1.0 / HFF, ALU.mult),
                banks=("bk_fA",), writes=("rows0",))
            B.op("vector", lambda e: e.tensor_single_scalar(
                rows[0:1, ST:2 * ST], P_fB[0:1, 0:ST], 1.0 / HFF, ALU.mult),
                banks=("bk_fB",), writes=("rows1",))
            B.op("vector", lambda e: e.tensor_mul(
                bc_f[0:1, 0:ST], rows[0:1, 0:ST], rows[0:1, 0:ST]),
                reads=("rows0",), writes=("bc_f",))
            B.op("vector", lambda e: e.tensor_sub(
                rows[0:1, ST:2 * ST], rows[0:1, ST:2 * ST], bc_f[0:1, 0:ST]),
                reads=("rows1", "bc_f"), writes=("rows1",))
            B.op("scalar", lambda e: e.activation(
                rows[0:1, ST:2 * ST], rows[0:1, ST:2 * ST], AF.Sqrt, bias=1e-5),
                reads=("rows1",), writes=("rows1",))
            dma("gpsimd", bass.AP(crhs, 0, [[ST, 2], [1, ST]]),
                rows[0:1, 0:2 * ST], "crhs0",
                reads=("rows0", "rows1"), sem="qcr")
            B.op("vector", lambda e: e.reciprocal(
                rows[0:1, ST:2 * ST], rows[0:1, ST:2 * ST]),
                reads=("rows1",), writes=("rows1",))
            B.op("vector", lambda e: e.tensor_copy(rbf_rs[0:1, :],
                                                   rows[0:1, ST:2 * ST]),
                 reads=("rows1",), writes=("rbf_rs",))
            bc_row(rbf_rs[0:1, :], "bc_f", bc_f, PSP[1])
            load_attf(s, 0)
            for dcn in range(8):
                if dcn + 1 < 8:
                    load_attf(s, dcn + 1)
                pb, bkn = P4[dcn % 4]
                for j in range(32):
                    B.op("tensor", lambda e, j=j, dcn=dcn, pb=pb: e.matmul(
                        pb[:, 0:ST],
                        Wpr[:, j * D + dcn * 128:j * D + dcn * 128 + 128],
                        h_sb[:, j * ST:(j + 1) * ST],
                        start=(j == 0), stop=False),
                        reads=(f"h_{j}", f"Wpr{j}"), banks=(bkn,),
                        grp=("pr", s, dcn))
                B.op("tensor", lambda e, dcn=dcn, pb=pb: e.matmul(
                    pb[:, 0:ST], c12[:, dcn * 128:(dcn + 1) * 128],
                    crhs[:, :], start=False, stop=True),
                    reads=("c12", "crhs0"), banks=(bkn,),
                    grp=("pr", s, dcn))
                B.op("vector", lambda e, dcn=dcn, pb=pb: e.tensor_mul(
                    ysb[:, :], pb[:, 0:ST], bc_f[:, :]),
                    banks=(bkn,), reads=("bc_f",), writes=("ysb",))
                B.op("vector", lambda e, dcn=dcn: e.tensor_add(
                    ysb[:, :], ysb[:, :], attf[dcn % 2][:, :]),
                    reads=(f"attf{dcn % 2}",), writes=("ysb",))
                dma("scalar",
                    bass.AP(y_d.tensor, dcn * 128 * T + s * ST,
                            [[T, 128], [1, ST]]),
                    ysb[:, :], reads=("ysb",), sem="qoy")
                if s + 1 < nst:
                    src = bass.AP(attn_d.tensor, dcn * 128 * T + (s + 1) * ST,
                                  [[T, 128], [1, ST]])
                    dma("gpsimd", attb[:, dcn * ST:(dcn + 1) * ST], src,
                        f"attb_{dcn}", sem="qat2")
        B.op("gpsimd", lambda e: e.engine_nop(),
             extra_waits=[(q, B.cnt[q]) for q in
                          ("qoa0", "qoa1", "qoy", "qod")], sem="G", inc=1)


    with nc.Block() as block:
        def mk(name):
            def run(eng):
                B.begin_pass(name, eng)
                schedule()
            return run
        block.sync(mk("sync"))
        block.gpsimd(mk("gpsimd"))
        block.tensor(mk("tensor"))
        block.vector(mk("vector"))
        block.scalar(mk("scalar"))

    p2b.close()
    p2w.close()
    stk.close()
    return nc


def _consts():
    ident = np.eye(128).astype(bfloat16)
    mbd = np.zeros((128, 128), np.float32)
    for i in range(4):
        mbd[32 * i:32 * i + 32, 32 * i:32 * i + 32] = np.tril(
            np.ones((32, 32), np.float32), -1)
    masku = np.tile(np.triu(np.ones((32, 32), np.float32), 0), (4, 1))
    return {
        "c_ident": ident,
        "c_maskbd": mbd.astype(bfloat16),
        "c_masku": masku.astype(bfloat16),
        "c_ones": np.ones((128, 1), bfloat16),
        "c_onesf": np.ones((128, 1), np.float32),
        "c_onesr": np.ones((1, ST), bfloat16),
        "c_onesrf": np.ones((1, 128), np.float32),
        "c_identf": np.eye(128, dtype=np.float32),
        "c_eps": np.tile(np.array([1e-5, 1e-12, 1.0 / 128, 0.0],
                                  np.float32), (128, 1)),
    }


def _in_map(ci, inputs, L=2048):
    g = lambda k: np.ascontiguousarray(np.asarray(inputs["L0_attn_" + k],
                                                  np.float32))
    gf = lambda k: np.ascontiguousarray(np.asarray(inputs["L0_ffn_" + k],
                                                   np.float32))
    x = np.asarray(inputs["L0_attn_hidden_states"], np.float32)
    xs = x[NSEQ * ci:NSEQ * (ci + 1), :L, :]
    x_fm = np.ascontiguousarray(
        xs.reshape(NSEQ, L // 128, 128, D).transpose(3, 1, 0, 2)
        .reshape(D, NSEQ * L))
    im = {
        "x": x_fm,
        "w_q": g("q_w"), "w_k": g("k_w"), "w_v": g("v_w"),
        "cw_q": g("conv_q_w"), "cw_k": g("conv_k_w"), "cw_v": g("conv_v_w"),
        "b_w": g("b_w"), "mix_w": g("mix_w"),
        "mix_b": g("mix_b").reshape(H, 1),
        "mix_bias": g("mix_bias").reshape(H, 1),
        "o_norm_w": g("o_norm_w").reshape(DK, 1), "o_w": g("o_w"),
        "w_fcT": np.ascontiguousarray(gf("W_fc").T),
        "ln_w": gf("ln_w"), "ln_b": gf("ln_b"),
        "w_prT": np.ascontiguousarray(gf("W_proj").T),
    }
    im.update(_consts())
    return im


def _decode_y(ycore, L=2048):
    nst = L // 128
    yr = np.asarray(ycore, np.float32).reshape(D, nst, 4, NSEQ, 32)
    return np.ascontiguousarray(
        yr.transpose(3, 1, 2, 4, 0).reshape(NSEQ, L, D))


def kernel(**inputs):
    nc = build_core(L=2048)
    in_maps = [_in_map(ci, inputs) for ci in range(8)]
    res = run_bass_kernel_spmd(nc, in_maps, core_ids=list(range(8)))
    outs = [_decode_y(r["y"]) for r in res.results]
    return np.concatenate(outs, axis=0)


# version 14
# speedup vs baseline: 1.0164x; 1.0164x over previous
import contextlib
import numpy as np
import ml_dtypes
import concourse.bass as bass
import concourse.mybir as mybir
from concourse.bass_utils import run_bass_kernel_spmd

F32 = mybir.dt.float32
BF16 = mybir.dt.bfloat16
AF = mybir.ActivationFunctionType
ALU = mybir.AluOpType
bfloat16 = ml_dtypes.bfloat16

NSEQ = 4
D = 1024
H = 8
DK = 128
CH = 32
HFF = 4096
ST = 512
ENG2SEM = {"sync": "qc", "gpsimd": "G", "tensor": "T",
           "vector": "V", "scalar": "S"}


DMA_SEMS = {"qc", "qcg", "qxb0", "qxb1", "qxf0", "qxf1", "qgs", "qst0", "qst1", "qat", "qat2", "qcr", "qoa0", "qoa1", "qoy", "qod"}


class Emitter:

    def __init__(self, nc, sems):
        self.nc = nc
        self.sems = sems

    def begin_pass(self, engine, eng):
        self.cur = engine
        self.eng = eng
        self.cnt = {k: 0 for k in self.sems}
        self.res = {}
        self.bk = {}
        self.lastwait = {}

    def _wait(self, sem, val):
        if val <= 0 or self.lastwait.get(sem, 0) >= val:
            return
        self.lastwait[sem] = val
        self.eng.wait_ge(self.sems[sem], val)

    def op(self, engine, fn, reads=(), writes=(), banks=(), sem=None, inc=1,
           extra_waits=(), grp=None):
        if sem is None:
            sem = ENG2SEM[engine]
        if self.cur == engine:
            own = ENG2SEM[engine]
            waits = {}

            def add(s, v, selfok=False):
                if (s == own and not selfok and s not in DMA_SEMS
                        and engine == "tensor"):
                    return
                waits[s] = max(waits.get(s, 0), v)

            for r in reads:
                e = self.res.get(r)
                if e and e["w"]:
                    add(*e["w"])
            for r in writes:
                e = self.res.get(r)
                if e:
                    if e["w"]:
                        add(*e["w"])
                    for sv in e["r"].values():
                        add(*sv)
            for b in banks:
                for oeng, (sv, og) in self.bk.get(b, {}).items():
                    if oeng == engine and engine == "tensor":
                        continue
                    add(sv[0], sv[1], selfok=True)
            for s, v in extra_waits:
                add(s, v, selfok=True)
            for s in sorted(waits):
                if s in DMA_SEMS:
                    self._wait(s, self.cnt[s])
                else:
                    self._wait(s, waits[s])
            inst = fn(self.eng)
            inst.then_inc(self.sems[sem], inc)
        self.cnt[sem] += inc
        snap = (sem, self.cnt[sem])
        for r in writes:
            self.res[r] = {"w": snap, "r": {}}
        for r in reads:
            e = self.res.setdefault(r, {"w": None, "r": {}})
            e["r"][engine] = snap
        for b in banks:
            self.bk.setdefault(b, {})[engine] = (snap, grp)
        return snap

    def barrier(self):
        for s, v in [(s, self.cnt[s]) for s in self.sems]:
            self._wait(s, v)
        self.res = {}
        self.bk = {}


def build_core(L=2048, dbg=False):
    nst = L // 128
    T = L * NSEQ

    nc = bass.Bass("TRN2", target_bir_lowering=False,
                   detect_race_conditions=False)
    dt = nc.dram_tensor

    x_d = dt("x", [D, T], F32, kind="ExternalInput").ap()
    wq_d = dt("w_q", [D, D], F32, kind="ExternalInput").ap()
    wk_d = dt("w_k", [D, D], F32, kind="ExternalInput").ap()
    wv_d = dt("w_v", [D, D], F32, kind="ExternalInput").ap()
    cwq_d = dt("cw_q", [D, 4], F32, kind="ExternalInput").ap()
    cwk_d = dt("cw_k", [D, 4], F32, kind="ExternalInput").ap()
    cwv_d = dt("cw_v", [D, 4], F32, kind="ExternalInput").ap()
    bw_d = dt("b_w", [D, H], F32, kind="ExternalInput").ap()
    mw_d = dt("mix_w", [D, H], F32, kind="ExternalInput").ap()
    mb_d = dt("mix_b", [H, 1], F32, kind="ExternalInput").ap()
    mb2_d = dt("mix_bias", [H, 1], F32, kind="ExternalInput").ap()
    onw_d = dt("o_norm_w", [DK, 1], F32, kind="ExternalInput").ap()
    ow_d = dt("o_w", [D, D], F32, kind="ExternalInput").ap()
    wfcT_d = dt("w_fcT", [D, HFF], F32, kind="ExternalInput").ap()
    lnw_d = dt("ln_w", [HFF], F32, kind="ExternalInput").ap()
    lnb_d = dt("ln_b", [HFF], F32, kind="ExternalInput").ap()
    wprT_d = dt("w_prT", [HFF, D], F32, kind="ExternalInput").ap()
    ident_d = dt("c_ident", [128, 128], BF16, kind="ExternalInput").ap()
    maskbd_d = dt("c_maskbd", [128, 128], BF16, kind="ExternalInput").ap()
    masku_d = dt("c_masku", [128, CH], BF16, kind="ExternalInput").ap()
    ones_d = dt("c_ones", [128, 1], BF16, kind="ExternalInput").ap()
    onesf_d = dt("c_onesf", [128, 1], F32, kind="ExternalInput").ap()
    onesr_d = dt("c_onesr", [1, ST], BF16, kind="ExternalInput").ap()
    eps_d = dt("c_eps", [128, 4], F32, kind="ExternalInput").ap()
    onesrf_d = dt("c_onesrf", [1, 128], F32, kind="ExternalInput").ap()
    identf_d = dt("c_identf", [128, 128], F32, kind="ExternalInput").ap()

    y_d = dt("y", [D, T], F32, kind="ExternalOutput").ap()
    attn_kind = {"kind": "ExternalOutput"} if dbg else {}
    attn_d = dt("attn_s", [D, T], F32, **attn_kind).ap()
    if dbg:
        dq_d = dt("dbg_q", [D, T], BF16, kind="ExternalOutput").ap()
        dk_d = dt("dbg_k", [D, T], BF16, kind="ExternalOutput").ap()
        dv_d = dt("dbg_v", [D, T], BF16, kind="ExternalOutput").ap()
        dbg_bg_d = dt("dbg_bg", [H, 2 * T], BF16, kind="ExternalOutput").ap()
        dbg_u_d = dt("dbg_u", [128, 16384], BF16, kind="ExternalOutput").ap()
        dofm_d = dt("dbg_ofm", [D, T], BF16, kind="ExternalOutput").ap()

    stk = contextlib.ExitStack()
    sbG = lambda n, sh, d_: stk.enter_context(nc.sbuf_tensor(n, sh, d_))
    psG = lambda n, sh, d_: stk.enter_context(nc.psum_tensor(n, sh, d_))
    sems = {s: stk.enter_context(nc.semaphore(s))
            for s in ("qc", "qcg", "qxb0", "qxb1", "qxf0", "qxf1", "qgs", "qst0", "qst1", "qat", "qat2", "qcr", "qoa0", "qoa1", "qoy", "qod",
                      "G", "T", "V", "S")}

    P_prA = psG("P_prA", [128, 512], F32)
    P_prB = psG("P_prB", [128, 512], F32)
    P_fA = psG("P_fA", [128, 512], F32)
    P_fB = psG("P_fB", [128, 512], F32)
    P_bA = psG("P_bA", [128, 1024], BF16)
    P_bB = psG("P_bB", [128, 1024], BF16)
    P_sA = psG("P_sA", [128, 512], F32)
    P_sB = psG("P_sB", [128, 512], F32)
    PPR = [(P_prA, "bk_prA"), (P_prB, "bk_prB")]
    PF = [(P_fA, "bk_fA"), (P_fB, "bk_fB")]
    PBb = [(P_bA, "bk_bA"), (P_bB, "bk_bB")]
    PSP = [(P_sA, "bk_sA"), (P_sB, "bk_sB")]
    P4 = PPR + PSP

    ident = sbG("ident", [128, 128], BF16)
    maskbd = sbG("maskbd", [128, 128], BF16)
    masku = sbG("masku", [128, CH], BF16)
    ones_c = sbG("ones_c", [128, 1], BF16)
    ones_cf = sbG("ones_cf", [128, 1], F32)
    ones_r = sbG("ones_r", [1, ST], BF16)
    ones_rf = sbG("ones_rf", [1, 128], F32)
    rows = sbG("rows", [1, 2 * ST], F32)
    rbf_mu = sbG("rbf_mu", [1, ST], BF16)
    rbf_rs = sbG("rbf_rs", [1, ST], BF16)
    bc_f = sbG("bc_f", [128, ST], F32)
    gam = sbG("gam", [1, 8], F32)
    gcoln = sbG("gcoln", [128, 2], F32)
    gcolp = sbG("gcolp", [128, 2], F32)
    gpart = sbG("gpart", [128, 16], F32)
    gred = sbG("gred", [128, 32], F32)
    gr4 = sbG("gr4", [128, 4], F32)
    ktmp = sbG("ktmp", [128, 4], F32)
    ktmp2 = sbG("ktmp2", [128, 4], F32)
    grow = sbG("grow", [1, 128], F32)
    lnwb = sbG("lnwb", [128, 64], BF16)
    lnw_f = sbG("lnw_f", [128, 32], F32)
    c12 = sbG("c12", [2, D], BF16)
    eps_s = sbG("eps_s", [128, 4], F32)
    crhs = sbG("crhs", [2, ST], BF16)


    p1 = contextlib.ExitStack()
    sb1 = lambda n, sh, d_: p1.enter_context(nc.sbuf_tensor(n, sh, d_))
    row_scr = sb1("row_scr", [1, HFF], BF16)
    w_q = sb1("w_qs", [128, 8192], BF16)
    w_k = sb1("w_ks", [128, 8192], BF16)
    w_v = sb1("w_vs", [128, 8192], BF16)
    o_w = sb1("o_ws", [128, 8192], BF16)
    bw_s = sb1("bw_s", [128, 64], BF16)
    mw_s = sb1("mw_s", [128, 64], BF16)
    mb_s = sb1("mb_s", [8, 2], F32)
    onw_s = sb1("onw_s", [128, 1], F32)
    cw = {p: sb1(f"cwsb_{p}", [128, 32], F32) for p in "qkv"}
    colsum = sb1("colsum", [1, 3 * D + 2 * H], BF16)
    xfc = [sb1(f"xfc{i}", [128, ST], F32) for i in range(2)]
    xb_ = [sb1(f"xb{i}", [128, 8 * ST], BF16) for i in range(2)]
    fm = {p: sb1(f"fm{p}", [128, 8 * ST], BF16) for p in "qkv"}
    convb = [sb1(f"convb{i}", [128, 4 * 131], BF16) for i in range(2)]
    acc0 = sb1("acc0", [128, ST], BF16)
    acc1 = sb1("acc1", [128, ST], BF16)
    tails = sb1("tails", [128, 24 * 12], BF16)
    bgfm = sb1("bgfm", [8, 2 * ST], BF16)
    bgtm = sb1("bgtm", [128, 64], BF16)
    ofm = sb1("ofm", [128, 8 * ST], BF16)
    attn_sb = [sb1(f"attnsb{i}", [128, ST], F32) for i in range(2)]
    S_bf = sb1("S_bf", [128, H * 512], BF16)
    NSL = 4
    sl = {}
    for nm, w, d_ in (("Abd", 128, BF16), ("C0", 128, BF16), ("D1", 128, BF16),
                      ("C1", 128, BF16), ("D2", 128, BF16), ("C2", 128, BF16),
                      ("D3", 128, BF16), ("C3", 128, BF16), ("C4", 128, BF16),
                      ("ktm", 128, BF16), ("vtm", 128, BF16), ("Xu", 128, BF16),
                      ("Ya", 128, BF16), ("Yb", 128, BF16), ("oms", 128, BF16),
                      ("ofin", 128, BF16), ("Mt", CH, BF16)):
        sl[nm] = [sb1(f"sl_{nm}{i}", [128, w], d_) for i in range(NSL)]
    ubd = [sb1(f"ubd{i}", [128, 512], BF16) for i in range(NSL)]
    scaq = sb1("scaq", [128, 12 * H], F32)
    scr_s = [sb1("scr_s0", [128, 128], BF16)]
    gstage = [sb1("gstage0", [128, 512], F32)]
    ident_f = sb1("ident_f", [128, 128], F32)
    p1.close()

    p2w = contextlib.ExitStack()
    sb2w = lambda n, sh, d_: p2w.enter_context(nc.sbuf_tensor(n, sh, d_))
    Wfc = sb2w("Wfc", [128, 8 * HFF], BF16)
    Wpr = sb2w("Wpr", [128, 32 * D], BF16)
    p2t = contextlib.ExitStack()
    sb2t = lambda n, sh, d_: p2t.enter_context(nc.sbuf_tensor(n, sh, d_))
    qstage = [sb2t(f"qstage{i}", [128, HFF], F32) for i in range(2)]
    sgn = sb2t("sgn", [128, HFF], BF16)
    p2t.close()
    p2b = contextlib.ExitStack()
    sb2b = lambda n, sh, d_: p2b.enter_context(nc.sbuf_tensor(n, sh, d_))
    attb = sb2b("attb", [128, 8 * ST], BF16)
    attf = [sb2b(f"attf{i}", [128, ST], F32) for i in range(2)]
    h_sb = sb2b("h_sb", [128, 32 * ST], BF16)
    csrow = sb2b("csrow", [1, HFF], BF16)
    ysb = sb2b("ysb", [128, ST], F32)
    x2s_p2 = [sb2b(f"x2s_p2{i}", [128, ST], BF16) for i in range(2)]

    nc.const_aps.register(F32, 1e-5, eps_s[:, 0:1])
    nc.const_aps.register(F32, 1e-12, eps_s[:, 1:2])
    nc.const_aps.register(F32, 1.0 / DK, eps_s[:, 2:3])

    B = Emitter(nc, sems)

    def dma(q, dst, src, tag=None, reads=(), extra=(), sem="qc"):
        return B.op(q, lambda e: e.dma_start(out=dst, in_=src),
                    reads=reads, writes=(tag,) if tag else (),
                    sem=sem, inc=16, extra_waits=extra)

    def schedule():
        consts = [
            dma("sync", ident[:, :], ident_d[:, :], "ident"),
            dma("sync", maskbd[:, :], maskbd_d[:, :], "maskbd"),
            dma("sync", masku[:, :], masku_d[:, :], "masku"),
            dma("sync", ones_c[:, :], ones_d[:, :], "ones_c"),
            dma("sync", ones_cf[:, :], onesf_d[:, :], "ones_cf"),
            dma("sync", ones_r[:, :], onesr_d[:, :], "ones_r"),
            dma("sync", ones_rf[:, :], onesrf_d[:, :], "ones_rf"),
            dma("sync", onw_s[:, :], onw_d[:, :], "onw"),
            dma("sync", mb_s[:, 0:1], mb_d[:, :], "mb0"),
            dma("sync", mb_s[:, 1:2], mb2_d[:, :], "mb1"),
            dma("sync", eps_s[:, :], eps_d[:, :], "eps"),
            dma("sync", ident_f[:, :], identf_d[:, :], "ident_f"),
        ]
        for wsb, wd in ((w_q, wq_d), (w_k, wk_d), (w_v, wv_d), (o_w, ow_d)):
            src = bass.AP(wd.tensor, 0, [[D, 128], [128 * D, 8], [1, D]])
            dst = bass.AP(wsb, 0, [[8192, 128], [1024, 8], [1, 1024]])
            consts.append(dma("gpsimd", dst, src, wsb.name, sem="qcg"))
        for wsb, wd in ((bw_s, bw_d), (mw_s, mw_d)):
            src = bass.AP(wd.tensor, 0, [[H, 128], [128 * H, 8], [1, H]])
            dst = bass.AP(wsb, 0, [[64, 128], [8, 8], [1, 8]])
            consts.append(dma("gpsimd", dst, src, wsb.name, sem="qcg"))
        for p, cd in (("q", cwq_d), ("k", cwk_d), ("v", cwv_d)):
            src = bass.AP(cd.tensor, 0, [[4, 128], [512, 8], [1, 4]])
            dst = bass.AP(cw[p], 0, [[32, 128], [4, 8], [1, 4]])
            consts.append(dma("sync", dst, src, f"cw{p}"))
        consts.append(dma("gpsimd", row_scr[0:1, :],
                          bass.AP(lnw_d.tensor, 0, [[HFF, 1], [1, HFF]]), "lnw_r",
                          sem="qcg"))

        def load_xb(s):
            par = s % 2
            for a in range(8):
                src = bass.AP(x_d.tensor, a * 128 * T + ST * s,
                              [[T, 128], [1, ST]])
                dma("gpsimd", xb_[par][:, a * ST:(a + 1) * ST], src,
                    f"xb{par}_{a}", sem=f"qxb{par}")

        def load_xfc(s, a):
            src = bass.AP(x_d.tensor, a * 128 * T + ST * s,
                          [[T, 128], [1, ST]])
            dma("sync", xfc[a % 2][:, :], src, f"xfc{a % 2}", sem=f"qxf{a % 2}")

        load_xb(0)

        B.op("vector", lambda e: e.tensor_add(mb_s[:, 0:1], mb_s[:, 0:1],
                                              mb_s[:, 1:2]),
             reads=("mb0", "mb1"), writes=("mbsum",), extra_waits=consts)
        B.op("vector", lambda e: e.tensor_scalar(
            o_w[:, :], o_w[:, :], onw_s[:, :], None, ALU.mult),
            reads=("o_ws", "onw"), writes=("o_ws",))
        for nm, wsb, off, width in (("q", w_q, 0, D), ("k", w_k, D, D),
                                    ("v", w_v, 2 * D, D),
                                    ("b", bw_s, 3 * D, H),
                                    ("m", mw_s, 3 * D + H, H)):
            for half in range(max(1, width // 512)):
                n0 = half * 512
                nn = min(width - n0, 512)
                for a in range(8):
                    rhs = (wsb[:, a * 1024 + n0:a * 1024 + n0 + nn]
                           if width == D else wsb[:, a * 8:(a + 1) * 8])
                    B.op("tensor", lambda e, rhs=rhs, a=a, nn=nn: e.matmul(
                        P_sA[0:1, 0:nn], ones_c[:, :], rhs,
                        start=(a == 0), stop=(a == 7)),
                        reads=(wsb.name, "ones_c"), banks=("bk_sA",))
                B.op("vector", lambda e, off=off, n0=n0, nn=nn: e.tensor_copy(
                    colsum[0:1, off + n0:off + n0 + nn], P_sA[0:1, 0:nn]),
                    banks=("bk_sA",), writes=("colsum",))
        for ci in range(32):
            B.op("tensor", lambda e, ci=ci: e.matmul(
                P_sB[:, ci:ci + 1], row_scr[0:1, ci * 128:(ci + 1) * 128],
                ones_r[0:1, 0:1], start=True, stop=True),
                reads=("lnw_r", "ones_r"), banks=("bk_sB",))
        B.op("vector", lambda e: e.tensor_copy(lnwb[:, 0:32], P_sB[:, 0:32]),
             banks=("bk_sB",), writes=("lnw",))
        B.op("vector", lambda e: e.tensor_copy(lnw_f[:, :], P_sB[:, 0:32]),
             banks=("bk_sB",), writes=("lnwf",))
        dma("gpsimd", row_scr[0:1, :],
            bass.AP(lnb_d.tensor, 0, [[HFF, 1], [1, HFF]]), "lnw_r",
            reads=("lnw_r",), sem="qcg")
        for ci in range(32):
            B.op("tensor", lambda e, ci=ci: e.matmul(
                P_sB[:, 32 + ci:33 + ci], row_scr[0:1, ci * 128:(ci + 1) * 128],
                ones_r[0:1, 0:1], start=True, stop=True),
                reads=("lnw_r", "ones_r"), banks=("bk_sB",))
        B.op("vector", lambda e: e.tensor_copy(lnwb[:, 32:64], P_sB[:, 32:64]),
             banks=("bk_sB",), writes=("lnb",))
        for i in range(NSL):
            B.op("gpsimd", lambda e, i=i: e.memset(ubd[i][:, :], 0),
                 writes=(f"ubd{i}",), sem="G", inc=1)
        for q in range(H):
            B.op("gpsimd", lambda e, q=q: e.memset(
                S_bf[:, q * 512:(q + 1) * 512], 0),
                writes=(f"S_{q}",), sem="G", inc=1)

        def ln_stats_rows(get_rhs, xtag, sqsrc, x2t, x2tag, dd):
            for a in range(8):
                B.op("tensor", lambda e, a=a: e.matmul(
                    P_sA[0:1, 0:ST], ones_c[:, :], get_rhs(a),
                    start=(a == 0), stop=(a == 7)),
                    reads=(f"{xtag}_{a}", "ones_c"), banks=("bk_sA",))
            for a in range(8):
                B.op("scalar", lambda e, a=a: e.activation(
                    x2t[a % 2][:, :], sqsrc(a), AF.Square),
                    reads=(f"{xtag}_{a}",), writes=(f"{x2tag}{a % 2}",))
                B.op("tensor", lambda e, a=a: e.matmul(
                    P_sB[0:1, 0:ST], ones_c[:, :], x2t[a % 2][:, :],
                    start=(a == 0), stop=(a == 7)),
                    reads=(f"{x2tag}{a % 2}", "ones_c"), banks=("bk_sB",))
            B.op("vector", lambda e, dd=dd: e.tensor_single_scalar(
                rows[0:1, 0:ST], P_sA[0:1, 0:ST], -1.0 / dd, ALU.mult),
                banks=("bk_sA",), writes=("rows0",))
            B.op("vector", lambda e, dd=dd: e.tensor_single_scalar(
                rows[0:1, ST:2 * ST], P_sB[0:1, 0:ST], 1.0 / dd, ALU.mult),
                banks=("bk_sB",), writes=("rows1",))
            B.op("vector", lambda e: e.tensor_mul(
                bc_f[0:1, 0:ST], rows[0:1, 0:ST], rows[0:1, 0:ST]),
                reads=("rows0",), writes=("bc_f",))
            B.op("vector", lambda e: e.tensor_sub(
                rows[0:1, ST:2 * ST], rows[0:1, ST:2 * ST], bc_f[0:1, 0:ST]),
                reads=("rows1", "bc_f"), writes=("rows1",))
            B.op("scalar", lambda e: e.activation(
                rows[0:1, ST:2 * ST], rows[0:1, ST:2 * ST], AF.Sqrt, bias=1e-5),
                reads=("rows1",), writes=("rows1",))
            B.op("vector", lambda e: e.reciprocal(rows[0:1, ST:2 * ST], rows[0:1, ST:2 * ST]),
                 reads=("rows1",), writes=("rows1",))
            B.op("vector", lambda e: e.tensor_copy(rbf_mu[0:1, :], rows[0:1, 0:ST]),
                 reads=("rows0",), writes=("rbf_mu",))
            B.op("vector", lambda e: e.tensor_copy(rbf_rs[0:1, :], rows[0:1, ST:2 * ST]),
                 reads=("rows1",), writes=("rbf_rs",))

        def bc_row(src_row, dsttag, dst, bank):
            bk, bkn = bank
            B.op("tensor", lambda e: e.matmul(
                bk[:, 0:ST], ones_r[0:1, 0:128], src_row, start=True, stop=True),
                reads=("rbf_rs", "rbf_mu", "ones_r"), banks=(bkn,))
            B.op("vector", lambda e: e.tensor_copy(dst[:, :], bk[:, 0:ST]),
                 banks=(bkn,), writes=(dsttag,))

        def stripe_phase1(s):
            par = s % 2
            xbs = xb_[par]
            if s + 1 < nst:
                load_xb(s + 1)
            ln_stats_rows(lambda a: xbs[:, a * ST:(a + 1) * ST], f"xb{par}",
                          lambda a: xbs[:, a * ST:(a + 1) * ST], [acc0, acc1],
                          "acc", D)
            bc_row(rbf_rs[0:1, :], "bc_f", bc_f, PPR[0])

            for gi, (wsb, csoff) in enumerate(((bw_s, 3 * D), (mw_s, 3 * D + H))):
                pb, bkn = PSP[gi]
                for a in range(8):
                    B.op("tensor", lambda e, a=a, wsb=wsb, pb=pb: e.matmul(
                        pb[0:8, 0:ST], wsb[:, a * 8:(a + 1) * 8],
                        xbs[:, a * ST:(a + 1) * ST], start=(a == 0), stop=False),
                        reads=(f"xb{par}_{a}", wsb.name), banks=(bkn,),
                        grp=("bg", s, gi))
                B.op("tensor", lambda e, csoff=csoff, pb=pb: e.matmul(
                    pb[0:8, 0:ST], colsum[0:1, csoff:csoff + 8],
                    rbf_mu[0:1, :], start=False, stop=True),
                    reads=("colsum", "rbf_mu"), banks=(bkn,),
                    grp=("bg", s, gi))
                B.op("vector", lambda e, pb=pb: e.tensor_mul(
                    pb[0:8, 0:ST], pb[0:8, 0:ST], bc_f[0:8, :]),
                    reads=("bc_f",), banks=(bkn,))
                outap = bass.AP(bgfm, gi * ST,
                                [[2 * ST, 8], [32, 4], [128, 4], [1, 32]])
                bias = mb_s[:, 0:1] if gi == 1 else 0.0
                B.op("scalar", lambda e, pb=pb, outap=outap, bias=bias:
                     e.activation(outap, pb[0:8, 0:ST], AF.Sigmoid, bias=bias),
                     reads=("mbsum",), banks=(bkn,), writes=(f"bgfm{gi}",))

            pi = 0
            for h in range(H):
                for p, wsb, csoff in (("k", w_k, D), ("v", w_v, 2 * D),
                                      ("q", w_q, 0)):
                    fmdst = fm[p]
                    pb, bkn = P4[pi % 4]
                    cb = convb[pi % 2]
                    cbn = f"convb{pi % 2}"
                    pi += 1
                    for a in range(8):
                        B.op("tensor", lambda e, a=a, h=h, wsb=wsb, pb=pb:
                             e.matmul(pb[:, 0:ST],
                                      wsb[:, a * 1024 + h * 128:a * 1024 + h * 128 + 128],
                                      xbs[:, a * ST:(a + 1) * ST],
                                      start=(a == 0), stop=False),
                             reads=(f"xb{par}_{a}", wsb.name), banks=(bkn,),
                             grp=("qkv", s, p, h))
                    B.op("tensor", lambda e, h=h, csoff=csoff, pb=pb: e.matmul(
                        pb[:, 0:ST],
                        colsum[0:1, csoff + h * 128:csoff + h * 128 + 128],
                        rbf_mu[0:1, :], start=False, stop=True),
                        reads=("colsum", "rbf_mu"), banks=(bkn,),
                        grp=("qkv", s, p, h))
                    ti = "qkv".index(p) * 8 + h
                    if s > 0:
                        B.op("gpsimd", lambda e, cb=cb, ti=ti: e.tensor_copy(
                            bass.AP(cb, 0, [[524, 128], [131, 4], [1, 3]]),
                            bass.AP(tails, ti * 12, [[288, 128], [3, 4], [1, 3]])),
                            reads=(f"tails{ti}",), writes=(cbn,), sem="G", inc=1)
                    else:
                        B.op("gpsimd", lambda e, cb=cb: e.memset(
                            bass.AP(cb, 0, [[524, 128], [131, 4], [1, 3]]), 0),
                            writes=(cbn,), sem="G", inc=1)
                    B.op("vector", lambda e, cb=cb, pb=pb: e.tensor_tensor(
                        bass.AP(cb, 3, [[524, 128], [131, 4], [1, 128]]),
                        pb[:, 0:ST], bc_f[:, :], ALU.mult),
                        reads=("bc_f", cbn), banks=(bkn,), writes=(cbn,))
                    B.op("vector", lambda e, cb=cb, p=p, h=h: e.tensor_single_scalar(
                        acc0[:, :], bass.AP(cb, 0, [[524, 128], [131, 4], [1, 128]]),
                        cw[p][:, h * 4:h * 4 + 1], ALU.mult),
                        reads=(cbn, f"cw{p}"), writes=("acc0",)),
                    B.op("vector", lambda e, cb=cb, p=p, h=h: e.scalar_tensor_tensor(
                        acc1[:, :], bass.AP(cb, 1, [[524, 128], [131, 4], [1, 128]]),
                        cw[p][:, h * 4 + 1:h * 4 + 2], acc0[:, :], ALU.mult, ALU.add),
                        reads=(cbn, "acc0", f"cw{p}"), writes=("acc1",))
                    B.op("vector", lambda e, cb=cb, p=p, h=h: e.scalar_tensor_tensor(
                        acc0[:, :], bass.AP(cb, 2, [[524, 128], [131, 4], [1, 128]]),
                        cw[p][:, h * 4 + 2:h * 4 + 3], acc1[:, :], ALU.mult, ALU.add),
                        reads=(cbn, "acc1", f"cw{p}"), writes=("acc0",))
                    B.op("vector", lambda e, cb=cb, p=p, h=h: e.scalar_tensor_tensor(
                        acc1[:, :], bass.AP(cb, 3, [[524, 128], [131, 4], [1, 128]]),
                        cw[p][:, h * 4 + 3:h * 4 + 4], acc0[:, :], ALU.mult, ALU.add),
                        reads=(cbn, "acc0", f"cw{p}"), writes=("acc1",))
                    B.op("gpsimd", lambda e, cb=cb, ti=ti: e.tensor_copy(
                        bass.AP(tails, ti * 12, [[288, 128], [3, 4], [1, 3]]),
                        bass.AP(cb, 128, [[524, 128], [131, 4], [1, 3]])),
                        reads=(cbn,), writes=(f"tails{ti}",), sem="G", inc=1)
                    outap = bass.AP(fmdst, h * ST,
                                    [[8 * ST, 128], [32, 4], [128, 4], [1, 32]])
                    B.op("scalar", lambda e, outap=outap: e.activation(
                        outap, acc1[:, :], AF.Silu),
                        reads=("acc1",), writes=(f"fm{p}_{h}",))

            if dbg:
                for p, dd in (("q", dq_d), ("k", dk_d), ("v", dv_d)):
                    for h in range(H):
                        dma("scalar",
                            bass.AP(dd.tensor, h * 128 * T + s * ST,
                                    [[T, 128], [1, ST]]),
                            fm[p][:, h * ST:(h + 1) * ST],
                            reads=(f"fm{p}_{h}",), sem="qod")
                dma("scalar", bass.AP(dbg_bg_d.tensor, 2 * ST * s,
                    [[2 * T, 8], [1, 2 * ST]]), bgfm[:, :],
                    reads=("bgfm0", "bgfm1"), sem="qod")

            for c4 in range(4):
                cpar = c4 % 2
                pbB, bknB = PBb[cpar]
                B.op("tensor", lambda e, c4=c4, pbB=pbB: e.transpose(
                    pbB[:, 768:776], bgfm[:, c4 * 128:(c4 + 1) * 128],
                    ident[0:8, 0:8]),
                    reads=("bgfm0", "ident"), banks=(bknB,))
                B.op("tensor", lambda e, c4=c4, pbB=pbB: e.transpose(
                    pbB[:, 776:784], bgfm[:, ST + c4 * 128:ST + (c4 + 1) * 128],
                    ident[0:8, 0:8]),
                    reads=("bgfm1", "ident"), banks=(bknB,))
                B.op("vector", lambda e, c4=c4, pbB=pbB: e.tensor_copy(
                    bgtm[:, c4 * 16:(c4 + 1) * 16], pbB[:, 768:784]),
                    banks=(bknB,), writes=(f"bgtm{c4}",))

                FB4 = [(P_fA, "bk_fA"), (P_fB, "bk_fB"),
                       (P_sA, "bk_sA"), (P_sB, "bk_sB")]
                for grp in range(2):
                    pks = []
                    for idx in range(4):
                        q = grp * 4 + idx
                        pf, bknF = FB4[idx]
                        pb, bknB2 = PBb[idx % 2]
                        pks.append(dict(
                            q=q, i=idx, pf=pf, bkF=bknF, pb=pb, bkB=bknB2,
                            bo=384 * (idx // 2),
                            kfn=f"fmk_{q}", qfn=f"fmq_{q}", vfn=f"fmv_{q}",
                            co=q * ST + c4 * 128, sc=q * 12,
                            Sq=S_bf[:, q * 512:(q + 1) * 512], Sqn=f"S_{q}"))

                    for pk in pks:
                        for col, pnm in ((0, "k"), (128, "v"), (256, "q")):
                            B.op("tensor", lambda e, pk=pk, col=col, pnm=pnm:
                                 e.transpose(
                                     pk["pb"][:, pk["bo"] + col:pk["bo"] + col + 128],
                                     fm[pnm][:, pk["co"]:pk["co"] + 128],
                                     ident[:, :]),
                                 reads=(f"fm{pnm}_{pk['q']}", "ident"),
                                 banks=(pk["bkB"],))
                    for pk in pks:
                        for i in range(4):
                            B.op("tensor", lambda e, pk=pk, i=i: e.matmul(
                                pk["pf"][32 * i:32 * i + 32, 32 * i:32 * i + 32],
                                fm["k"][:, pk["co"] + 32 * i:pk["co"] + 32 * i + 32],
                                fm["k"][:, pk["co"] + 32 * i:pk["co"] + 32 * i + 32],
                                start=True, stop=True, tile_position=(0, 32 * i)),
                                reads=(pk["kfn"],), banks=(pk["bkF"],))
                        for i in range(4):
                            B.op("tensor", lambda e, pk=pk, i=i: e.matmul(
                                pk["pf"][32 * i:32 * i + 32, 128:160],
                                fm["k"][:, pk["co"] + 32 * i:pk["co"] + 32 * i + 32],
                                fm["q"][:, pk["co"] + 32 * i:pk["co"] + 32 * i + 32],
                                start=True, stop=True, tile_position=(0, 32 * i)),
                                reads=(pk["kfn"], pk["qfn"]), banks=(pk["bkF"],))
                    for pk in pks:
                        i_, sc = pk["i"], pk["sc"]
                        B.op("scalar", lambda e, pk=pk: e.activation(
                            scr_s[0][:, :], pk["pb"][:, pk["bo"]:pk["bo"] + 128],
                            AF.Square, accum_out=scaq[:, pk["sc"]:pk["sc"] + 1]),
                            banks=(pk["bkB"],),
                            writes=(f"ssk{i_}", "scr0"))
                        B.op("scalar", lambda e, pk=pk: e.activation(
                            scr_s[0][:, :], pk["pb"][:, pk["bo"] + 256:pk["bo"] + 384],
                            AF.Square, accum_out=scaq[:, pk["sc"] + 1:pk["sc"] + 2]),
                            banks=(pk["bkB"],),
                            writes=(f"ssq{i_}", "scr0"))
                        B.op("vector", lambda e, pk=pk: e.tensor_copy(
                            sl["ktm"][pk["i"]][:, :], pk["pb"][:, pk["bo"]:pk["bo"] + 128]),
                            banks=(pk["bkB"],), writes=(f"ktm{i_}",))
                        B.op("vector", lambda e, pk=pk: e.tensor_copy(
                            sl["vtm"][pk["i"]][:, :], pk["pb"][:, pk["bo"] + 128:pk["bo"] + 256]),
                            banks=(pk["bkB"],), writes=(f"vtm{i_}",))
                        B.op("scalar", lambda e, sc=sc: e.activation(
                            scaq[:, sc + 2:sc + 3], scaq[:, sc:sc + 1],
                            AF.Sqrt, bias=1e-12),
                            reads=(f"ssk{i_}",), writes=(f"sqk{i_}",))
                        B.op("scalar", lambda e, sc=sc: e.activation(
                            scaq[:, sc + 3:sc + 4], scaq[:, sc + 1:sc + 2],
                            AF.Sqrt, bias=1e-12),
                            reads=(f"ssq{i_}",), writes=(f"sqq{i_}",))
                        B.op("vector", lambda e, sc=sc: e.reciprocal(
                            scaq[:, sc + 4:sc + 5], scaq[:, sc + 2:sc + 3]),
                            reads=(f"sqk{i_}",), writes=(f"rsk{i_}",))
                        B.op("vector", lambda e, sc=sc: e.reciprocal(
                            scaq[:, sc + 5:sc + 6], scaq[:, sc + 3:sc + 4]),
                            reads=(f"sqq{i_}",), writes=(f"rsq{i_}",))
                        bcol = bgtm[:, c4 * 16 + pk["q"]:c4 * 16 + pk["q"] + 1]
                        gcol = bgtm[:, c4 * 16 + 8 + pk["q"]:c4 * 16 + 8 + pk["q"] + 1]
                        B.op("vector", lambda e, sc=sc: e.tensor_mul(
                            scaq[:, sc + 6:sc + 7], scaq[:, sc + 4:sc + 5],
                            scaq[:, sc + 4:sc + 5]),
                            reads=(f"rsk{i_}",), writes=(f"rs2{i_}",))
                        B.op("vector", lambda e, sc=sc, bcol=bcol:
                             e.scalar_tensor_tensor(
                                 scaq[:, sc + 7:sc + 8], scaq[:, sc + 6:sc + 7],
                                 -1.0, bcol, ALU.mult, ALU.mult),
                             reads=(f"rs2{i_}", f"bgtm{c4}"), writes=(f"nrs2b{i_}",))
                        B.op("vector", lambda e, sc=sc, bcol=bcol: e.tensor_mul(
                            scaq[:, sc + 8:sc + 9], scaq[:, sc + 4:sc + 5], bcol),
                            reads=(f"rsk{i_}", f"bgtm{c4}"), writes=(f"rsb{i_}",))
                        B.op("vector", lambda e, sc=sc, gcol=gcol: e.tensor_mul(
                            scaq[:, sc + 9:sc + 10], gcol, scaq[:, sc + 5:sc + 6]),
                            reads=(f"rsq{i_}", f"bgtm{c4}"), writes=(f"grsq{i_}",))
                        B.op("vector", lambda e, sc=sc, gcol=gcol: e.tensor_scalar(
                            scaq[:, sc + 10:sc + 11], gcol, -1.0, 1.0,
                            ALU.mult, ALU.add),
                            reads=(f"bgtm{c4}",), writes=(f"omg{i_}",))
                        B.op("vector", lambda e, pk=pk, sc=sc:
                             e.scalar_tensor_tensor(
                                 sl["Abd"][pk["i"]][:, :], pk["pf"][:, 0:128],
                                 scaq[:, sc + 7:sc + 8], maskbd[:, :],
                                 ALU.mult, ALU.mult),
                             reads=(f"nrs2b{i_}", "maskbd"), banks=(pk["bkF"],),
                             writes=(f"Abd{i_}",))
                        B.op("vector", lambda e, pk=pk: e.tensor_mul(
                            sl["Mt"][pk["i"]][:, :], pk["pf"][:, 128:160],
                            masku[:, :]),
                            reads=("masku",), banks=(pk["bkF"],),
                            writes=(f"Mt{i_}",))
                        B.op("vector", lambda e, pk=pk, sc=sc:
                             e.tensor_single_scalar(
                                 sl["Xu"][pk["i"]][:, :], pk["pb"][:, pk["bo"] + 128:pk["bo"] + 256],
                                 scaq[:, sc + 8:sc + 9], ALU.mult),
                             reads=(f"rsb{i_}",), banks=(pk["bkB"],),
                             writes=(f"Xu{i_}",))
                    for pk in pks:
                        i_ = pk["i"]
                        ppR, bknR = PPR[pk["i"] % 2]
                        rco = (pk["i"] // 2) * 128
                        for i in range(4):
                            B.op("tensor", lambda e, pk=pk, i=i, ppR=ppR,
                                 rco=rco: e.matmul(
                                ppR[32 * i:32 * i + 32, rco:rco + 128],
                                fm["k"][:, pk["co"] + 32 * i:pk["co"] + 32 * i + 32],
                                pk["Sq"][:, 128 * i:128 * (i + 1)],
                                start=True, stop=True, tile_position=(0, 32 * i)),
                                reads=(pk["kfn"], pk["Sqn"]), banks=(bknR,))
                    for pk in pks:
                        i_ = pk["i"]
                        B.op("tensor", lambda e, pk=pk: e.transpose(
                            pk["pb"][:, pk["bo"]:pk["bo"] + 128], sl["Abd"][pk["i"]][:, :],
                            ident[:, :]),
                            reads=(f"Abd{i_}", "ident"), banks=(pk["bkB"],))
                        B.op("scalar", lambda e, pk=pk: e.copy(
                            sl["C0"][pk["i"]][:, :], pk["pb"][:, pk["bo"]:pk["bo"] + 128]),
                            banks=(pk["bkB"],), writes=(f"C0{i_}",))
                    for lvl in ((("D1", "C0", "Abd"), ("C1", "Abd", "C0")),
                                (("D2", "C1", "D1"), ("C2", "D1", "C1")),
                                (("D3", "C2", "D2"), ("C3", "D2", "C2")),
                                (("C4", "D3", "C3"),)):
                        for pk in pks:
                            i_ = pk["i"]
                            for li, (dstn, ltn, rtn) in enumerate(lvl):
                                co = 160 + 128 * li
                                B.op("tensor", lambda e, pk=pk, ltn=ltn,
                                     rtn=rtn, co=co:
                                     e.matmul(pk["pf"][:, co:co + 128],
                                              sl[ltn][pk["i"]][:, :],
                                              sl[rtn][pk["i"]][:, :],
                                              start=True, stop=True),
                                     reads=(f"{ltn}{i_}", f"{rtn}{i_}"),
                                     banks=(pk["bkF"],))
                        for pk in pks:
                            i_ = pk["i"]
                            for li, (dstn, ltn, rtn) in enumerate(lvl):
                                co = 160 + 128 * li
                                if li == 0:
                                    B.op("vector", lambda e, pk=pk, dstn=dstn,
                                         co=co: e.tensor_copy(
                                        sl[dstn][pk["i"]][:, :],
                                        pk["pf"][:, co:co + 128]),
                                        banks=(pk["bkF"],),
                                        writes=(f"{dstn}{i_}",))
                                else:
                                    B.op("scalar", lambda e, pk=pk, dstn=dstn,
                                         co=co: e.copy(
                                        sl[dstn][pk["i"]][:, :],
                                        pk["pf"][:, co:co + 128]),
                                        banks=(pk["bkF"],),
                                        writes=(f"{dstn}{i_}",))
                    for pk in pks:
                        i_ = pk["i"]
                        ppR, bknR = PPR[pk["i"] % 2]
                        rco = (pk["i"] // 2) * 128
                        B.op("vector", lambda e, pk=pk, ppR=ppR, rco=rco:
                             e.scalar_tensor_tensor(
                            sl["Ya"][pk["i"]][:, :], ppR[:, rco:rco + 128],
                            scaq[:, pk["sc"] + 7:pk["sc"] + 8],
                            sl["Xu"][pk["i"]][:, :], ALU.mult, ALU.add),
                            reads=(f"nrs2b{i_}", f"Xu{i_}"), banks=(bknR,),
                            writes=(f"Ya{i_}",))
                    for ai, cn in enumerate(("C4", "C3", "C2", "C1")):
                        yi, yo = ("Ya", "Yb") if ai % 2 == 0 else ("Yb", "Ya")
                        for pk in pks:
                            i_ = pk["i"]
                            B.op("tensor", lambda e, pk=pk, cn=cn, yi=yi:
                                 e.matmul(pk["pf"][:, 288:416],
                                          sl[cn][pk["i"]][:, :],
                                          sl[yi][pk["i"]][:, :],
                                          start=True, stop=False),
                                 reads=(f"{cn}{i_}", f"{yi}{i_}"),
                                 banks=(pk["bkF"],))
                            B.op("tensor", lambda e, pk=pk, yi=yi:
                                 e.matmul(pk["pf"][:, 288:416],
                                          ident[:, :],
                                          sl[yi][pk["i"]][:, :],
                                          start=False, stop=True),
                                 reads=(f"{yi}{i_}", "ident"),
                                 banks=(pk["bkF"],))
                            B.op("scalar", lambda e, pk=pk, yo=yo: e.copy(
                                sl[yo][pk["i"]][:, :], pk["pf"][:, 288:416]),
                                banks=(pk["bkF"],), writes=(f"{yo}{i_}",))
                    for pk in pks:
                        i_ = pk["i"]
                        B.op("tensor", lambda e, pk=pk: e.matmul(
                            pk["pf"][:, 288:416], sl["C0"][pk["i"]][:, :],
                            sl["Ya"][pk["i"]][:, :], start=True, stop=False),
                            reads=(f"C0{i_}", f"Ya{i_}"), banks=(pk["bkF"],))
                        B.op("tensor", lambda e, pk=pk: e.matmul(
                            pk["pf"][:, 288:416], ident[:, :],
                            sl["Ya"][pk["i"]][:, :], start=False, stop=True),
                            reads=(f"Ya{i_}", "ident"), banks=(pk["bkF"],))
                        for i in range(4):
                            B.op("scalar", lambda e, pk=pk, i=i: e.copy(
                                ubd[pk["i"]][32 * i:32 * i + 32,
                                             128 * i:128 * (i + 1)],
                                pk["pf"][32 * i:32 * i + 32, 288:416]),
                                banks=(pk["bkF"],), writes=(f"ubd{i_}",))
                    if dbg and s == 0:
                        for pk in pks:
                            dma("scalar", bass.AP(
                                dbg_u_d.tensor,
                                (c4 * 8 + pk["q"]) * 512, [[16384, 128], [1, 512]]),
                                ubd[pk["i"]][:, :],
                                reads=(f"ubd{pk['i']}",), sem="qod")
                    for pk in pks:
                        i_ = pk["i"]
                        ppR, bknR = PPR[pk["i"] % 2]
                        oco = 256 + (pk["i"] // 2) * 128
                        for i in range(4):
                            B.op("tensor", lambda e, pk=pk, i=i, ppR=ppR,
                                 oco=oco: e.matmul(
                                ppR[32 * i:32 * i + 32, oco:oco + 128],
                                fm["q"][:, pk["co"] + 32 * i:pk["co"] + 32 * i + 32],
                                pk["Sq"][:, 128 * i:128 * (i + 1)],
                                start=True, stop=True, tile_position=(0, 32 * i)),
                                reads=(pk["qfn"], pk["Sqn"]), banks=(bknR,))
                        for i in range(4):
                            B.op("tensor", lambda e, pk=pk, i=i, ppR=ppR,
                                 oco=oco: e.matmul(
                                ppR[32 * i:32 * i + 32, oco:oco + 128],
                                sl["Mt"][pk["i"]][32 * i:32 * i + 32, :],
                                ubd[pk["i"]][32 * i:32 * i + 32,
                                             128 * i:128 * (i + 1)],
                                start=False, stop=True,
                                tile_position=(32 * i, 32 * i),
                                skip_group_check=True),
                                reads=(f"Mt{i_}", f"ubd{i_}"), banks=(bknR,))
                    for pk in pks:
                        i_, sc = pk["i"], pk["sc"]
                        B.op("vector", lambda e, pk=pk, sc=sc:
                             e.tensor_single_scalar(
                                 sl["ofin"][pk["i"]][:, :],
                                 sl["vtm"][pk["i"]][:, :],
                                 scaq[:, sc + 10:sc + 11], ALU.mult),
                             reads=(f"vtm{i_}", f"omg{i_}"), writes=(f"vcon{i_}",))
                        ppR, bknR = PPR[pk["i"] % 2]
                        oco = 256 + (pk["i"] // 2) * 128
                        B.op("vector", lambda e, pk=pk, sc=sc, ppR=ppR,
                             oco=oco:
                             e.scalar_tensor_tensor(
                                 sl["oms"][pk["i"]][:, :], ppR[:, oco:oco + 128],
                                 scaq[:, sc + 9:sc + 10],
                                 sl["ofin"][pk["i"]][:, :], ALU.mult, ALU.add),
                             reads=(f"grsq{i_}", f"vcon{i_}"), banks=(bknR,),
                             writes=(f"oms{i_}",))
                        B.op("scalar", lambda e, pk=pk, sc=sc: e.activation(
                            scr_s[0][:, :], sl["oms"][pk["i"]][:, :],
                            AF.Square, accum_out=scaq[:, sc + 11:sc + 12]),
                            reads=(f"oms{i_}",),
                            writes=(f"sso{i_}", "scr0"))
                        B.op("scalar", lambda e, sc=sc: e.activation(
                            scaq[:, sc + 11:sc + 12], scaq[:, sc + 11:sc + 12],
                            AF.Sqrt, bias=1e-5, scale=1.0 / DK),
                            reads=(f"sso{i_}",), writes=(f"sso{i_}",))
                        B.op("vector", lambda e, sc=sc: e.reciprocal(
                            scaq[:, sc + 11:sc + 12], scaq[:, sc + 11:sc + 12]),
                            reads=(f"sso{i_}",), writes=(f"sso{i_}",))
                        B.op("vector", lambda e, pk=pk, sc=sc:
                             e.tensor_single_scalar(
                                 sl["ofin"][pk["i"]][:, :],
                                 sl["oms"][pk["i"]][:, :],
                                 scaq[:, sc + 11:sc + 12], ALU.mult),
                             reads=(f"oms{i_}", f"sso{i_}"), writes=(f"ofin{i_}",))
                    for pk in pks:
                        i_ = pk["i"]
                        pp, bknP = PPR[pk["i"] % 2]
                        B.op("tensor", lambda e, pk=pk, pp=pp: e.matmul(
                            pp[:, :], sl["ktm"][pk["i"]][:, :],
                            ubd[pk["i"]][:, :], start=True, stop=True),
                            reads=(f"ktm{i_}", f"ubd{i_}"), banks=(bknP,))
                        B.op("vector", lambda e, pk=pk, pp=pp: e.tensor_add(
                            pk["Sq"][:, :], pp[:, :], pk["Sq"][:, :]),
                            reads=(pk["Sqn"],), banks=(bknP,),
                            writes=(pk["Sqn"],))
                    for pk in pks:
                        i_ = pk["i"]
                        B.op("tensor", lambda e, pk=pk: e.transpose(
                            pk["pb"][:, pk["bo"] + 128:pk["bo"] + 256], sl["ofin"][pk["i"]][:, :],
                            ident[:, :]),
                            reads=(f"ofin{i_}", "ident"), banks=(pk["bkB"],))
                        B.op("scalar", lambda e, pk=pk, c4=c4: e.copy(
                            ofm[:, pk["q"] * ST + c4 * 128:pk["q"] * ST + c4 * 128 + 128],
                            pk["pb"][:, pk["bo"] + 128:pk["bo"] + 256]),
                            banks=(pk["bkB"],), writes=(f"ofm{pk['q']}_{c4}",))

            load_xfc(s, 0)
            for oc in range(8):
                if oc + 1 < 8:
                    load_xfc(s, oc + 1)
                pb, bkn = PPR[oc % 2]
                for a in range(8):
                    B.op("tensor", lambda e, a=a, oc=oc, pb=pb: e.matmul(
                        pb[:, 0:ST],
                        o_w[:, a * 1024 + oc * 128:a * 1024 + oc * 128 + 128],
                        ofm[:, a * ST:(a + 1) * ST], start=(a == 0), stop=(a == 7)),
                        reads=tuple(f"ofm{a}_{cc}" for cc in range(4)) + ("o_ws",),
                        banks=(bkn,), grp=("o", s, oc))
                asb = attn_sb[oc % 2]
                xin = bass.AP(xfc[oc % 2], 0,
                              [[ST, 128], [32, 4], [128, 4], [1, 32]])
                B.op("vector", lambda e, pb=pb, asb=asb, xin=xin: e.tensor_add(
                    asb[:, :], pb[:, 0:ST], xin),
                    reads=(f"xfc{oc % 2}", f"asb{oc % 2}"), banks=(bkn,),
                    writes=(f"asb{oc % 2}",))
                dma("scalar",
                    bass.AP(attn_d.tensor, oc * 128 * T + s * ST,
                            [[T, 128], [1, ST]]),
                    asb[:, :], reads=(f"asb{oc % 2}",), sem=f"qoa{oc % 2}")
            ntile = 64 + 64
            per = (ntile + nst - 1) // nst
            for k in range(per * s, min(per * (s + 1), ntile)):
                if k < 64:
                    rr, cc = k // 8, k % 8
                    src = bass.AP(wfcT_d.tensor, rr * 128 * HFF + cc * 512,
                                  [[HFF, 128], [1, 512]])
                    gi = 0
                else:
                    rr, cc = (k - 64) // 2, (k - 64) % 2
                    src = bass.AP(wprT_d.tensor, rr * 128 * D + cc * 512,
                                  [[D, 128], [1, 512]])
                    gi = 1
                dma("sync", gstage[0][:, :], src, "gstage0", sem="qgs")
                B.op("vector", lambda e: e.tensor_reduce(
                    gred[:, :], bass.AP(gstage[0], 0, [[512, 128], [16, 32], [1, 16]]),
                    mybir.AxisListType.X, ALU.add, apply_absolute_value=True),
                    reads=("gstage0",), writes=("gred",))
                B.op("vector", lambda e: e.tensor_reduce(
                    gr4[:, :], bass.AP(gred, 0, [[32, 128], [8, 4], [1, 8]]),
                    mybir.AxisListType.X, ALU.add),
                    reads=("gred",), writes=("gr4",))
                ac = gpart[:, 8 * gi:8 * gi + 4]
                cp = gpart[:, 8 * gi + 4:8 * gi + 8]
                if k in (0, 64):
                    B.op("vector", lambda e, ac=ac: e.tensor_copy(ac, gr4[:, :]),
                        reads=("gr4",), writes=(f"gpart{gi}",))
                    B.op("vector", lambda e, cp=cp: e.memset(cp, 0),
                        writes=(f"gcomp{gi}",))
                else:
                    B.op("vector", lambda e, ac=ac: e.tensor_add(
                        ktmp[:, :], ac, gr4[:, :]),
                        reads=("gr4", f"gpart{gi}"), writes=("ktmp",))
                    B.op("vector", lambda e, ac=ac: e.tensor_sub(
                        ktmp2[:, :], ac, ktmp[:, :]),
                        reads=("ktmp", f"gpart{gi}"), writes=("ktmp2",))
                    B.op("vector", lambda e: e.tensor_add(
                        ktmp2[:, :], ktmp2[:, :], gr4[:, :]),
                        reads=("ktmp2", "gr4"), writes=("ktmp2",))
                    B.op("vector", lambda e, cp=cp: e.tensor_add(
                        cp, cp, ktmp2[:, :]),
                        reads=("ktmp2", f"gcomp{gi}"), writes=(f"gcomp{gi}",))
                    B.op("vector", lambda e, ac=ac: e.tensor_copy(ac, ktmp[:, :]),
                        reads=("ktmp",), writes=(f"gpart{gi}",))

            if dbg:
                for a in range(8):
                    dma("scalar", bass.AP(dofm_d.tensor, a * 128 * T + s * ST,
                        [[T, 128], [1, ST]]), ofm[:, a * ST:(a + 1) * ST],
                        reads=tuple(f"ofm{a}_{cc}" for cc in range(4)), sem="qod")

        B.op("vector", lambda e: e.memset(P_fA[:, 0:160], 0), banks=("bk_fA",))
        B.op("vector", lambda e: e.memset(P_fB[:, 0:160], 0), banks=("bk_fB",))
        B.op("vector", lambda e: e.memset(P_sA[:, 0:160], 0), banks=("bk_sA",))
        B.op("vector", lambda e: e.memset(P_sB[:, 0:160], 0), banks=("bk_sB",))
        B.barrier()
        for s in range(nst):
            stripe_phase1(s)

        for gi in range(2):
            B.op("vector", lambda e, gi=gi: e.tensor_add(
                ktmp[:, :], gpart[:, 8 * gi:8 * gi + 4],
                gpart[:, 8 * gi + 4:8 * gi + 8]),
                reads=(f"gpart{gi}", f"gcomp{gi}"), writes=("ktmp",))
            B.op("vector", lambda e: e.tensor_reduce(
                gr4[:, 0:1], ktmp[:, :], mybir.AxisListType.X, ALU.add),
                reads=("ktmp",), writes=("gr4",))
            B.op("tensor", lambda e: e.transpose(
                P_sA[0:1, 0:128], gr4[:, 0:1], ident_f[:, :]),
                reads=("gr4", "ident_f"), banks=("bk_sA",))
            B.op("vector", lambda e: e.tensor_copy(grow[0:1, :], P_sA[0:1, 0:128]),
                 banks=("bk_sA",), writes=("grow",))
            for half in (64, 32, 16, 8, 4, 2, 1):
                B.op("vector", lambda e, half=half: e.tensor_add(
                    grow[0:1, 0:half], grow[0:1, 0:half],
                    grow[0:1, half:2 * half]),
                    reads=("grow",), writes=("grow",))
            B.op("vector", lambda e, gi=gi: e.tensor_scalar(
                gam[0:1, 4 * gi:4 * gi + 1], grow[0:1, 0:1],
                0.5 / float(HFF * D), 0.5e-6, ALU.mult, ALU.add),
                reads=("grow",), writes=(f"gam{gi}",))
            B.op("tensor", lambda e, gi=gi: e.matmul(
                P_sA[:, 0:1], ones_rf[0:1, 0:128],
                gam[0:1, 4 * gi:4 * gi + 1], start=True, stop=True),
                reads=(f"gam{gi}", "ones_rf"), banks=("bk_sA",))
            B.op("vector", lambda e, gi=gi: e.tensor_single_scalar(
                gcolp[:, gi:gi + 1], P_sA[:, 0:1], 1.0, ALU.mult),
                banks=("bk_sA",), writes=(f"gcolp{gi}",))
            B.op("vector", lambda e, gi=gi: e.tensor_single_scalar(
                gcoln[:, gi:gi + 1], P_sA[:, 0:1], -1.0, ALU.mult),
                banks=("bk_sA",), writes=(f"gcoln{gi}",))

        B.barrier()
        for k in range(40):
            qp = k % 2
            if k < 8:
                src = wfcT_d[128 * k:128 * (k + 1), :]
                dst = Wfc[:, k * HFF:(k + 1) * HFF]
                gi, ww, dn = 0, HFF, f"Wfc{k}"
            else:
                kk = k - 8
                src = wprT_d[128 * kk:128 * (kk + 1), :]
                dst = Wpr[:, kk * D:(kk + 1) * D]
                gi, ww, dn = 1, D, f"Wpr{kk}"
            dma("sync", qstage[qp][:, 0:ww], src, f"qstage{qp}", sem=f"qst{qp}")
            B.op("scalar", lambda e, qp=qp, gi=gi, ww=ww: e.activation(
                sgn[:, 0:ww], qstage[qp][:, 0:ww], AF.Sign,
                bias=gcoln[:, gi:gi + 1]),
                reads=(f"qstage{qp}", f"gcoln{gi}", "sgn_r"), writes=("sgn",))
            B.op("scalar", lambda e, qp=qp, gi=gi, ww=ww, dst=dst: e.activation(
                dst, qstage[qp][:, 0:ww], AF.Sign, bias=gcolp[:, gi:gi + 1]),
                reads=(f"qstage{qp}", f"gcolp{gi}"), writes=(dn,))
            B.op("vector", lambda e, ww=ww, dst=dst: e.tensor_add(
                dst, dst, sgn[:, 0:ww]),
                reads=("sgn", dn), writes=(dn, "sgn_r"))
            B.op("vector", lambda e, dst=dst: e.tensor_single_scalar(
                dst, dst, 0.5, ALU.mult),
                reads=(dn,), writes=(dn,))

        for half in range(2):
            n0 = half * 512
            pbh, bknH = PSP[half]
            for hc in range(32):
                lh = bass.AP(lnwb, hc, [[64, 128], [32, 2]])
                B.op("tensor", lambda e, hc=hc, n0=n0, pbh=pbh, lh=lh: e.matmul(
                    pbh[0:2, 0:512], lh, Wpr[:, hc * D + n0:hc * D + n0 + 512],
                    start=(hc == 0), stop=(hc == 31)),
                    reads=(f"Wpr{hc}", "lnw", "lnb"), banks=(bknH,))
            B.op("vector", lambda e, n0=n0, pbh=pbh: e.tensor_copy(
                c12[:, n0:n0 + 512], pbh[0:2, 0:512]),
                banks=(bknH,), writes=("c12",))
        for hc in range(32):
            B.op("vector", lambda e, hc=hc: e.tensor_scalar(
                Wpr[:, hc * D:(hc + 1) * D], Wpr[:, hc * D:(hc + 1) * D],
                lnw_f[:, hc:hc + 1], None, ALU.mult),
                reads=(f"Wpr{hc}", "lnwf"), writes=(f"Wpr{hc}",))

        B.barrier()
        for jc in range(8):
            for a in range(8):
                B.op("tensor", lambda e, jc=jc, a=a: e.matmul(
                    P_sA[0:1, 0:512], ones_c[:, :],
                    Wfc[:, a * HFF + jc * 512:a * HFF + jc * 512 + 512],
                    start=(a == 0), stop=(a == 7)),
                    reads=(f"Wfc{a}", "ones_c"), banks=("bk_sA",))
            B.op("vector", lambda e, jc=jc: e.tensor_copy(
                csrow[0:1, jc * 512:(jc + 1) * 512], P_sA[0:1, 0:512]),
                banks=("bk_sA",), writes=("csrow",))

        def load_att(s):
            for a in range(8):
                src = bass.AP(attn_d.tensor, a * 128 * T + s * ST,
                              [[T, 128], [1, ST]])
                dma("gpsimd", attb[:, a * ST:(a + 1) * ST], src, f"attb_{a}",
                    sem="qat2")

        def load_attf(s, a):
            src = bass.AP(attn_d.tensor, a * 128 * T + s * ST,
                          [[T, 128], [1, ST]])
            dma("sync", attf[a % 2][:, :], src, f"attf{a % 2}", sem="qat")

        load_att(0)
        for s in range(nst):
            ln_stats_rows(lambda a: attb[:, a * ST:(a + 1) * ST], "attb",
                          lambda a: attb[:, a * ST:(a + 1) * ST], x2s_p2,
                          "x2p", D)
            bc_row(rbf_rs[0:1, :], "bc_f", bc_f, PSP[1])
            def fc_group(j):
                pb, bkn = P4[j % 4]
                for a in range(8):
                    B.op("tensor", lambda e, a=a, j=j, pb=pb: e.matmul(
                        pb[:, 0:ST],
                        Wfc[:, a * HFF + j * 128:a * HFF + j * 128 + 128],
                        attb[:, a * ST:(a + 1) * ST],
                        start=(a == 0), stop=False),
                        reads=(f"attb_{a}", f"Wfc{a}"), banks=(bkn,),
                        grp=("fc", s, j))
                B.op("tensor", lambda e, j=j, pb=pb: e.matmul(
                    pb[:, 0:ST], csrow[0:1, j * 128:(j + 1) * 128],
                    rbf_mu[0:1, :], start=False, stop=True),
                    reads=("csrow", "rbf_mu"), banks=(bkn,),
                    grp=("fc", s, j))

            def fc_epi(j):
                pb, bkn = P4[j % 4]
                B.op("vector", lambda e, j=j, pb=pb: e.tensor_mul(
                    ysb[:, :], pb[:, 0:ST], bc_f[:, :]),
                    banks=(bkn,), reads=("bc_f",), writes=("ysb",))
                B.op("scalar", lambda e, j=j: e.activation(
                    h_sb[:, j * ST:(j + 1) * ST], ysb[:, :], AF.Gelu),
                    reads=("ysb",), writes=(f"h_{j}",))
                B.op("tensor", lambda e, j=j: e.matmul(
                    P_fA[0:1, 0:ST], ones_c[:, :], h_sb[:, j * ST:(j + 1) * ST],
                    start=(j == 0), stop=(j == 31)),
                    reads=(f"h_{j}", "ones_c"), banks=("bk_fA",))
                B.op("scalar", lambda e, j=j: e.activation(
                    x2s_p2[j % 2][:, :], h_sb[:, j * ST:(j + 1) * ST], AF.Square),
                    reads=(f"h_{j}",), writes=(f"x2p{j % 2}",))
                B.op("tensor", lambda e, j=j: e.matmul(
                    P_fB[0:1, 0:ST], ones_c[:, :], x2s_p2[j % 2][:, :],
                    start=(j == 0), stop=(j == 31)),
                    reads=(f"x2p{j % 2}", "ones_c"), banks=("bk_fB",))

            for j in range(32):
                fc_group(j)
                if j > 0:
                    fc_epi(j - 1)
            fc_epi(31)
            B.op("vector", lambda e: e.tensor_single_scalar(
                rows[0:1, 0:ST], P_fA[0:1, 0:ST], -1.0 / HFF, ALU.mult),
                banks=("bk_fA",), writes=("rows0",))
            B.op("vector", lambda e: e.tensor_single_scalar(
                rows[0:1, ST:2 * ST], P_fB[0:1, 0:ST], 1.0 / HFF, ALU.mult),
                banks=("bk_fB",), writes=("rows1",))
            B.op("vector", lambda e: e.tensor_mul(
                bc_f[0:1, 0:ST], rows[0:1, 0:ST], rows[0:1, 0:ST]),
                reads=("rows0",), writes=("bc_f",))
            B.op("vector", lambda e: e.tensor_sub(
                rows[0:1, ST:2 * ST], rows[0:1, ST:2 * ST], bc_f[0:1, 0:ST]),
                reads=("rows1", "bc_f"), writes=("rows1",))
            B.op("scalar", lambda e: e.activation(
                rows[0:1, ST:2 * ST], rows[0:1, ST:2 * ST], AF.Sqrt, bias=1e-5),
                reads=("rows1",), writes=("rows1",))
            dma("gpsimd", bass.AP(crhs, 0, [[ST, 2], [1, ST]]),
                rows[0:1, 0:2 * ST], "crhs0",
                reads=("rows0", "rows1"), sem="qcr")
            B.op("vector", lambda e: e.reciprocal(
                rows[0:1, ST:2 * ST], rows[0:1, ST:2 * ST]),
                reads=("rows1",), writes=("rows1",))
            B.op("vector", lambda e: e.tensor_copy(rbf_rs[0:1, :],
                                                   rows[0:1, ST:2 * ST]),
                 reads=("rows1",), writes=("rbf_rs",))
            bc_row(rbf_rs[0:1, :], "bc_f", bc_f, PSP[1])
            load_attf(s, 0)
            for dcn in range(8):
                if dcn + 1 < 8:
                    load_attf(s, dcn + 1)
                pb, bkn = P4[dcn % 4]
                for j in range(32):
                    B.op("tensor", lambda e, j=j, dcn=dcn, pb=pb: e.matmul(
                        pb[:, 0:ST],
                        Wpr[:, j * D + dcn * 128:j * D + dcn * 128 + 128],
                        h_sb[:, j * ST:(j + 1) * ST],
                        start=(j == 0), stop=False),
                        reads=(f"h_{j}", f"Wpr{j}"), banks=(bkn,),
                        grp=("pr", s, dcn))
                B.op("tensor", lambda e, dcn=dcn, pb=pb: e.matmul(
                    pb[:, 0:ST], c12[:, dcn * 128:(dcn + 1) * 128],
                    crhs[:, :], start=False, stop=True),
                    reads=("c12", "crhs0"), banks=(bkn,),
                    grp=("pr", s, dcn))
                B.op("vector", lambda e, dcn=dcn, pb=pb: e.tensor_mul(
                    ysb[:, :], pb[:, 0:ST], bc_f[:, :]),
                    banks=(bkn,), reads=("bc_f",), writes=("ysb",))
                B.op("vector", lambda e, dcn=dcn: e.tensor_add(
                    ysb[:, :], ysb[:, :], attf[dcn % 2][:, :]),
                    reads=(f"attf{dcn % 2}",), writes=("ysb",))
                dma("scalar",
                    bass.AP(y_d.tensor, dcn * 128 * T + s * ST,
                            [[T, 128], [1, ST]]),
                    ysb[:, :], reads=("ysb",), sem="qoy")
                if s + 1 < nst:
                    src = bass.AP(attn_d.tensor, dcn * 128 * T + (s + 1) * ST,
                                  [[T, 128], [1, ST]])
                    dma("gpsimd", attb[:, dcn * ST:(dcn + 1) * ST], src,
                        f"attb_{dcn}", sem="qat2")
        B.op("gpsimd", lambda e: e.engine_nop(),
             extra_waits=[(q, B.cnt[q]) for q in
                          ("qoa0", "qoa1", "qoy", "qod")], sem="G", inc=1)


    with nc.Block() as block:
        def mk(name):
            def run(eng):
                B.begin_pass(name, eng)
                schedule()
            return run
        block.sync(mk("sync"))
        block.gpsimd(mk("gpsimd"))
        block.tensor(mk("tensor"))
        block.vector(mk("vector"))
        block.scalar(mk("scalar"))

    p2b.close()
    p2w.close()
    stk.close()
    return nc


def _consts():
    ident = np.eye(128).astype(bfloat16)
    mbd = np.zeros((128, 128), np.float32)
    for i in range(4):
        mbd[32 * i:32 * i + 32, 32 * i:32 * i + 32] = np.tril(
            np.ones((32, 32), np.float32), -1)
    masku = np.tile(np.triu(np.ones((32, 32), np.float32), 0), (4, 1))
    return {
        "c_ident": ident,
        "c_maskbd": mbd.astype(bfloat16),
        "c_masku": masku.astype(bfloat16),
        "c_ones": np.ones((128, 1), bfloat16),
        "c_onesf": np.ones((128, 1), np.float32),
        "c_onesr": np.ones((1, ST), bfloat16),
        "c_onesrf": np.ones((1, 128), np.float32),
        "c_identf": np.eye(128, dtype=np.float32),
        "c_eps": np.tile(np.array([1e-5, 1e-12, 1.0 / 128, 0.0],
                                  np.float32), (128, 1)),
    }


def _in_map(ci, inputs, L=2048):
    g = lambda k: np.ascontiguousarray(np.asarray(inputs["L0_attn_" + k],
                                                  np.float32))
    gf = lambda k: np.ascontiguousarray(np.asarray(inputs["L0_ffn_" + k],
                                                   np.float32))
    x = np.asarray(inputs["L0_attn_hidden_states"], np.float32)
    xs = x[NSEQ * ci:NSEQ * (ci + 1), :L, :]
    x_fm = np.ascontiguousarray(
        xs.reshape(NSEQ, L // 128, 128, D).transpose(3, 1, 0, 2)
        .reshape(D, NSEQ * L))
    im = {
        "x": x_fm,
        "w_q": g("q_w"), "w_k": g("k_w"), "w_v": g("v_w"),
        "cw_q": g("conv_q_w"), "cw_k": g("conv_k_w"), "cw_v": g("conv_v_w"),
        "b_w": g("b_w"), "mix_w": g("mix_w"),
        "mix_b": g("mix_b").reshape(H, 1),
        "mix_bias": g("mix_bias").reshape(H, 1),
        "o_norm_w": g("o_norm_w").reshape(DK, 1), "o_w": g("o_w"),
        "w_fcT": np.ascontiguousarray(gf("W_fc").T),
        "ln_w": gf("ln_w"), "ln_b": gf("ln_b"),
        "w_prT": np.ascontiguousarray(gf("W_proj").T),
    }
    im.update(_consts())
    return im


def _decode_y(ycore, L=2048):
    nst = L // 128
    yr = np.asarray(ycore, np.float32).reshape(D, nst, 4, NSEQ, 32)
    return np.ascontiguousarray(
        yr.transpose(3, 1, 2, 4, 0).reshape(NSEQ, L, D))


def kernel(**inputs):
    nc = build_core(L=2048)
    in_maps = [_in_map(ci, inputs) for ci in range(8)]
    res = run_bass_kernel_spmd(nc, in_maps, core_ids=list(range(8)))
    outs = [_decode_y(r["y"]) for r in res.results]
    return np.concatenate(outs, axis=0)


# version 15 (final)
# speedup vs baseline: 1.0244x; 1.0078x over previous
import contextlib
import numpy as np
import ml_dtypes
import concourse.bass as bass
import concourse.mybir as mybir
from concourse.bass_utils import run_bass_kernel_spmd

F32 = mybir.dt.float32
BF16 = mybir.dt.bfloat16
AF = mybir.ActivationFunctionType
ALU = mybir.AluOpType
bfloat16 = ml_dtypes.bfloat16

NSEQ = 4
D = 1024
H = 8
DK = 128
CH = 32
HFF = 4096
ST = 512
ENG2SEM = {"sync": "qc", "gpsimd": "G", "tensor": "T",
           "vector": "V", "scalar": "S"}


DMA_SEMS = {"qc", "qcg", "qxb0", "qxb1", "qxf0", "qxf1", "qgs", "qst0", "qst1", "qat", "qat2", "qcr", "qoa0", "qoa1", "qoy", "qod"}


class Emitter:

    def __init__(self, nc, sems):
        self.nc = nc
        self.sems = sems

    def begin_pass(self, engine, eng):
        self.cur = engine
        self.eng = eng
        self.cnt = {k: 0 for k in self.sems}
        self.res = {}
        self.bk = {}
        self.lastwait = {}

    def _wait(self, sem, val):
        if val <= 0 or self.lastwait.get(sem, 0) >= val:
            return
        self.lastwait[sem] = val
        self.eng.wait_ge(self.sems[sem], val)

    def op(self, engine, fn, reads=(), writes=(), banks=(), sem=None, inc=1,
           extra_waits=(), grp=None):
        if sem is None:
            sem = ENG2SEM[engine]
        if self.cur == engine:
            own = ENG2SEM[engine]
            waits = {}

            def add(s, v, selfok=False):
                if (s == own and not selfok and s not in DMA_SEMS
                        and engine == "tensor"):
                    return
                waits[s] = max(waits.get(s, 0), v)

            for r in reads:
                e = self.res.get(r)
                if e and e["w"]:
                    add(*e["w"])
            for r in writes:
                e = self.res.get(r)
                if e:
                    if e["w"]:
                        add(*e["w"])
                    for sv in e["r"].values():
                        add(*sv)
            for b in banks:
                for oeng, (sv, og) in self.bk.get(b, {}).items():
                    if oeng == engine and engine == "tensor":
                        continue
                    add(sv[0], sv[1], selfok=True)
            for s, v in extra_waits:
                add(s, v, selfok=True)
            for s in sorted(waits):
                if s in DMA_SEMS:
                    self._wait(s, self.cnt[s])
                else:
                    self._wait(s, waits[s])
            inst = fn(self.eng)
            inst.then_inc(self.sems[sem], inc)
        self.cnt[sem] += inc
        snap = (sem, self.cnt[sem])
        for r in writes:
            self.res[r] = {"w": snap, "r": {}}
        for r in reads:
            e = self.res.setdefault(r, {"w": None, "r": {}})
            e["r"][engine] = snap
        for b in banks:
            self.bk.setdefault(b, {})[engine] = (snap, grp)
        return snap

    def barrier(self):
        for s, v in [(s, self.cnt[s]) for s in self.sems]:
            self._wait(s, v)
        self.res = {}
        self.bk = {}


def build_core(L=2048, dbg=False):
    nst = L // 128
    T = L * NSEQ

    nc = bass.Bass("TRN2", target_bir_lowering=False,
                   detect_race_conditions=False)
    dt = nc.dram_tensor

    x_d = dt("x", [D, T], F32, kind="ExternalInput").ap()
    wq_d = dt("w_q", [D, D], F32, kind="ExternalInput").ap()
    wk_d = dt("w_k", [D, D], F32, kind="ExternalInput").ap()
    wv_d = dt("w_v", [D, D], F32, kind="ExternalInput").ap()
    cwq_d = dt("cw_q", [D, 4], F32, kind="ExternalInput").ap()
    cwk_d = dt("cw_k", [D, 4], F32, kind="ExternalInput").ap()
    cwv_d = dt("cw_v", [D, 4], F32, kind="ExternalInput").ap()
    bw_d = dt("b_w", [D, H], F32, kind="ExternalInput").ap()
    mw_d = dt("mix_w", [D, H], F32, kind="ExternalInput").ap()
    mb_d = dt("mix_b", [H, 1], F32, kind="ExternalInput").ap()
    mb2_d = dt("mix_bias", [H, 1], F32, kind="ExternalInput").ap()
    onw_d = dt("o_norm_w", [DK, 1], F32, kind="ExternalInput").ap()
    ow_d = dt("o_w", [D, D], F32, kind="ExternalInput").ap()
    wfcT_d = dt("w_fcT", [D, HFF], F32, kind="ExternalInput").ap()
    lnw_d = dt("ln_w", [HFF], F32, kind="ExternalInput").ap()
    lnb_d = dt("ln_b", [HFF], F32, kind="ExternalInput").ap()
    wprT_d = dt("w_prT", [HFF, D], F32, kind="ExternalInput").ap()
    ident_d = dt("c_ident", [128, 128], BF16, kind="ExternalInput").ap()
    maskbd_d = dt("c_maskbd", [128, 128], BF16, kind="ExternalInput").ap()
    masku_d = dt("c_masku", [128, CH], BF16, kind="ExternalInput").ap()
    ones_d = dt("c_ones", [128, 1], BF16, kind="ExternalInput").ap()
    onesf_d = dt("c_onesf", [128, 1], F32, kind="ExternalInput").ap()
    onesr_d = dt("c_onesr", [1, ST], BF16, kind="ExternalInput").ap()
    eps_d = dt("c_eps", [128, 4], F32, kind="ExternalInput").ap()
    onesrf_d = dt("c_onesrf", [1, 128], F32, kind="ExternalInput").ap()
    identf_d = dt("c_identf", [128, 128], F32, kind="ExternalInput").ap()

    y_d = dt("y", [D, T], F32, kind="ExternalOutput").ap()
    attn_kind = {"kind": "ExternalOutput"} if dbg else {}
    attn_d = dt("attn_s", [D, T], F32, **attn_kind).ap()
    if dbg:
        dq_d = dt("dbg_q", [D, T], BF16, kind="ExternalOutput").ap()
        dk_d = dt("dbg_k", [D, T], BF16, kind="ExternalOutput").ap()
        dv_d = dt("dbg_v", [D, T], BF16, kind="ExternalOutput").ap()
        dbg_bg_d = dt("dbg_bg", [H, 2 * T], BF16, kind="ExternalOutput").ap()
        dbg_u_d = dt("dbg_u", [128, 16384], BF16, kind="ExternalOutput").ap()
        dofm_d = dt("dbg_ofm", [D, T], BF16, kind="ExternalOutput").ap()

    stk = contextlib.ExitStack()
    sbG = lambda n, sh, d_: stk.enter_context(nc.sbuf_tensor(n, sh, d_))
    psG = lambda n, sh, d_: stk.enter_context(nc.psum_tensor(n, sh, d_))
    sems = {s: stk.enter_context(nc.semaphore(s))
            for s in ("qc", "qcg", "qxb0", "qxb1", "qxf0", "qxf1", "qgs", "qst0", "qst1", "qat", "qat2", "qcr", "qoa0", "qoa1", "qoy", "qod",
                      "G", "T", "V", "S")}

    P_prA = psG("P_prA", [128, 512], F32)
    P_prB = psG("P_prB", [128, 512], F32)
    P_fA = psG("P_fA", [128, 512], F32)
    P_fB = psG("P_fB", [128, 512], F32)
    P_bA = psG("P_bA", [128, 1024], BF16)
    P_bB = psG("P_bB", [128, 1024], BF16)
    P_sA = psG("P_sA", [128, 512], F32)
    P_sB = psG("P_sB", [128, 512], F32)
    PPR = [(P_prA, "bk_prA"), (P_prB, "bk_prB")]
    PF = [(P_fA, "bk_fA"), (P_fB, "bk_fB")]
    PBb = [(P_bA, "bk_bA"), (P_bB, "bk_bB")]
    PSP = [(P_sA, "bk_sA"), (P_sB, "bk_sB")]
    P4 = PPR + PSP

    ident = sbG("ident", [128, 128], BF16)
    maskbd = sbG("maskbd", [128, 128], BF16)
    masku = sbG("masku", [128, CH], BF16)
    ones_c = sbG("ones_c", [128, 1], BF16)
    ones_cf = sbG("ones_cf", [128, 1], F32)
    ones_r = sbG("ones_r", [1, ST], BF16)
    ones_rf = sbG("ones_rf", [1, 128], F32)
    rows = sbG("rows", [1, 2 * ST], F32)
    rbf_mu = sbG("rbf_mu", [1, ST], BF16)
    rbf_rs = sbG("rbf_rs", [1, ST], BF16)
    bc_f = sbG("bc_f", [128, ST], F32)
    gam = sbG("gam", [1, 8], F32)
    gcoln = sbG("gcoln", [128, 2], F32)
    gcolp = sbG("gcolp", [128, 2], F32)
    gpart = sbG("gpart", [128, 16], F32)
    gred = sbG("gred", [128, 32], F32)
    gr4 = sbG("gr4", [128, 4], F32)
    ktmp = sbG("ktmp", [128, 4], F32)
    ktmp2 = sbG("ktmp2", [128, 4], F32)
    grow = sbG("grow", [1, 128], F32)
    lnwb = sbG("lnwb", [128, 64], BF16)
    lnw_f = sbG("lnw_f", [128, 32], F32)
    c12 = sbG("c12", [2, D], BF16)
    eps_s = sbG("eps_s", [128, 4], F32)
    crhs = sbG("crhs", [2, ST], BF16)


    p1 = contextlib.ExitStack()
    sb1 = lambda n, sh, d_: p1.enter_context(nc.sbuf_tensor(n, sh, d_))
    row_scr = sb1("row_scr", [1, HFF], BF16)
    w_q = sb1("w_qs", [128, 8192], BF16)
    w_k = sb1("w_ks", [128, 8192], BF16)
    w_v = sb1("w_vs", [128, 8192], BF16)
    o_w = sb1("o_ws", [128, 8192], BF16)
    bw_s = sb1("bw_s", [128, 64], BF16)
    mw_s = sb1("mw_s", [128, 64], BF16)
    mb_s = sb1("mb_s", [8, 2], F32)
    onw_s = sb1("onw_s", [128, 1], F32)
    cw = {p: sb1(f"cwsb_{p}", [128, 32], F32) for p in "qkv"}
    colsum = sb1("colsum", [1, 3 * D + 2 * H], BF16)
    xfc = [sb1(f"xfc{i}", [128, ST], F32) for i in range(2)]
    xb_ = [sb1(f"xb{i}", [128, 8 * ST], BF16) for i in range(2)]
    fm = {p: sb1(f"fm{p}", [128, 8 * ST], BF16) for p in "qkv"}
    convb = [sb1(f"convb{i}", [128, 4 * 131], BF16) for i in range(2)]
    acc0 = sb1("acc0", [128, ST], BF16)
    acc1 = sb1("acc1", [128, ST], BF16)
    tails = sb1("tails", [128, 24 * 12], BF16)
    bgfm = sb1("bgfm", [8, 2 * ST], BF16)
    bgtm = sb1("bgtm", [128, 64], BF16)
    ofm = sb1("ofm", [128, 8 * ST], BF16)
    attn_sb = [sb1(f"attnsb{i}", [128, ST], F32) for i in range(2)]
    S_bf = sb1("S_bf", [128, H * 512], BF16)
    NSL = 4
    sl = {}
    for nm, w, d_ in (("Abd", 128, BF16), ("C0", 128, BF16), ("D1", 128, BF16),
                      ("C1", 128, BF16), ("D2", 128, BF16), ("C2", 128, BF16),
                      ("D3", 128, BF16), ("C3", 128, BF16), ("C4", 128, BF16),
                      ("ktm", 128, BF16), ("vtm", 128, BF16), ("Xu", 128, BF16),
                      ("Ya", 128, BF16), ("Yb", 128, BF16), ("oms", 128, BF16),
                      ("ofin", 128, BF16), ("Mt", CH, BF16)):
        sl[nm] = [sb1(f"sl_{nm}{i}", [128, w], d_) for i in range(NSL)]
    ubd = [sb1(f"ubd{i}", [128, 512], BF16) for i in range(NSL)]
    scaq = sb1("scaq", [128, 12 * H], F32)
    scr_s = [sb1("scr_s0", [128, 128], BF16)]
    gstage = [sb1(f"gstage{i}", [128, 512], F32) for i in range(2)]
    ident_f = sb1("ident_f", [128, 128], F32)
    p1.close()

    p2w = contextlib.ExitStack()
    sb2w = lambda n, sh, d_: p2w.enter_context(nc.sbuf_tensor(n, sh, d_))
    Wfc = sb2w("Wfc", [128, 8 * HFF], BF16)
    Wpr = sb2w("Wpr", [128, 32 * D], BF16)
    p2t = contextlib.ExitStack()
    sb2t = lambda n, sh, d_: p2t.enter_context(nc.sbuf_tensor(n, sh, d_))
    qstage = [sb2t(f"qstage{i}", [128, HFF], F32) for i in range(2)]
    sgn = sb2t("sgn", [128, HFF], BF16)
    p2t.close()
    p2b = contextlib.ExitStack()
    sb2b = lambda n, sh, d_: p2b.enter_context(nc.sbuf_tensor(n, sh, d_))
    attb = sb2b("attb", [128, 8 * ST], BF16)
    attf = [sb2b(f"attf{i}", [128, ST], F32) for i in range(2)]
    h_sb = sb2b("h_sb", [128, 32 * ST], BF16)
    csrow = sb2b("csrow", [1, HFF], BF16)
    ysb = sb2b("ysb", [128, ST], F32)
    x2s_p2 = [sb2b(f"x2s_p2{i}", [128, ST], BF16) for i in range(2)]

    nc.const_aps.register(F32, 1e-5, eps_s[:, 0:1])
    nc.const_aps.register(F32, 1e-12, eps_s[:, 1:2])
    nc.const_aps.register(F32, 1.0 / DK, eps_s[:, 2:3])

    B = Emitter(nc, sems)

    def dma(q, dst, src, tag=None, reads=(), extra=(), sem="qc"):
        return B.op(q, lambda e: e.dma_start(out=dst, in_=src),
                    reads=reads, writes=(tag,) if tag else (),
                    sem=sem, inc=16, extra_waits=extra)

    def schedule():
        consts = [
            dma("sync", ident[:, :], ident_d[:, :], "ident"),
            dma("sync", maskbd[:, :], maskbd_d[:, :], "maskbd"),
            dma("sync", masku[:, :], masku_d[:, :], "masku"),
            dma("sync", ones_c[:, :], ones_d[:, :], "ones_c"),
            dma("sync", ones_cf[:, :], onesf_d[:, :], "ones_cf"),
            dma("sync", ones_r[:, :], onesr_d[:, :], "ones_r"),
            dma("sync", ones_rf[:, :], onesrf_d[:, :], "ones_rf"),
            dma("sync", onw_s[:, :], onw_d[:, :], "onw"),
            dma("sync", mb_s[:, 0:1], mb_d[:, :], "mb0"),
            dma("sync", mb_s[:, 1:2], mb2_d[:, :], "mb1"),
            dma("sync", eps_s[:, :], eps_d[:, :], "eps"),
            dma("sync", ident_f[:, :], identf_d[:, :], "ident_f"),
        ]
        for wsb, wd in ((w_q, wq_d), (w_k, wk_d), (w_v, wv_d), (o_w, ow_d)):
            src = bass.AP(wd.tensor, 0, [[D, 128], [128 * D, 8], [1, D]])
            dst = bass.AP(wsb, 0, [[8192, 128], [1024, 8], [1, 1024]])
            consts.append(dma("gpsimd", dst, src, wsb.name, sem="qcg"))
        for wsb, wd in ((bw_s, bw_d), (mw_s, mw_d)):
            src = bass.AP(wd.tensor, 0, [[H, 128], [128 * H, 8], [1, H]])
            dst = bass.AP(wsb, 0, [[64, 128], [8, 8], [1, 8]])
            consts.append(dma("gpsimd", dst, src, wsb.name, sem="qcg"))
        for p, cd in (("q", cwq_d), ("k", cwk_d), ("v", cwv_d)):
            src = bass.AP(cd.tensor, 0, [[4, 128], [512, 8], [1, 4]])
            dst = bass.AP(cw[p], 0, [[32, 128], [4, 8], [1, 4]])
            consts.append(dma("sync", dst, src, f"cw{p}"))
        consts.append(dma("gpsimd", row_scr[0:1, :],
                          bass.AP(lnw_d.tensor, 0, [[HFF, 1], [1, HFF]]), "lnw_r",
                          sem="qcg"))

        def load_xb(s):
            par = s % 2
            for a in range(8):
                src = bass.AP(x_d.tensor, a * 128 * T + ST * s,
                              [[T, 128], [1, ST]])
                dma("gpsimd", xb_[par][:, a * ST:(a + 1) * ST], src,
                    f"xb{par}_{a}", sem=f"qxb{par}")

        def load_xfc(s, a):
            src = bass.AP(x_d.tensor, a * 128 * T + ST * s,
                          [[T, 128], [1, ST]])
            dma("sync", xfc[a % 2][:, :], src, f"xfc{a % 2}", sem=f"qxf{a % 2}")

        load_xb(0)

        B.op("vector", lambda e: e.tensor_add(mb_s[:, 0:1], mb_s[:, 0:1],
                                              mb_s[:, 1:2]),
             reads=("mb0", "mb1"), writes=("mbsum",), extra_waits=consts)
        B.op("vector", lambda e: e.tensor_scalar(
            o_w[:, :], o_w[:, :], onw_s[:, :], None, ALU.mult),
            reads=("o_ws", "onw"), writes=("o_ws",))
        for nm, wsb, off, width in (("q", w_q, 0, D), ("k", w_k, D, D),
                                    ("v", w_v, 2 * D, D),
                                    ("b", bw_s, 3 * D, H),
                                    ("m", mw_s, 3 * D + H, H)):
            for half in range(max(1, width // 512)):
                n0 = half * 512
                nn = min(width - n0, 512)
                for a in range(8):
                    rhs = (wsb[:, a * 1024 + n0:a * 1024 + n0 + nn]
                           if width == D else wsb[:, a * 8:(a + 1) * 8])
                    B.op("tensor", lambda e, rhs=rhs, a=a, nn=nn: e.matmul(
                        P_sA[0:1, 0:nn], ones_c[:, :], rhs,
                        start=(a == 0), stop=(a == 7)),
                        reads=(wsb.name, "ones_c"), banks=("bk_sA",))
                B.op("vector", lambda e, off=off, n0=n0, nn=nn: e.tensor_copy(
                    colsum[0:1, off + n0:off + n0 + nn], P_sA[0:1, 0:nn]),
                    banks=("bk_sA",), writes=("colsum",))
        for ci in range(32):
            B.op("tensor", lambda e, ci=ci: e.matmul(
                P_sB[:, ci:ci + 1], row_scr[0:1, ci * 128:(ci + 1) * 128],
                ones_r[0:1, 0:1], start=True, stop=True),
                reads=("lnw_r", "ones_r"), banks=("bk_sB",))
        B.op("vector", lambda e: e.tensor_copy(lnwb[:, 0:32], P_sB[:, 0:32]),
             banks=("bk_sB",), writes=("lnw",))
        B.op("vector", lambda e: e.tensor_copy(lnw_f[:, :], P_sB[:, 0:32]),
             banks=("bk_sB",), writes=("lnwf",))
        dma("gpsimd", row_scr[0:1, :],
            bass.AP(lnb_d.tensor, 0, [[HFF, 1], [1, HFF]]), "lnw_r",
            reads=("lnw_r",), sem="qcg")
        for ci in range(32):
            B.op("tensor", lambda e, ci=ci: e.matmul(
                P_sB[:, 32 + ci:33 + ci], row_scr[0:1, ci * 128:(ci + 1) * 128],
                ones_r[0:1, 0:1], start=True, stop=True),
                reads=("lnw_r", "ones_r"), banks=("bk_sB",))
        B.op("vector", lambda e: e.tensor_copy(lnwb[:, 32:64], P_sB[:, 32:64]),
             banks=("bk_sB",), writes=("lnb",))
        for i in range(NSL):
            B.op("gpsimd", lambda e, i=i: e.memset(ubd[i][:, :], 0),
                 writes=(f"ubd{i}",), sem="G", inc=1)
        for q in range(H):
            B.op("gpsimd", lambda e, q=q: e.memset(
                S_bf[:, q * 512:(q + 1) * 512], 0),
                writes=(f"S_{q}",), sem="G", inc=1)

        def ln_stats_rows(get_rhs, xtag, sqsrc, x2t, x2tag, dd):
            for a in range(8):
                B.op("tensor", lambda e, a=a: e.matmul(
                    P_sA[0:1, 0:ST], ones_c[:, :], get_rhs(a),
                    start=(a == 0), stop=(a == 7)),
                    reads=(f"{xtag}_{a}", "ones_c"), banks=("bk_sA",))
            for a in range(8):
                B.op("scalar", lambda e, a=a: e.activation(
                    x2t[a % 2][:, :], sqsrc(a), AF.Square),
                    reads=(f"{xtag}_{a}",), writes=(f"{x2tag}{a % 2}",))
                B.op("tensor", lambda e, a=a: e.matmul(
                    P_sB[0:1, 0:ST], ones_c[:, :], x2t[a % 2][:, :],
                    start=(a == 0), stop=(a == 7)),
                    reads=(f"{x2tag}{a % 2}", "ones_c"), banks=("bk_sB",))
            B.op("vector", lambda e, dd=dd: e.tensor_single_scalar(
                rows[0:1, 0:ST], P_sA[0:1, 0:ST], -1.0 / dd, ALU.mult),
                banks=("bk_sA",), writes=("rows0",))
            B.op("vector", lambda e, dd=dd: e.tensor_single_scalar(
                rows[0:1, ST:2 * ST], P_sB[0:1, 0:ST], 1.0 / dd, ALU.mult),
                banks=("bk_sB",), writes=("rows1",))
            B.op("vector", lambda e: e.tensor_mul(
                bc_f[0:1, 0:ST], rows[0:1, 0:ST], rows[0:1, 0:ST]),
                reads=("rows0",), writes=("bc_f",))
            B.op("vector", lambda e: e.tensor_sub(
                rows[0:1, ST:2 * ST], rows[0:1, ST:2 * ST], bc_f[0:1, 0:ST]),
                reads=("rows1", "bc_f"), writes=("rows1",))
            B.op("scalar", lambda e: e.activation(
                rows[0:1, ST:2 * ST], rows[0:1, ST:2 * ST], AF.Sqrt, bias=1e-5),
                reads=("rows1",), writes=("rows1",))
            B.op("vector", lambda e: e.reciprocal(rows[0:1, ST:2 * ST], rows[0:1, ST:2 * ST]),
                 reads=("rows1",), writes=("rows1",))
            B.op("vector", lambda e: e.tensor_copy(rbf_mu[0:1, :], rows[0:1, 0:ST]),
                 reads=("rows0",), writes=("rbf_mu",))
            B.op("vector", lambda e: e.tensor_copy(rbf_rs[0:1, :], rows[0:1, ST:2 * ST]),
                 reads=("rows1",), writes=("rbf_rs",))

        def bc_row(src_row, dsttag, dst, bank):
            bk, bkn = bank
            B.op("tensor", lambda e: e.matmul(
                bk[:, 0:ST], ones_r[0:1, 0:128], src_row, start=True, stop=True),
                reads=("rbf_rs", "rbf_mu", "ones_r"), banks=(bkn,))
            B.op("vector", lambda e: e.tensor_copy(dst[:, :], bk[:, 0:ST]),
                 banks=(bkn,), writes=(dsttag,))

        def stripe_phase1(s):
            par = s % 2
            xbs = xb_[par]
            if s + 1 < nst:
                load_xb(s + 1)
            ln_stats_rows(lambda a: xbs[:, a * ST:(a + 1) * ST], f"xb{par}",
                          lambda a: xbs[:, a * ST:(a + 1) * ST], [acc0, acc1],
                          "acc", D)
            bc_row(rbf_rs[0:1, :], "bc_f", bc_f, PPR[0])

            for gi, (wsb, csoff) in enumerate(((bw_s, 3 * D), (mw_s, 3 * D + H))):
                pb, bkn = PSP[gi]
                for a in range(8):
                    B.op("tensor", lambda e, a=a, wsb=wsb, pb=pb: e.matmul(
                        pb[0:8, 0:ST], wsb[:, a * 8:(a + 1) * 8],
                        xbs[:, a * ST:(a + 1) * ST], start=(a == 0), stop=False),
                        reads=(f"xb{par}_{a}", wsb.name), banks=(bkn,),
                        grp=("bg", s, gi))
                B.op("tensor", lambda e, csoff=csoff, pb=pb: e.matmul(
                    pb[0:8, 0:ST], colsum[0:1, csoff:csoff + 8],
                    rbf_mu[0:1, :], start=False, stop=True),
                    reads=("colsum", "rbf_mu"), banks=(bkn,),
                    grp=("bg", s, gi))
                B.op("vector", lambda e, pb=pb: e.tensor_mul(
                    pb[0:8, 0:ST], pb[0:8, 0:ST], bc_f[0:8, :]),
                    reads=("bc_f",), banks=(bkn,))
                outap = bass.AP(bgfm, gi * ST,
                                [[2 * ST, 8], [32, 4], [128, 4], [1, 32]])
                bias = mb_s[:, 0:1] if gi == 1 else 0.0
                B.op("scalar", lambda e, pb=pb, outap=outap, bias=bias:
                     e.activation(outap, pb[0:8, 0:ST], AF.Sigmoid, bias=bias),
                     reads=("mbsum",), banks=(bkn,), writes=(f"bgfm{gi}",))

            pi = 0
            for h in range(H):
                for p, wsb, csoff in (("k", w_k, D), ("v", w_v, 2 * D),
                                      ("q", w_q, 0)):
                    fmdst = fm[p]
                    pb, bkn = P4[pi % 4]
                    cb = convb[pi % 2]
                    cbn = f"convb{pi % 2}"
                    pi += 1
                    for a in range(8):
                        B.op("tensor", lambda e, a=a, h=h, wsb=wsb, pb=pb:
                             e.matmul(pb[:, 0:ST],
                                      wsb[:, a * 1024 + h * 128:a * 1024 + h * 128 + 128],
                                      xbs[:, a * ST:(a + 1) * ST],
                                      start=(a == 0), stop=False),
                             reads=(f"xb{par}_{a}", wsb.name), banks=(bkn,),
                             grp=("qkv", s, p, h))
                    B.op("tensor", lambda e, h=h, csoff=csoff, pb=pb: e.matmul(
                        pb[:, 0:ST],
                        colsum[0:1, csoff + h * 128:csoff + h * 128 + 128],
                        rbf_mu[0:1, :], start=False, stop=True),
                        reads=("colsum", "rbf_mu"), banks=(bkn,),
                        grp=("qkv", s, p, h))
                    ti = "qkv".index(p) * 8 + h
                    if s > 0:
                        B.op("gpsimd", lambda e, cb=cb, ti=ti: e.tensor_copy(
                            bass.AP(cb, 0, [[524, 128], [131, 4], [1, 3]]),
                            bass.AP(tails, ti * 12, [[288, 128], [3, 4], [1, 3]])),
                            reads=(f"tails{ti}",), writes=(cbn,), sem="G", inc=1)
                    else:
                        B.op("gpsimd", lambda e, cb=cb: e.memset(
                            bass.AP(cb, 0, [[524, 128], [131, 4], [1, 3]]), 0),
                            writes=(cbn,), sem="G", inc=1)
                    B.op("vector", lambda e, cb=cb, pb=pb: e.tensor_tensor(
                        bass.AP(cb, 3, [[524, 128], [131, 4], [1, 128]]),
                        pb[:, 0:ST], bc_f[:, :], ALU.mult),
                        reads=("bc_f", cbn), banks=(bkn,), writes=(cbn,))
                    B.op("vector", lambda e, cb=cb, p=p, h=h: e.tensor_single_scalar(
                        acc0[:, :], bass.AP(cb, 0, [[524, 128], [131, 4], [1, 128]]),
                        cw[p][:, h * 4:h * 4 + 1], ALU.mult),
                        reads=(cbn, f"cw{p}"), writes=("acc0",)),
                    B.op("vector", lambda e, cb=cb, p=p, h=h: e.scalar_tensor_tensor(
                        acc1[:, :], bass.AP(cb, 1, [[524, 128], [131, 4], [1, 128]]),
                        cw[p][:, h * 4 + 1:h * 4 + 2], acc0[:, :], ALU.mult, ALU.add),
                        reads=(cbn, "acc0", f"cw{p}"), writes=("acc1",))
                    B.op("vector", lambda e, cb=cb, p=p, h=h: e.scalar_tensor_tensor(
                        acc0[:, :], bass.AP(cb, 2, [[524, 128], [131, 4], [1, 128]]),
                        cw[p][:, h * 4 + 2:h * 4 + 3], acc1[:, :], ALU.mult, ALU.add),
                        reads=(cbn, "acc1", f"cw{p}"), writes=("acc0",))
                    B.op("vector", lambda e, cb=cb, p=p, h=h: e.scalar_tensor_tensor(
                        acc1[:, :], bass.AP(cb, 3, [[524, 128], [131, 4], [1, 128]]),
                        cw[p][:, h * 4 + 3:h * 4 + 4], acc0[:, :], ALU.mult, ALU.add),
                        reads=(cbn, "acc0", f"cw{p}"), writes=("acc1",))
                    B.op("gpsimd", lambda e, cb=cb, ti=ti: e.tensor_copy(
                        bass.AP(tails, ti * 12, [[288, 128], [3, 4], [1, 3]]),
                        bass.AP(cb, 128, [[524, 128], [131, 4], [1, 3]])),
                        reads=(cbn,), writes=(f"tails{ti}",), sem="G", inc=1)
                    outap = bass.AP(fmdst, h * ST,
                                    [[8 * ST, 128], [32, 4], [128, 4], [1, 32]])
                    B.op("scalar", lambda e, outap=outap: e.activation(
                        outap, acc1[:, :], AF.Silu),
                        reads=("acc1",), writes=(f"fm{p}_{h}",))

            if dbg:
                for p, dd in (("q", dq_d), ("k", dk_d), ("v", dv_d)):
                    for h in range(H):
                        dma("scalar",
                            bass.AP(dd.tensor, h * 128 * T + s * ST,
                                    [[T, 128], [1, ST]]),
                            fm[p][:, h * ST:(h + 1) * ST],
                            reads=(f"fm{p}_{h}",), sem="qod")
                dma("scalar", bass.AP(dbg_bg_d.tensor, 2 * ST * s,
                    [[2 * T, 8], [1, 2 * ST]]), bgfm[:, :],
                    reads=("bgfm0", "bgfm1"), sem="qod")

            for c4 in range(4):
                cpar = c4 % 2
                pbB, bknB = PBb[cpar]
                B.op("tensor", lambda e, c4=c4, pbB=pbB: e.transpose(
                    pbB[:, 768:776], bgfm[:, c4 * 128:(c4 + 1) * 128],
                    ident[0:8, 0:8]),
                    reads=("bgfm0", "ident"), banks=(bknB,))
                B.op("tensor", lambda e, c4=c4, pbB=pbB: e.transpose(
                    pbB[:, 776:784], bgfm[:, ST + c4 * 128:ST + (c4 + 1) * 128],
                    ident[0:8, 0:8]),
                    reads=("bgfm1", "ident"), banks=(bknB,))
                B.op("vector", lambda e, c4=c4, pbB=pbB: e.tensor_copy(
                    bgtm[:, c4 * 16:(c4 + 1) * 16], pbB[:, 768:784]),
                    banks=(bknB,), writes=(f"bgtm{c4}",))

                FB4 = [(P_fA, "bk_fA"), (P_fB, "bk_fB"),
                       (P_sA, "bk_sA"), (P_sB, "bk_sB")]
                for grp in range(2):
                    pks = []
                    for idx in range(4):
                        q = grp * 4 + idx
                        pf, bknF = FB4[idx]
                        pb, bknB2 = PBb[idx % 2]
                        pks.append(dict(
                            q=q, i=idx, pf=pf, bkF=bknF, pb=pb, bkB=bknB2,
                            bo=384 * (idx // 2),
                            kfn=f"fmk_{q}", qfn=f"fmq_{q}", vfn=f"fmv_{q}",
                            co=q * ST + c4 * 128, sc=q * 12,
                            Sq=S_bf[:, q * 512:(q + 1) * 512], Sqn=f"S_{q}"))

                    for pk in pks:
                        for col, pnm in ((0, "k"), (128, "v"), (256, "q")):
                            B.op("tensor", lambda e, pk=pk, col=col, pnm=pnm:
                                 e.transpose(
                                     pk["pb"][:, pk["bo"] + col:pk["bo"] + col + 128],
                                     fm[pnm][:, pk["co"]:pk["co"] + 128],
                                     ident[:, :]),
                                 reads=(f"fm{pnm}_{pk['q']}", "ident"),
                                 banks=(pk["bkB"],))
                    for pk in pks:
                        for i in range(4):
                            B.op("tensor", lambda e, pk=pk, i=i: e.matmul(
                                pk["pf"][32 * i:32 * i + 32, 32 * i:32 * i + 32],
                                fm["k"][:, pk["co"] + 32 * i:pk["co"] + 32 * i + 32],
                                fm["k"][:, pk["co"] + 32 * i:pk["co"] + 32 * i + 32],
                                start=True, stop=True, tile_position=(0, 32 * i)),
                                reads=(pk["kfn"],), banks=(pk["bkF"],))
                        for i in range(4):
                            B.op("tensor", lambda e, pk=pk, i=i: e.matmul(
                                pk["pf"][32 * i:32 * i + 32, 128:160],
                                fm["k"][:, pk["co"] + 32 * i:pk["co"] + 32 * i + 32],
                                fm["q"][:, pk["co"] + 32 * i:pk["co"] + 32 * i + 32],
                                start=True, stop=True, tile_position=(0, 32 * i)),
                                reads=(pk["kfn"], pk["qfn"]), banks=(pk["bkF"],))
                    for pk in pks:
                        i_, sc = pk["i"], pk["sc"]
                        B.op("scalar", lambda e, pk=pk: e.activation(
                            scr_s[0][:, :], pk["pb"][:, pk["bo"]:pk["bo"] + 128],
                            AF.Square, accum_out=scaq[:, pk["sc"]:pk["sc"] + 1]),
                            banks=(pk["bkB"],),
                            writes=(f"ssk{i_}", "scr0"))
                        B.op("scalar", lambda e, pk=pk: e.activation(
                            scr_s[0][:, :], pk["pb"][:, pk["bo"] + 256:pk["bo"] + 384],
                            AF.Square, accum_out=scaq[:, pk["sc"] + 1:pk["sc"] + 2]),
                            banks=(pk["bkB"],),
                            writes=(f"ssq{i_}", "scr0"))
                        B.op("vector", lambda e, pk=pk: e.tensor_copy(
                            sl["ktm"][pk["i"]][:, :], pk["pb"][:, pk["bo"]:pk["bo"] + 128]),
                            banks=(pk["bkB"],), writes=(f"ktm{i_}",))
                        B.op("vector", lambda e, pk=pk: e.tensor_copy(
                            sl["vtm"][pk["i"]][:, :], pk["pb"][:, pk["bo"] + 128:pk["bo"] + 256]),
                            banks=(pk["bkB"],), writes=(f"vtm{i_}",))
                        B.op("scalar", lambda e, sc=sc: e.activation(
                            scaq[:, sc + 2:sc + 3], scaq[:, sc:sc + 1],
                            AF.Sqrt, bias=1e-12),
                            reads=(f"ssk{i_}",), writes=(f"sqk{i_}",))
                        B.op("scalar", lambda e, sc=sc: e.activation(
                            scaq[:, sc + 3:sc + 4], scaq[:, sc + 1:sc + 2],
                            AF.Sqrt, bias=1e-12),
                            reads=(f"ssq{i_}",), writes=(f"sqq{i_}",))
                        B.op("vector", lambda e, sc=sc: e.reciprocal(
                            scaq[:, sc + 4:sc + 5], scaq[:, sc + 2:sc + 3]),
                            reads=(f"sqk{i_}",), writes=(f"rsk{i_}",))
                        B.op("vector", lambda e, sc=sc: e.reciprocal(
                            scaq[:, sc + 5:sc + 6], scaq[:, sc + 3:sc + 4]),
                            reads=(f"sqq{i_}",), writes=(f"rsq{i_}",))
                        bcol = bgtm[:, c4 * 16 + pk["q"]:c4 * 16 + pk["q"] + 1]
                        gcol = bgtm[:, c4 * 16 + 8 + pk["q"]:c4 * 16 + 8 + pk["q"] + 1]
                        B.op("vector", lambda e, sc=sc: e.tensor_mul(
                            scaq[:, sc + 6:sc + 7], scaq[:, sc + 4:sc + 5],
                            scaq[:, sc + 4:sc + 5]),
                            reads=(f"rsk{i_}",), writes=(f"rs2{i_}",))
                        B.op("vector", lambda e, sc=sc, bcol=bcol:
                             e.scalar_tensor_tensor(
                                 scaq[:, sc + 7:sc + 8], scaq[:, sc + 6:sc + 7],
                                 -1.0, bcol, ALU.mult, ALU.mult),
                             reads=(f"rs2{i_}", f"bgtm{c4}"), writes=(f"nrs2b{i_}",))
                        B.op("vector", lambda e, sc=sc, bcol=bcol: e.tensor_mul(
                            scaq[:, sc + 8:sc + 9], scaq[:, sc + 4:sc + 5], bcol),
                            reads=(f"rsk{i_}", f"bgtm{c4}"), writes=(f"rsb{i_}",))
                        B.op("vector", lambda e, sc=sc, gcol=gcol: e.tensor_mul(
                            scaq[:, sc + 9:sc + 10], gcol, scaq[:, sc + 5:sc + 6]),
                            reads=(f"rsq{i_}", f"bgtm{c4}"), writes=(f"grsq{i_}",))
                        B.op("vector", lambda e, sc=sc, gcol=gcol: e.tensor_scalar(
                            scaq[:, sc + 10:sc + 11], gcol, -1.0, 1.0,
                            ALU.mult, ALU.add),
                            reads=(f"bgtm{c4}",), writes=(f"omg{i_}",))
                        B.op("vector", lambda e, pk=pk, sc=sc:
                             e.scalar_tensor_tensor(
                                 sl["Abd"][pk["i"]][:, :], pk["pf"][:, 0:128],
                                 scaq[:, sc + 7:sc + 8], maskbd[:, :],
                                 ALU.mult, ALU.mult),
                             reads=(f"nrs2b{i_}", "maskbd"), banks=(pk["bkF"],),
                             writes=(f"Abd{i_}",))
                        B.op("vector", lambda e, pk=pk: e.tensor_mul(
                            sl["Mt"][pk["i"]][:, :], pk["pf"][:, 128:160],
                            masku[:, :]),
                            reads=("masku",), banks=(pk["bkF"],),
                            writes=(f"Mt{i_}",))
                        B.op("vector", lambda e, pk=pk, sc=sc:
                             e.tensor_single_scalar(
                                 sl["Xu"][pk["i"]][:, :], pk["pb"][:, pk["bo"] + 128:pk["bo"] + 256],
                                 scaq[:, sc + 8:sc + 9], ALU.mult),
                             reads=(f"rsb{i_}",), banks=(pk["bkB"],),
                             writes=(f"Xu{i_}",))
                    for pk in pks:
                        i_ = pk["i"]
                        ppR, bknR = PPR[pk["i"] % 2]
                        rco = (pk["i"] // 2) * 128
                        for i in range(4):
                            B.op("tensor", lambda e, pk=pk, i=i, ppR=ppR,
                                 rco=rco: e.matmul(
                                ppR[32 * i:32 * i + 32, rco:rco + 128],
                                fm["k"][:, pk["co"] + 32 * i:pk["co"] + 32 * i + 32],
                                pk["Sq"][:, 128 * i:128 * (i + 1)],
                                start=True, stop=True, tile_position=(0, 32 * i)),
                                reads=(pk["kfn"], pk["Sqn"]), banks=(bknR,))
                    for pk in pks:
                        i_ = pk["i"]
                        B.op("tensor", lambda e, pk=pk: e.transpose(
                            pk["pb"][:, pk["bo"]:pk["bo"] + 128], sl["Abd"][pk["i"]][:, :],
                            ident[:, :]),
                            reads=(f"Abd{i_}", "ident"), banks=(pk["bkB"],))
                        B.op("scalar", lambda e, pk=pk: e.copy(
                            sl["C0"][pk["i"]][:, :], pk["pb"][:, pk["bo"]:pk["bo"] + 128]),
                            banks=(pk["bkB"],), writes=(f"C0{i_}",))
                    for lvl in ((("D1", "C0", "Abd"), ("C1", "Abd", "C0")),
                                (("D2", "C1", "D1"), ("C2", "D1", "C1")),
                                (("D3", "C2", "D2"), ("C3", "D2", "C2")),
                                (("C4", "D3", "C3"),)):
                        for pk in pks:
                            i_ = pk["i"]
                            for li, (dstn, ltn, rtn) in enumerate(lvl):
                                co = 160 + 128 * li
                                B.op("tensor", lambda e, pk=pk, ltn=ltn,
                                     rtn=rtn, co=co:
                                     e.matmul(pk["pf"][:, co:co + 128],
                                              sl[ltn][pk["i"]][:, :],
                                              sl[rtn][pk["i"]][:, :],
                                              start=True, stop=True),
                                     reads=(f"{ltn}{i_}", f"{rtn}{i_}"),
                                     banks=(pk["bkF"],))
                        for pk in pks:
                            i_ = pk["i"]
                            for li, (dstn, ltn, rtn) in enumerate(lvl):
                                co = 160 + 128 * li
                                if li == 0:
                                    B.op("vector", lambda e, pk=pk, dstn=dstn,
                                         co=co: e.tensor_copy(
                                        sl[dstn][pk["i"]][:, :],
                                        pk["pf"][:, co:co + 128]),
                                        banks=(pk["bkF"],),
                                        writes=(f"{dstn}{i_}",))
                                else:
                                    B.op("scalar", lambda e, pk=pk, dstn=dstn,
                                         co=co: e.copy(
                                        sl[dstn][pk["i"]][:, :],
                                        pk["pf"][:, co:co + 128]),
                                        banks=(pk["bkF"],),
                                        writes=(f"{dstn}{i_}",))
                    for pk in pks:
                        i_ = pk["i"]
                        ppR, bknR = PPR[pk["i"] % 2]
                        rco = (pk["i"] // 2) * 128
                        B.op("vector", lambda e, pk=pk, ppR=ppR, rco=rco:
                             e.scalar_tensor_tensor(
                            sl["Ya"][pk["i"]][:, :], ppR[:, rco:rco + 128],
                            scaq[:, pk["sc"] + 7:pk["sc"] + 8],
                            sl["Xu"][pk["i"]][:, :], ALU.mult, ALU.add),
                            reads=(f"nrs2b{i_}", f"Xu{i_}"), banks=(bknR,),
                            writes=(f"Ya{i_}",))
                    for ai, cn in enumerate(("C4", "C3", "C2", "C1")):
                        yi, yo = ("Ya", "Yb") if ai % 2 == 0 else ("Yb", "Ya")
                        for pk in pks:
                            i_ = pk["i"]
                            B.op("tensor", lambda e, pk=pk, cn=cn, yi=yi:
                                 e.matmul(pk["pf"][:, 288:416],
                                          sl[cn][pk["i"]][:, :],
                                          sl[yi][pk["i"]][:, :],
                                          start=True, stop=False),
                                 reads=(f"{cn}{i_}", f"{yi}{i_}"),
                                 banks=(pk["bkF"],))
                            B.op("tensor", lambda e, pk=pk, yi=yi:
                                 e.matmul(pk["pf"][:, 288:416],
                                          ident[:, :],
                                          sl[yi][pk["i"]][:, :],
                                          start=False, stop=True),
                                 reads=(f"{yi}{i_}", "ident"),
                                 banks=(pk["bkF"],))
                            B.op("scalar", lambda e, pk=pk, yo=yo: e.copy(
                                sl[yo][pk["i"]][:, :], pk["pf"][:, 288:416]),
                                banks=(pk["bkF"],), writes=(f"{yo}{i_}",))
                    for pk in pks:
                        i_ = pk["i"]
                        B.op("tensor", lambda e, pk=pk: e.matmul(
                            pk["pf"][:, 288:416], sl["C0"][pk["i"]][:, :],
                            sl["Ya"][pk["i"]][:, :], start=True, stop=False),
                            reads=(f"C0{i_}", f"Ya{i_}"), banks=(pk["bkF"],))
                        B.op("tensor", lambda e, pk=pk: e.matmul(
                            pk["pf"][:, 288:416], ident[:, :],
                            sl["Ya"][pk["i"]][:, :], start=False, stop=True),
                            reads=(f"Ya{i_}", "ident"), banks=(pk["bkF"],))
                        for i in range(4):
                            B.op("scalar", lambda e, pk=pk, i=i: e.copy(
                                ubd[pk["i"]][32 * i:32 * i + 32,
                                             128 * i:128 * (i + 1)],
                                pk["pf"][32 * i:32 * i + 32, 288:416]),
                                banks=(pk["bkF"],), writes=(f"ubd{i_}",))
                    if dbg and s == 0:
                        for pk in pks:
                            dma("scalar", bass.AP(
                                dbg_u_d.tensor,
                                (c4 * 8 + pk["q"]) * 512, [[16384, 128], [1, 512]]),
                                ubd[pk["i"]][:, :],
                                reads=(f"ubd{pk['i']}",), sem="qod")
                    for pk in pks:
                        i_ = pk["i"]
                        ppR, bknR = PPR[pk["i"] % 2]
                        oco = 256 + (pk["i"] // 2) * 128
                        for i in range(4):
                            B.op("tensor", lambda e, pk=pk, i=i, ppR=ppR,
                                 oco=oco: e.matmul(
                                ppR[32 * i:32 * i + 32, oco:oco + 128],
                                fm["q"][:, pk["co"] + 32 * i:pk["co"] + 32 * i + 32],
                                pk["Sq"][:, 128 * i:128 * (i + 1)],
                                start=True, stop=True, tile_position=(0, 32 * i)),
                                reads=(pk["qfn"], pk["Sqn"]), banks=(bknR,))
                        for i in range(4):
                            B.op("tensor", lambda e, pk=pk, i=i, ppR=ppR,
                                 oco=oco: e.matmul(
                                ppR[32 * i:32 * i + 32, oco:oco + 128],
                                sl["Mt"][pk["i"]][32 * i:32 * i + 32, :],
                                ubd[pk["i"]][32 * i:32 * i + 32,
                                             128 * i:128 * (i + 1)],
                                start=False, stop=True,
                                tile_position=(32 * i, 32 * i),
                                skip_group_check=True),
                                reads=(f"Mt{i_}", f"ubd{i_}"), banks=(bknR,))
                    for pk in pks:
                        i_, sc = pk["i"], pk["sc"]
                        B.op("vector", lambda e, pk=pk, sc=sc:
                             e.tensor_single_scalar(
                                 sl["ofin"][pk["i"]][:, :],
                                 sl["vtm"][pk["i"]][:, :],
                                 scaq[:, sc + 10:sc + 11], ALU.mult),
                             reads=(f"vtm{i_}", f"omg{i_}"), writes=(f"vcon{i_}",))
                        ppR, bknR = PPR[pk["i"] % 2]
                        oco = 256 + (pk["i"] // 2) * 128
                        B.op("vector", lambda e, pk=pk, sc=sc, ppR=ppR,
                             oco=oco:
                             e.scalar_tensor_tensor(
                                 sl["oms"][pk["i"]][:, :], ppR[:, oco:oco + 128],
                                 scaq[:, sc + 9:sc + 10],
                                 sl["ofin"][pk["i"]][:, :], ALU.mult, ALU.add),
                             reads=(f"grsq{i_}", f"vcon{i_}"), banks=(bknR,),
                             writes=(f"oms{i_}",))
                        B.op("scalar", lambda e, pk=pk, sc=sc: e.activation(
                            scr_s[0][:, :], sl["oms"][pk["i"]][:, :],
                            AF.Square, accum_out=scaq[:, sc + 11:sc + 12]),
                            reads=(f"oms{i_}",),
                            writes=(f"sso{i_}", "scr0"))
                        B.op("scalar", lambda e, sc=sc: e.activation(
                            scaq[:, sc + 11:sc + 12], scaq[:, sc + 11:sc + 12],
                            AF.Sqrt, bias=1e-5, scale=1.0 / DK),
                            reads=(f"sso{i_}",), writes=(f"sso{i_}",))
                        B.op("vector", lambda e, sc=sc: e.reciprocal(
                            scaq[:, sc + 11:sc + 12], scaq[:, sc + 11:sc + 12]),
                            reads=(f"sso{i_}",), writes=(f"sso{i_}",))
                        B.op("vector", lambda e, pk=pk, sc=sc:
                             e.tensor_single_scalar(
                                 sl["ofin"][pk["i"]][:, :],
                                 sl["oms"][pk["i"]][:, :],
                                 scaq[:, sc + 11:sc + 12], ALU.mult),
                             reads=(f"oms{i_}", f"sso{i_}"), writes=(f"ofin{i_}",))
                    for pk in pks:
                        i_ = pk["i"]
                        pp, bknP = PPR[pk["i"] % 2]
                        B.op("tensor", lambda e, pk=pk, pp=pp: e.matmul(
                            pp[:, :], sl["ktm"][pk["i"]][:, :],
                            ubd[pk["i"]][:, :], start=True, stop=True),
                            reads=(f"ktm{i_}", f"ubd{i_}"), banks=(bknP,))
                        B.op("vector", lambda e, pk=pk, pp=pp: e.tensor_add(
                            pk["Sq"][:, :], pp[:, :], pk["Sq"][:, :]),
                            reads=(pk["Sqn"],), banks=(bknP,),
                            writes=(pk["Sqn"],))
                    for pk in pks:
                        i_ = pk["i"]
                        B.op("tensor", lambda e, pk=pk: e.transpose(
                            pk["pb"][:, pk["bo"] + 128:pk["bo"] + 256], sl["ofin"][pk["i"]][:, :],
                            ident[:, :]),
                            reads=(f"ofin{i_}", "ident"), banks=(pk["bkB"],))
                        B.op("scalar", lambda e, pk=pk, c4=c4: e.copy(
                            ofm[:, pk["q"] * ST + c4 * 128:pk["q"] * ST + c4 * 128 + 128],
                            pk["pb"][:, pk["bo"] + 128:pk["bo"] + 256]),
                            banks=(pk["bkB"],), writes=(f"ofm{pk['q']}_{c4}",))

            load_xfc(s, 0)
            for oc in range(8):
                if oc + 1 < 8:
                    load_xfc(s, oc + 1)
                pb, bkn = PPR[oc % 2]
                for a in range(8):
                    B.op("tensor", lambda e, a=a, oc=oc, pb=pb: e.matmul(
                        pb[:, 0:ST],
                        o_w[:, a * 1024 + oc * 128:a * 1024 + oc * 128 + 128],
                        ofm[:, a * ST:(a + 1) * ST], start=(a == 0), stop=(a == 7)),
                        reads=tuple(f"ofm{a}_{cc}" for cc in range(4)) + ("o_ws",),
                        banks=(bkn,), grp=("o", s, oc))
                asb = attn_sb[oc % 2]
                xin = bass.AP(xfc[oc % 2], 0,
                              [[ST, 128], [32, 4], [128, 4], [1, 32]])
                B.op("vector", lambda e, pb=pb, asb=asb, xin=xin: e.tensor_add(
                    asb[:, :], pb[:, 0:ST], xin),
                    reads=(f"xfc{oc % 2}", f"asb{oc % 2}"), banks=(bkn,),
                    writes=(f"asb{oc % 2}",))
                dma("scalar",
                    bass.AP(attn_d.tensor, oc * 128 * T + s * ST,
                            [[T, 128], [1, ST]]),
                    asb[:, :], reads=(f"asb{oc % 2}",), sem=f"qoa{oc % 2}")
            ntile = 64 + 64
            per = (ntile + nst - 1) // nst
            for k in range(per * s, min(per * (s + 1), ntile)):
                if k < 64:
                    rr, cc = k // 8, k % 8
                    src = bass.AP(wfcT_d.tensor, rr * 128 * HFF + cc * 512,
                                  [[HFF, 128], [1, 512]])
                    gi = 0
                else:
                    rr, cc = (k - 64) // 2, (k - 64) % 2
                    src = bass.AP(wprT_d.tensor, rr * 128 * D + cc * 512,
                                  [[D, 128], [1, 512]])
                    gi = 1
                gst = gstage[k % 2]
                dma("sync", gst[:, :], src, f"gstage{k % 2}", sem="qgs")
                B.op("vector", lambda e, gst=gst: e.tensor_reduce(
                    gred[:, :], bass.AP(gst, 0, [[512, 128], [16, 32], [1, 16]]),
                    mybir.AxisListType.X, ALU.add, apply_absolute_value=True),
                    reads=(f"gstage{k % 2}",), writes=("gred",))
                B.op("vector", lambda e: e.tensor_reduce(
                    gr4[:, :], bass.AP(gred, 0, [[32, 128], [8, 4], [1, 8]]),
                    mybir.AxisListType.X, ALU.add),
                    reads=("gred",), writes=("gr4",))
                ac = gpart[:, 8 * gi:8 * gi + 4]
                cp = gpart[:, 8 * gi + 4:8 * gi + 8]
                if k in (0, 64):
                    B.op("vector", lambda e, ac=ac: e.tensor_copy(ac, gr4[:, :]),
                        reads=("gr4",), writes=(f"gpart{gi}",))
                    B.op("vector", lambda e, cp=cp: e.memset(cp, 0),
                        writes=(f"gcomp{gi}",))
                else:
                    B.op("vector", lambda e, ac=ac: e.tensor_add(
                        ktmp[:, :], ac, gr4[:, :]),
                        reads=("gr4", f"gpart{gi}"), writes=("ktmp",))
                    B.op("vector", lambda e, ac=ac: e.tensor_sub(
                        ktmp2[:, :], ac, ktmp[:, :]),
                        reads=("ktmp", f"gpart{gi}"), writes=("ktmp2",))
                    B.op("vector", lambda e: e.tensor_add(
                        ktmp2[:, :], ktmp2[:, :], gr4[:, :]),
                        reads=("ktmp2", "gr4"), writes=("ktmp2",))
                    B.op("vector", lambda e, cp=cp: e.tensor_add(
                        cp, cp, ktmp2[:, :]),
                        reads=("ktmp2", f"gcomp{gi}"), writes=(f"gcomp{gi}",))
                    B.op("vector", lambda e, ac=ac: e.tensor_copy(ac, ktmp[:, :]),
                        reads=("ktmp",), writes=(f"gpart{gi}",))

            if dbg:
                for a in range(8):
                    dma("scalar", bass.AP(dofm_d.tensor, a * 128 * T + s * ST,
                        [[T, 128], [1, ST]]), ofm[:, a * ST:(a + 1) * ST],
                        reads=tuple(f"ofm{a}_{cc}" for cc in range(4)), sem="qod")

        B.op("vector", lambda e: e.memset(P_fA[:, 0:160], 0), banks=("bk_fA",))
        B.op("vector", lambda e: e.memset(P_fB[:, 0:160], 0), banks=("bk_fB",))
        B.op("vector", lambda e: e.memset(P_sA[:, 0:160], 0), banks=("bk_sA",))
        B.op("vector", lambda e: e.memset(P_sB[:, 0:160], 0), banks=("bk_sB",))
        B.barrier()
        for s in range(nst):
            stripe_phase1(s)

        for gi in range(2):
            B.op("vector", lambda e, gi=gi: e.tensor_add(
                ktmp[:, :], gpart[:, 8 * gi:8 * gi + 4],
                gpart[:, 8 * gi + 4:8 * gi + 8]),
                reads=(f"gpart{gi}", f"gcomp{gi}"), writes=("ktmp",))
            B.op("vector", lambda e: e.tensor_reduce(
                gr4[:, 0:1], ktmp[:, :], mybir.AxisListType.X, ALU.add),
                reads=("ktmp",), writes=("gr4",))
            B.op("tensor", lambda e: e.transpose(
                P_sA[0:1, 0:128], gr4[:, 0:1], ident_f[:, :]),
                reads=("gr4", "ident_f"), banks=("bk_sA",))
            B.op("vector", lambda e: e.tensor_copy(grow[0:1, :], P_sA[0:1, 0:128]),
                 banks=("bk_sA",), writes=("grow",))
            for half in (64, 32, 16, 8, 4, 2, 1):
                B.op("vector", lambda e, half=half: e.tensor_add(
                    grow[0:1, 0:half], grow[0:1, 0:half],
                    grow[0:1, half:2 * half]),
                    reads=("grow",), writes=("grow",))
            B.op("vector", lambda e, gi=gi: e.tensor_scalar(
                gam[0:1, 4 * gi:4 * gi + 1], grow[0:1, 0:1],
                0.5 / float(HFF * D), 0.5e-6, ALU.mult, ALU.add),
                reads=("grow",), writes=(f"gam{gi}",))
            B.op("tensor", lambda e, gi=gi: e.matmul(
                P_sA[:, 0:1], ones_rf[0:1, 0:128],
                gam[0:1, 4 * gi:4 * gi + 1], start=True, stop=True),
                reads=(f"gam{gi}", "ones_rf"), banks=("bk_sA",))
            B.op("vector", lambda e, gi=gi: e.tensor_single_scalar(
                gcolp[:, gi:gi + 1], P_sA[:, 0:1], 1.0, ALU.mult),
                banks=("bk_sA",), writes=(f"gcolp{gi}",))
            B.op("vector", lambda e, gi=gi: e.tensor_single_scalar(
                gcoln[:, gi:gi + 1], P_sA[:, 0:1], -1.0, ALU.mult),
                banks=("bk_sA",), writes=(f"gcoln{gi}",))

        B.barrier()
        for k in range(40):
            qp = k % 2
            if k < 8:
                src = wfcT_d[128 * k:128 * (k + 1), :]
                dst = Wfc[:, k * HFF:(k + 1) * HFF]
                gi, ww, dn = 0, HFF, f"Wfc{k}"
            else:
                kk = k - 8
                src = wprT_d[128 * kk:128 * (kk + 1), :]
                dst = Wpr[:, kk * D:(kk + 1) * D]
                gi, ww, dn = 1, D, f"Wpr{kk}"
            dma("sync", qstage[qp][:, 0:ww], src, f"qstage{qp}", sem=f"qst{qp}")
            B.op("scalar", lambda e, qp=qp, gi=gi, ww=ww: e.activation(
                sgn[:, 0:ww], qstage[qp][:, 0:ww], AF.Sign,
                bias=gcoln[:, gi:gi + 1]),
                reads=(f"qstage{qp}", f"gcoln{gi}", "sgn_r"), writes=("sgn",))
            B.op("scalar", lambda e, qp=qp, gi=gi, ww=ww, dst=dst: e.activation(
                dst, qstage[qp][:, 0:ww], AF.Sign, bias=gcolp[:, gi:gi + 1]),
                reads=(f"qstage{qp}", f"gcolp{gi}"), writes=(dn,))
            B.op("vector", lambda e, ww=ww, dst=dst: e.tensor_add(
                dst, dst, sgn[:, 0:ww]),
                reads=("sgn", dn), writes=(dn, "sgn_r"))
            B.op("vector", lambda e, dst=dst: e.tensor_single_scalar(
                dst, dst, 0.5, ALU.mult),
                reads=(dn,), writes=(dn,))

        for half in range(2):
            n0 = half * 512
            pbh, bknH = PSP[half]
            for hc in range(32):
                lh = bass.AP(lnwb, hc, [[64, 128], [32, 2]])
                B.op("tensor", lambda e, hc=hc, n0=n0, pbh=pbh, lh=lh: e.matmul(
                    pbh[0:2, 0:512], lh, Wpr[:, hc * D + n0:hc * D + n0 + 512],
                    start=(hc == 0), stop=(hc == 31)),
                    reads=(f"Wpr{hc}", "lnw", "lnb"), banks=(bknH,))
            B.op("vector", lambda e, n0=n0, pbh=pbh: e.tensor_copy(
                c12[:, n0:n0 + 512], pbh[0:2, 0:512]),
                banks=(bknH,), writes=("c12",))
        for hc in range(32):
            B.op("vector", lambda e, hc=hc: e.tensor_scalar(
                Wpr[:, hc * D:(hc + 1) * D], Wpr[:, hc * D:(hc + 1) * D],
                lnw_f[:, hc:hc + 1], None, ALU.mult),
                reads=(f"Wpr{hc}", "lnwf"), writes=(f"Wpr{hc}",))

        B.barrier()
        for jc in range(8):
            for a in range(8):
                B.op("tensor", lambda e, jc=jc, a=a: e.matmul(
                    P_sA[0:1, 0:512], ones_c[:, :],
                    Wfc[:, a * HFF + jc * 512:a * HFF + jc * 512 + 512],
                    start=(a == 0), stop=(a == 7)),
                    reads=(f"Wfc{a}", "ones_c"), banks=("bk_sA",))
            B.op("vector", lambda e, jc=jc: e.tensor_copy(
                csrow[0:1, jc * 512:(jc + 1) * 512], P_sA[0:1, 0:512]),
                banks=("bk_sA",), writes=("csrow",))

        def load_att(s):
            for a in range(8):
                src = bass.AP(attn_d.tensor, a * 128 * T + s * ST,
                              [[T, 128], [1, ST]])
                dma("gpsimd", attb[:, a * ST:(a + 1) * ST], src, f"attb_{a}",
                    sem="qat2")

        def load_attf(s, a):
            src = bass.AP(attn_d.tensor, a * 128 * T + s * ST,
                          [[T, 128], [1, ST]])
            dma("sync", attf[a % 2][:, :], src, f"attf{a % 2}", sem="qat")

        load_att(0)
        for s in range(nst):
            ln_stats_rows(lambda a: attb[:, a * ST:(a + 1) * ST], "attb",
                          lambda a: attb[:, a * ST:(a + 1) * ST], x2s_p2,
                          "x2p", D)
            bc_row(rbf_rs[0:1, :], "bc_f", bc_f, PSP[1])
            def fc_group(j):
                pb, bkn = P4[j % 4]
                for a in range(8):
                    B.op("tensor", lambda e, a=a, j=j, pb=pb: e.matmul(
                        pb[:, 0:ST],
                        Wfc[:, a * HFF + j * 128:a * HFF + j * 128 + 128],
                        attb[:, a * ST:(a + 1) * ST],
                        start=(a == 0), stop=False),
                        reads=(f"attb_{a}", f"Wfc{a}"), banks=(bkn,),
                        grp=("fc", s, j))
                B.op("tensor", lambda e, j=j, pb=pb: e.matmul(
                    pb[:, 0:ST], csrow[0:1, j * 128:(j + 1) * 128],
                    rbf_mu[0:1, :], start=False, stop=True),
                    reads=("csrow", "rbf_mu"), banks=(bkn,),
                    grp=("fc", s, j))

            def fc_epi(j):
                pb, bkn = P4[j % 4]
                B.op("vector", lambda e, j=j, pb=pb: e.tensor_mul(
                    ysb[:, :], pb[:, 0:ST], bc_f[:, :]),
                    banks=(bkn,), reads=("bc_f",), writes=("ysb",))
                B.op("scalar", lambda e, j=j: e.activation(
                    h_sb[:, j * ST:(j + 1) * ST], ysb[:, :], AF.Gelu),
                    reads=("ysb",), writes=(f"h_{j}",))
                B.op("tensor", lambda e, j=j: e.matmul(
                    P_fA[0:1, 0:ST], ones_c[:, :], h_sb[:, j * ST:(j + 1) * ST],
                    start=(j == 0), stop=(j == 31)),
                    reads=(f"h_{j}", "ones_c"), banks=("bk_fA",))
                B.op("scalar", lambda e, j=j: e.activation(
                    x2s_p2[j % 2][:, :], h_sb[:, j * ST:(j + 1) * ST], AF.Square),
                    reads=(f"h_{j}",), writes=(f"x2p{j % 2}",))
                B.op("tensor", lambda e, j=j: e.matmul(
                    P_fB[0:1, 0:ST], ones_c[:, :], x2s_p2[j % 2][:, :],
                    start=(j == 0), stop=(j == 31)),
                    reads=(f"x2p{j % 2}", "ones_c"), banks=("bk_fB",))

            for j in range(32):
                fc_group(j)
                if j > 0:
                    fc_epi(j - 1)
            fc_epi(31)
            B.op("vector", lambda e: e.tensor_single_scalar(
                rows[0:1, 0:ST], P_fA[0:1, 0:ST], -1.0 / HFF, ALU.mult),
                banks=("bk_fA",), writes=("rows0",))
            B.op("vector", lambda e: e.tensor_single_scalar(
                rows[0:1, ST:2 * ST], P_fB[0:1, 0:ST], 1.0 / HFF, ALU.mult),
                banks=("bk_fB",), writes=("rows1",))
            B.op("vector", lambda e: e.tensor_mul(
                bc_f[0:1, 0:ST], rows[0:1, 0:ST], rows[0:1, 0:ST]),
                reads=("rows0",), writes=("bc_f",))
            B.op("vector", lambda e: e.tensor_sub(
                rows[0:1, ST:2 * ST], rows[0:1, ST:2 * ST], bc_f[0:1, 0:ST]),
                reads=("rows1", "bc_f"), writes=("rows1",))
            B.op("scalar", lambda e: e.activation(
                rows[0:1, ST:2 * ST], rows[0:1, ST:2 * ST], AF.Sqrt, bias=1e-5),
                reads=("rows1",), writes=("rows1",))
            dma("gpsimd", bass.AP(crhs, 0, [[ST, 2], [1, ST]]),
                rows[0:1, 0:2 * ST], "crhs0",
                reads=("rows0", "rows1"), sem="qcr")
            B.op("vector", lambda e: e.reciprocal(
                rows[0:1, ST:2 * ST], rows[0:1, ST:2 * ST]),
                reads=("rows1",), writes=("rows1",))
            B.op("vector", lambda e: e.tensor_copy(rbf_rs[0:1, :],
                                                   rows[0:1, ST:2 * ST]),
                 reads=("rows1",), writes=("rbf_rs",))
            bc_row(rbf_rs[0:1, :], "bc_f", bc_f, PSP[1])
            load_attf(s, 0)
            for dcn in range(8):
                if dcn + 1 < 8:
                    load_attf(s, dcn + 1)
                pb, bkn = P4[dcn % 4]
                for j in range(32):
                    B.op("tensor", lambda e, j=j, dcn=dcn, pb=pb: e.matmul(
                        pb[:, 0:ST],
                        Wpr[:, j * D + dcn * 128:j * D + dcn * 128 + 128],
                        h_sb[:, j * ST:(j + 1) * ST],
                        start=(j == 0), stop=False),
                        reads=(f"h_{j}", f"Wpr{j}"), banks=(bkn,),
                        grp=("pr", s, dcn))
                B.op("tensor", lambda e, dcn=dcn, pb=pb: e.matmul(
                    pb[:, 0:ST], c12[:, dcn * 128:(dcn + 1) * 128],
                    crhs[:, :], start=False, stop=True),
                    reads=("c12", "crhs0"), banks=(bkn,),
                    grp=("pr", s, dcn))
                B.op("vector", lambda e, dcn=dcn, pb=pb: e.tensor_mul(
                    ysb[:, :], pb[:, 0:ST], bc_f[:, :]),
                    banks=(bkn,), reads=("bc_f",), writes=("ysb",))
                B.op("vector", lambda e, dcn=dcn: e.tensor_add(
                    ysb[:, :], ysb[:, :], attf[dcn % 2][:, :]),
                    reads=(f"attf{dcn % 2}",), writes=("ysb",))
                dma("scalar",
                    bass.AP(y_d.tensor, dcn * 128 * T + s * ST,
                            [[T, 128], [1, ST]]),
                    ysb[:, :], reads=("ysb",), sem="qoy")
                if s + 1 < nst:
                    src = bass.AP(attn_d.tensor, dcn * 128 * T + (s + 1) * ST,
                                  [[T, 128], [1, ST]])
                    dma("gpsimd", attb[:, dcn * ST:(dcn + 1) * ST], src,
                        f"attb_{dcn}", sem="qat2")
        B.op("gpsimd", lambda e: e.engine_nop(),
             extra_waits=[(q, B.cnt[q]) for q in
                          ("qoa0", "qoa1", "qoy", "qod")], sem="G", inc=1)


    with nc.Block() as block:
        def mk(name):
            def run(eng):
                B.begin_pass(name, eng)
                schedule()
            return run
        block.sync(mk("sync"))
        block.gpsimd(mk("gpsimd"))
        block.tensor(mk("tensor"))
        block.vector(mk("vector"))
        block.scalar(mk("scalar"))

    p2b.close()
    p2w.close()
    stk.close()
    return nc


def _consts():
    ident = np.eye(128).astype(bfloat16)
    mbd = np.zeros((128, 128), np.float32)
    for i in range(4):
        mbd[32 * i:32 * i + 32, 32 * i:32 * i + 32] = np.tril(
            np.ones((32, 32), np.float32), -1)
    masku = np.tile(np.triu(np.ones((32, 32), np.float32), 0), (4, 1))
    return {
        "c_ident": ident,
        "c_maskbd": mbd.astype(bfloat16),
        "c_masku": masku.astype(bfloat16),
        "c_ones": np.ones((128, 1), bfloat16),
        "c_onesf": np.ones((128, 1), np.float32),
        "c_onesr": np.ones((1, ST), bfloat16),
        "c_onesrf": np.ones((1, 128), np.float32),
        "c_identf": np.eye(128, dtype=np.float32),
        "c_eps": np.tile(np.array([1e-5, 1e-12, 1.0 / 128, 0.0],
                                  np.float32), (128, 1)),
    }


def _in_map(ci, inputs, L=2048):
    g = lambda k: np.ascontiguousarray(np.asarray(inputs["L0_attn_" + k],
                                                  np.float32))
    gf = lambda k: np.ascontiguousarray(np.asarray(inputs["L0_ffn_" + k],
                                                   np.float32))
    x = np.asarray(inputs["L0_attn_hidden_states"], np.float32)
    xs = x[NSEQ * ci:NSEQ * (ci + 1), :L, :]
    x_fm = np.ascontiguousarray(
        xs.reshape(NSEQ, L // 128, 128, D).transpose(3, 1, 0, 2)
        .reshape(D, NSEQ * L))
    im = {
        "x": x_fm,
        "w_q": g("q_w"), "w_k": g("k_w"), "w_v": g("v_w"),
        "cw_q": g("conv_q_w"), "cw_k": g("conv_k_w"), "cw_v": g("conv_v_w"),
        "b_w": g("b_w"), "mix_w": g("mix_w"),
        "mix_b": g("mix_b").reshape(H, 1),
        "mix_bias": g("mix_bias").reshape(H, 1),
        "o_norm_w": g("o_norm_w").reshape(DK, 1), "o_w": g("o_w"),
        "w_fcT": np.ascontiguousarray(gf("W_fc").T),
        "ln_w": gf("ln_w"), "ln_b": gf("ln_b"),
        "w_prT": np.ascontiguousarray(gf("W_proj").T),
    }
    im.update(_consts())
    return im


def _decode_y(ycore, L=2048):
    nst = L // 128
    yr = np.asarray(ycore, np.float32).reshape(D, nst, 4, NSEQ, 32)
    return np.ascontiguousarray(
        yr.transpose(3, 1, 2, 4, 0).reshape(NSEQ, L, D))


def kernel(**inputs):
    nc = build_core(L=2048)
    in_maps = [_in_map(ci, inputs) for ci in range(8)]
    res = run_bass_kernel_spmd(nc, in_maps, core_ids=list(range(8)))
    outs = [_decode_y(r["y"]) for r in res.results]
    return np.concatenate(outs, axis=0)
